# version 08
# baseline (speedup 1.0000x reference)
import os
os.environ.setdefault("TRN_ALLOW_ICI_ALL_TO_ALL", "1")

from contextlib import ExitStack

import numpy as np

import concourse.bass as bass
import concourse.mybir as mybir
from concourse.bass_utils import run_bass_kernel_spmd

F32 = mybir.dt.float32
BF16 = mybir.dt.bfloat16
U8 = mybir.dt.uint8
AF = mybir.ActivationFunctionType
ALU = mybir.AluOpType
AX = mybir.AxisListType

NCORES = 8
SEQ = 8192
D = 2048
HEADS = 16
HD = 128
CHUNK = 128
NCHUNK = SEQ // CHUNK
GRP = 4
NGRP = NCHUNK // GRP
HPC = HEADS // NCORES
TPC = SEQ // NCORES
SLAB = 256
NSLAB = TPC // SLAB
HID = 5632
NHT = HID // 128
NKT = D // 128
GW = GRP * 128
PI = float(np.pi)
LN_EPS = 1e-5
RMS_EPS = 1e-6
BASE = 10000.0

ENG2KEY = {"tensor": "T", "vector": "V", "scalar": "S", "gpsimd": "G"}
DMA2KEY = {"sync": "QS", "scalar": "QC", "gpsimd": "QG"}


class Em:

    def __init__(self, max_ops=None):
        self.plan = {e: [] for e in ("tensor", "vector", "scalar", "gpsimd", "sync")}
        self.n = {k: 0 for k in ("T", "V", "S", "G", "QS", "QC", "QG", "CC")}
        self.wm = {e: {} for e in self.plan}
        self.max_ops = max_ops
        self.emitted = 0

    def _emit(self, eng, fn, deps, inc_key, inc_amt):
        if self.max_ops is not None and fn is not None:
            if self.emitted >= self.max_ops:
                return None
            self.emitted += 1
        waits = []
        for d in deps:
            if d is None:
                continue
            k, v = d
            if v <= 0 or self.wm[eng].get(k, 0) >= v:
                continue
            self.wm[eng][k] = v
            waits.append((k, v))
        tok = None
        if inc_key is not None:
            self.n[inc_key] += inc_amt
            tok = (inc_key, self.n[inc_key])
        self.plan[eng].append((fn, waits, inc_key))
        return tok

    def op(self, eng, fn, deps=(), inc=True):
        return self._emit(eng, fn, deps, ENG2KEY[eng] if inc else None, 1)

    def dma(self, eng, fn, deps=()):
        return self.dma_batch(eng, [fn], deps)

    def dma_batch(self, eng, fns, deps=()):
        key = DMA2KEY[eng]
        pre = self.n[key]
        deps = list(deps)
        if pre:
            deps.append((key, pre))
        tok = None
        for i, fn in enumerate(fns):
            tok = self._emit(eng, fn, deps if i == 0 else (), key, 16)
        return tok

    def cc(self, fn, deps=()):
        return self._emit("gpsimd", fn, deps, "CC", 1)

    def wait(self, eng, deps):
        self._emit(eng, None, deps, None, 0)

    def join_tokens(self):
        return [(k, v) for k, v in self.n.items() if v > 0]

    def mm_group(self, pairs, out_ap, deps, start=True, stop=True, inc=True):
        n = len(pairs)
        tok = None
        for i, (l, r) in enumerate(pairs):
            def mk(l=l, r=r, o=out_ap, st=(start and i == 0),
                   sp=(stop and i == n - 1)):
                return lambda e: e.matmul(o, l, r, start=st, stop=sp)
            tok = self.op("tensor", mk(), deps=deps if i == 0 else (),
                          inc=(inc and i == n - 1))
        return tok


def build_program(debug_dump=False, stop="full", max_ops=None):
    nc = bass.Bass("TRN2", target_bir_lowering=False, debug=False)
    em = Em(max_ops)

    dram = {}

    def din(name, shape, dtype):
        dram[name] = nc.dram_tensor(name, list(shape), dtype, kind="ExternalInput")
        return dram[name]

    din("xg_in", (NGRP, NKT, 128, GW), F32)
    din("xs_in", (NKT, 128, TPC), F32)
    din("wq_in", (HPC, NKT, 128, 128), F32)
    din("wk_in", (HPC, NKT, 128, 128), F32)
    din("wv_in", (HPC, NKT, 128, 128), F32)
    din("wo_in", (NKT, NKT, 128, 128), F32)
    din("wg1_in", (NKT, 128, 128), F32)
    din("wg2_in", (128, D), F32)
    din("bg_in", (NHT, NKT, 128, 128), U8)
    din("bu_in", (NHT, NKT, 128, 128), U8)
    din("bd_in", (NKT, NHT, 128, 128), U8)
    din("sg_in", (128, NHT), F32)
    din("su_in", (128, NHT), F32)
    din("sd_in", (128, NKT), F32)
    din("nw_in", (128, NKT), F32)
    din("nwtm_in", (128, D), F32)
    din("cos_in", (NGRP, 128, GW), F32)
    din("sinsg_in", (NGRP, 128, GW), F32)
    din("ones_row_in", (1, 128), F32)
    din("c128_row_in", (1, 128), F32)
    din("invd_col_in", (128, 1), F32)
    din("ones_colb_in", (128, 1), F32)
    din("ident_in", (128, 128), F32)
    din("perm_in", (128, 128), F32)
    din("zero_col_in", (128, 1), F32)
    din("one_col_in", (128, 1), F32)
    din("dt_in", (HPC, 128, 128), F32)
    din("dq_in", (HPC, 128, GW), F32)
    din("dk_in", (128, HPC), F32)
    din("lam_in", (128, HPC), F32)

    out_t = nc.dram_tensor("out", [TPC, D], F32, kind="ExternalOutput")
    a2a_in = nc.dram_tensor("a2a_in", [NCORES, TPC, HPC * HD], F32)
    a2a_out = nc.dram_tensor("a2a_out", [NCORES, TPC, HPC * HD], F32)
    if debug_dump:
        dbg_ain = nc.dram_tensor("dbg_ain", [NCORES, TPC, HPC * HD], F32,
                                 kind="ExternalOutput")
        dbg_aout = nc.dram_tensor("dbg_aout", [NCORES, TPC, HPC * HD], F32,
                                  kind="ExternalOutput")
        dbg_onh = nc.dram_tensor("dbg_onh", [NSLAB, 128, NKT * SLAB], BF16,
                                 kind="ExternalOutput")
        dbg_onl = nc.dram_tensor("dbg_onl", [NSLAB, 128, NKT * SLAB], BF16,
                                 kind="ExternalOutput")
        dbg_rr = nc.dram_tensor("dbg_rr", [NSLAB, 128, NKT * SLAB], F32,
                                kind="ExternalOutput")
        dbg_x2h = nc.dram_tensor("dbg_x2h", [NSLAB, 128, NKT * SLAB], BF16,
                                 kind="ExternalOutput")
        dbg_hh = nc.dram_tensor("dbg_hh", [NSLAB, 128, NHT * SLAB], BF16,
                                kind="ExternalOutput")
        dbg_r2 = nc.dram_tensor("dbg_r2", [NSLAB, 128, NKT * SLAB], F32,
                                kind="ExternalOutput")

    sem_names = ["T", "V", "S", "G", "QS", "QC", "QG", "CC"]

    with ExitStack() as top:
        sems = {k: top.enter_context(nc.semaphore(k)) for k in sem_names}
        P = [top.enter_context(nc.psum_tensor(f"pb{i}", [128, 512], F32))
             for i in range(7)]
        bank_free = [None] * 8
        blk = top.enter_context(nc.Block())

        def sbuf(name, shape, dtype, stack):
            return stack.enter_context(nc.sbuf_tensor(name, list(shape), dtype))

        sb_ones_row = sbuf("c_onesr", [1, 128], F32, top)
        sb_c128_row = sbuf("c_c128r", [1, 128], F32, top)
        sb_invd = sbuf("c_invd", [128, 1], F32, top)
        sb_onesb_f = sbuf("c_onesbf", [128, 1], F32, top)
        sb_onesb = sbuf("c_onesb", [128, 1], BF16, top)
        sb_ident = sbuf("c_ident", [128, 128], F32, top)
        sb_perm = sbuf("c_perm", [128, 128], F32, top)
        sb_zero = sbuf("c_zero", [128, 1], F32, top)
        sb_one = sbuf("c_one", [128, 1], F32, top)
        sb_dk = sbuf("c_dk", [128, HPC], F32, top)
        sb_lam = sbuf("c_lam", [128, HPC], F32, top)
        sb_sg = sbuf("c_sg", [128, NHT], F32, top)
        sb_su = sbuf("c_su", [128, NHT], F32, top)
        sb_sgu = sbuf("c_sgu", [128, NHT], F32, top)
        sb_sd = sbuf("c_sd", [128, NKT], F32, top)
        sb_nw = sbuf("c_nw", [128, NKT], F32, top)
        WG1H = sbuf("wg1h", [128, NKT * 128], BF16, top)
        WG1L = sbuf("wg1l", [128, NKT * 128], BF16, top)
        WG2H = sbuf("wg2h", [128, D], BF16, top)
        WG2L = sbuf("wg2l", [128, D], BF16, top)

        sb_epsc = sbuf("c_epsc", [128, 1], F32, top)
        nc.const_aps.register(F32, 0.0, sb_zero[:, :])
        nc.const_aps.register(F32, LN_EPS, sb_epsc[:, :])
        em.op("vector", lambda e: e.memset(sb_epsc[:, :], LN_EPS), deps=())

        def ld(dst_ap, src_ap, eng="sync", deps=()):
            def f(e, d=dst_ap, s=src_ap):
                return e.dma_start(out=d, in_=s)
            return em.dma(eng, f, deps)

        tok_c = []
        tok_c.append(ld(sb_ones_row[:, :], dram["ones_row_in"].ap()))
        tok_c.append(ld(sb_c128_row[:, :], dram["c128_row_in"].ap()))
        tok_c.append(ld(sb_invd[:, :], dram["invd_col_in"].ap()))
        tok_c.append(ld(sb_onesb_f[:, :], dram["ones_colb_in"].ap()))
        tok_c.append(ld(sb_ident[:, :], dram["ident_in"].ap()))
        tok_c.append(ld(sb_perm[:, :], dram["perm_in"].ap()))
        tok_c.append(ld(sb_zero[:, :], dram["zero_col_in"].ap()))
        tok_c.append(ld(sb_one[:, :], dram["one_col_in"].ap()))
        tok_c.append(ld(sb_dk[:, :], dram["dk_in"].ap()))
        tok_c.append(ld(sb_lam[:, :], dram["lam_in"].ap()))
        tok_c.append(ld(sb_sg[:, :], dram["sg_in"].ap()))
        tok_c.append(ld(sb_su[:, :], dram["su_in"].ap()))
        tok_c.append(ld(sb_sd[:, :], dram["sd_in"].ap()))
        tok_c.append(ld(sb_nw[:, :], dram["nw_in"].ap()))
        tok_sgu = em.op("vector",
                        lambda e: e.tensor_mul(sb_sgu[:, :], sb_sg[:, :], sb_su[:, :]),
                        deps=tok_c)
        tok_onesb = em.op("scalar", lambda e: e.copy(sb_onesb[:, :], sb_onesb_f[:, :]),
                          deps=tok_c)

        def split_pair(f32_ap, hi_ap, lo_ap, dep_toks, scale_ap=None):
            if scale_ap is None:
                def fh(e, o=hi_ap, i=f32_ap):
                    return e.copy(o, i)

                def fl(e, o=lo_ap, i=f32_ap, h=hi_ap):
                    return e.tensor_sub(o, i, h)
            else:
                def fh(e, o=hi_ap, i=f32_ap, sc=scale_ap):
                    return e.activation(o, i, AF.Copy, scale=sc)

                def fl(e, o=lo_ap, i=f32_ap, sc=scale_ap, h=hi_ap):
                    return e.scalar_tensor_tensor(o, i, sc, h, ALU.mult, ALU.subtract)
            th = em.op("scalar", fh, deps=dep_toks)
            tl = em.op("vector", fl, deps=list(dep_toks) + [th])
            return th, tl


        NSEC = 8
        SECW = (NKT // NSEC) * GW
        with ExitStack() as pa:
            PB7 = pa.enter_context(nc.psum_tensor("pb7a", [128, 1024], BF16))
            sb_identb = sbuf("c_identb", [128, 128], BF16, pa)
            sb_dt = sbuf("c_dt", [128, HPC * 128], F32, pa)
            sb_dq = sbuf("c_dq", [128, HPC * GW], F32, pa)
            WQH = sbuf("wqh", [128, HPC * NKT * 128], BF16, pa)
            WQL = sbuf("wql", [128, HPC * NKT * 128], BF16, pa)
            WKH = sbuf("wkh", [128, HPC * NKT * 128], BF16, pa)
            WKL = sbuf("wkl", [128, HPC * NKT * 128], BF16, pa)
            WVH = sbuf("wvh", [128, HPC * NKT * 128], BF16, pa)
            WVL = sbuf("wvl", [128, HPC * NKT * 128], BF16, pa)
            XGT = [sbuf(f"XGT{i}", [128, SECW], F32, pa) for i in range(2)]
            XH = [sbuf(f"XH{i}", [128, NKT * GW], BF16, pa) for i in range(2)]
            XL = [sbuf(f"XL{i}", [128, NKT * GW], BF16, pa) for i in range(2)]

            tok_c.append(ld(sb_dt[:, :].rearrange("p (h a) -> p h a", h=HPC),
                            dram["dt_in"].ap().rearrange("h p a -> p h a")))
            tok_c.append(ld(sb_dq[:, :].rearrange("p (h a) -> p h a", h=HPC),
                            dram["dq_in"].ap().rearrange("h p a -> p h a")))
            tok_identb = em.op("vector",
                               lambda e: e.tensor_copy(sb_identb[:, :],
                                                       sb_ident[:, :]),
                               deps=tok_c)
            with nc.sbuf_tensor("wtmp", [128, HPC * NKT * 128], F32) as wtmp:
                prev_reads = list(tok_c)
                for nm, bh, bl in (("wq", WQH, WQL), ("wk", WKH, WKL),
                                   ("wv", WVH, WVL)):
                    t_ld = ld(wtmp[:, :].rearrange("p (h k a) -> p h k a",
                                                   h=HPC, k=NKT),
                              dram[nm + "_in"].ap().rearrange("h k p a -> p h k a"),
                              deps=prev_reads)
                    th, tl = split_pair(wtmp[:, :], bh[:, :], bl[:, :], [t_ld])
                    prev_reads = [th, tl]
                t_ld = ld(wtmp[:, 0:NKT * 128].rearrange("p (k a) -> p k a", k=NKT),
                          dram["wg1_in"].ap().rearrange("k p a -> p k a"),
                          deps=prev_reads)
                th, tl = split_pair(wtmp[:, 0:NKT * 128], WG1H[:, :], WG1L[:, :],
                                    [t_ld])
                t_ld = ld(wtmp[:, 0:D], dram["wg2_in"].ap(), deps=[th, tl])
                th, tl = split_pair(wtmp[:, 0:D], WG2H[:, :], WG2L[:, :], [t_ld])
                tok_wsplit = [th, tl]
            COS = [sbuf(f"COS{i}", [128, GW], F32, pa) for i in range(2)]
            SINSG = [sbuf(f"SINSG{i}", [128, GW], F32, pa) for i in range(2)]
            QS_ = sbuf("QS", [128, HPC * GW], F32, pa)
            KS_ = sbuf("KS", [128, HPC * GW], F32, pa)
            RT1 = sbuf("RT1", [128, GW], F32, pa)
            RT2 = sbuf("RT2", [128, GW], F32, pa)
            QR = QS_
            KR = KS_
            QRH = [sbuf(f"QRH{i}", [128, HPC * GW], BF16, pa) for i in range(2)]
            QRL = [sbuf(f"QRL{i}", [128, HPC * GW], BF16, pa) for i in range(2)]
            KRH = [sbuf(f"KRH{i}", [128, HPC * GW], BF16, pa) for i in range(2)]
            KRL = [sbuf(f"KRL{i}", [128, HPC * GW], BF16, pa) for i in range(2)]
            QDH = [sbuf(f"QDH{i}", [128, HPC * GW], BF16, pa) for i in range(2)]
            QDL = [sbuf(f"QDL{i}", [128, HPC * GW], BF16, pa) for i in range(2)]
            VH = [sbuf(f"VH{i}", [128, HPC * GW], BF16, pa) for i in range(2)]
            VL = [sbuf(f"VL{i}", [128, HPC * GW], BF16, pa) for i in range(2)]
            VTH = sbuf("VTH", [128, HPC * 128], BF16, pa)
            VTL = sbuf("VTL", [128, HPC * 128], BF16, pa)
            KDH = sbuf("KDH", [128, HPC * 128], BF16, pa)
            KDL = sbuf("KDL", [128, HPC * 128], BF16, pa)
            SCD = sbuf("SCD", [128, 128], F32, pa)
            SCDH = sbuf("SCDH", [128, 128], BF16, pa)
            SCDL = sbuf("SCDL", [128, 128], BF16, pa)
            ST = sbuf("STT", [128, HPC * 128], F32, pa)
            STH = sbuf("STH", [128, HPC * 128], BF16, pa)
            STL = sbuf("STL", [128, HPC * 128], BF16, pa)
            OST = [sbuf(f"OST{i}", [128, HPC * 128], F32, pa) for i in range(2)]
            if os.environ.get("KPRINT_SBUF"):
                print("phaseA peak, sbuf bytes remaining:", nc.sbuf_bytes_remaining)

            def gsl(buf, ht, lo=0, hi=GW):
                return buf[:, ht * GW + lo: ht * GW + hi]

            t_s0 = em.op("vector", lambda e: e.memset(ST[:, :], 0.0), deps=tok_c)
            t_sh0 = em.op("scalar", lambda e: e.memzero(STH[:, :]), deps=tok_c)
            t_sl0 = em.op("vector", lambda e: e.memset(STL[:, :], 0.0), deps=[t_s0])
            st_tok = {"S": t_s0, "H": t_sh0, "L": t_sl0}

            xg_ld = [None] * NGRP
            xsplit = [None] * NGRP
            grp_data = [None] * NGRP
            grp_done = [None] * NGRP
            ost_free = [None, None]
            scd_free = None
            a2a_last = None
            trig_free = [None, None]

            sec_free = [None, None]
            def emit_load_split(g):
                b = g % 2
                kps = NKT // NSEC
                last_h = last_l = None
                for sec in range(NSEC):
                    tb = (g * NSEC + sec) % 2
                    deps = list(tok_wsplit)
                    if sec_free[tb] is not None:
                        deps.append(sec_free[tb])
                    t_ld = ld(XGT[tb][:, :].rearrange("p (k a) -> p k a", k=kps),
                              dram["xg_in"].ap()[g, sec * kps:(sec + 1) * kps]
                              .rearrange("k p a -> p k a"),
                              deps=deps)
                    sdeps = [t_ld]
                    if g >= 2 and sec == 0:
                        sdeps.append(grp_done[g - 2])
                    cs = slice(sec * SECW, (sec + 1) * SECW)
                    last_h, last_l = split_pair(XGT[tb][:, :], XH[b][:, cs],
                                                XL[b][:, cs], sdeps)
                    sec_free[tb] = last_l
                xsplit[g] = (last_h, last_l)

            UNIT_ORDER = [("q", 0), ("q", 1), ("k", 0), ("k", 1),
                          ("v", 0), ("v", 1)]

            def make_group_units(g):
                b = g % 2
                d = {"v": {}}
                grp_data[g] = d

                def unit(idx):
                    proj, ht = UNIT_ORDER[idx]
                    th_x, tl_x = xsplit[g]
                    if idx == 0:
                        tdeps = ([trig_free[b]] if trig_free[b] else [])
                        d["t_cos"] = ld(COS[b][:, :], dram["cos_in"].ap()[g],
                                        deps=tdeps)
                        d["t_sinsg"] = ld(SINSG[b][:, :],
                                          dram["sinsg_in"].ap()[g], deps=tdeps)
                    wh, wl = {"q": (WQH, WQL), "k": (WKH, WKL),
                              "v": (WVH, WVL)}[proj]
                    slot = idx % 2
                    bank = P[1 + slot]
                    pairs = []
                    for wb, xb in ((wh, XH[b]), (wh, XL[b]), (wl, XH[b])):
                        for kt in range(NKT):
                            pairs.append(
                                (wb[:, (ht * NKT + kt) * 128:
                                     (ht * NKT + kt + 1) * 128],
                                 xb[:, kt * GW:(kt + 1) * GW]))
                    mmdeps = [th_x, tl_x] + \
                        ([bank_free[1 + slot]] if bank_free[1 + slot] else [])
                    t_mm = em.mm_group(pairs, bank[:, 0:GW], mmdeps)
                    if proj == "v":
                        tvh = em.op("scalar",
                                    lambda e, p=bank, o=gsl(VH[b], ht):
                                    e.copy(o, p[:, 0:GW]), deps=[t_mm])
                        tvl = em.op("vector",
                                    lambda e, p=bank, o=gsl(VL[b], ht),
                                    h=gsl(VH[b], ht):
                                    e.tensor_sub(o, p[:, 0:GW], h),
                                    deps=[t_mm, tvh])
                        d["v"][ht] = (tvh, tvl)
                        bank_free[1 + slot] = tvl
                        return
                    sbuf_src = QS_ if proj == "q" else KS_
                    dst = gsl(sbuf_src, ht)
                    tsi = em.op("scalar",
                                lambda e, p=bank, o=dst:
                                e.activation(o, p[:, 0:GW], AF.Silu),
                                deps=[t_mm])
                    bank_free[1 + slot] = tsi
                    dsth, dstl = ((QRH[b], QRL[b]) if proj == "q"
                                  else (KRH[b], KRL[b]))
                    t_sw = em.mm_group(
                        [(sb_perm[:, :], dst)], P[3][:, 0:GW],
                        deps=[tsi] + ([bank_free[3]] if bank_free[3] else []))
                    t_m1 = em.op("vector",
                                 lambda e, s=dst, bb=b:
                                 e.tensor_mul(RT1[:, :], s, COS[bb][:, :]),
                                 deps=[tsi, d["t_cos"]])
                    t_m2 = em.op("vector",
                                 lambda e, bb=b: e.tensor_mul(
                                     RT2[:, :], P[3][:, 0:GW], SINSG[bb][:, :]),
                                 deps=[t_sw, d["t_sinsg"], t_m1])
                    bank_free[3] = t_m2
                    trig_free[b] = t_m2
                    t_rot = em.op("vector",
                                  lambda e, o=dst:
                                  e.tensor_add(o, RT1[:, :], RT2[:, :]),
                                  deps=[t_m1, t_m2])
                    sh, sl = split_pair(dst, gsl(dsth, ht), gsl(dstl, ht),
                                        [t_rot])
                    d[(proj, ht)] = (sh, sl)
                    if proj == "q":
                        t_qd = em.op("vector",
                                     lambda e, s=dst, q=gsl(sb_dq, ht):
                                     e.tensor_mul(RT1[:, :], s, q),
                                     deps=[t_rot, sh, sl])
                        dh, dl = split_pair(RT1[:, :], gsl(QDH[b], ht),
                                            gsl(QDL[b], ht), [t_qd])
                        d[("qd", ht)] = (dh, dl)

                return [lambda i=i: unit(i) for i in range(6)]

            def make_scan_units(g):
                d = grp_data[g]
                pb_ = g % 2

                def unit(c):
                    nonlocal st_tok, scd_free, a2a_last
                    last_mm = None
                    gc = g * GRP + c
                    peer = gc // (TPC // CHUNK)
                    lch = gc % (TPC // CHUNK)
                    for ht in range(HPC):
                        hs = slice(ht * 128, (ht + 1) * 128)
                        qh, ql = d[("q", ht)]
                        kh, kl = d[("k", ht)]
                        qdh, qdl = d[("qd", ht)]
                        vh, vl = d["v"][ht]
                        tr_deps = [vh, vl, kh, kl, tok_identb] + \
                            ([bank_free[7]] if bank_free[7] else [])
                        t_tr = None
                        srcs = [gsl(VH[pb_], ht, c * 128, (c + 1) * 128),
                                gsl(VL[pb_], ht, c * 128, (c + 1) * 128),
                                gsl(KRH[pb_], ht, c * 128, (c + 1) * 128),
                                gsl(KRL[pb_], ht, c * 128, (c + 1) * 128)]
                        for i, s_ap in enumerate(srcs):
                            def mk(s=s_ap, o=i * 256):
                                return lambda e: e.transpose(
                                    PB7[:, o:o + 128], s, sb_identb[:, :])
                            t_tr = em.op("tensor", mk(),
                                         deps=tr_deps if i == 0 else (), inc=(i == 3))
                        t_evh = em.op("scalar", lambda e, o=VTH[:, hs]:
                                      e.copy(o, PB7[:, 0:128]), deps=[t_tr])
                        t_evl = em.op("vector", lambda e, o=VTL[:, hs]:
                                      e.tensor_copy(o, PB7[:, 256:384]),
                                      deps=[t_tr, t_evh])
                        t_ekh0 = em.op("scalar",
                                       lambda e, o=KDH[:, hs]:
                                       e.copy(o, PB7[:, 512:640]),
                                       deps=[t_tr, t_evl])
                        t_ekl0 = em.op("vector",
                                       lambda e, o=KDL[:, hs]:
                                       e.tensor_copy(o, PB7[:, 768:896]),
                                       deps=[t_tr, t_ekh0])
                        bank_free[7] = t_ekl0
                        t_ekh = em.op("vector",
                                      lambda e, o=KDH[:, hs], k=sb_dk[:, ht:ht + 1]:
                                      e.tensor_scalar_mul(o, o, k),
                                      deps=[t_ekh0, t_ekl0])
                        t_ekl = em.op("vector",
                                      lambda e, o=KDL[:, hs], k=sb_dk[:, ht:ht + 1]:
                                      e.tensor_scalar_mul(o, o, k), deps=[t_ekl0])

                        sc_pairs = [
                            (gsl(KRH[pb_], ht, c * 128, (c + 1) * 128),
                             gsl(QRH[pb_], ht, c * 128, (c + 1) * 128)),
                            (gsl(KRH[pb_], ht, c * 128, (c + 1) * 128),
                             gsl(QRL[pb_], ht, c * 128, (c + 1) * 128)),
                            (gsl(KRL[pb_], ht, c * 128, (c + 1) * 128),
                             gsl(QRH[pb_], ht, c * 128, (c + 1) * 128))]
                        t_sc = em.mm_group(
                            sc_pairs, P[4][:, 0:128],
                            deps=[qh, ql, kh, kl] +
                            ([bank_free[4]] if bank_free[4] else []))
                        t_scd = em.op("vector", lambda e, dt=sb_dt[:, hs]:
                                      e.tensor_mul(SCD[:, :], P[4][:, 0:128], dt),
                                      deps=[t_sc] + ([scd_free] if scd_free else []))
                        bank_free[4] = t_scd
                        t_sdh, t_sdl = split_pair(SCD[:, :], SCDH[:, :], SCDL[:, :],
                                                  [t_scd])
                        scd_free = t_sdl

                        kv_pairs = [(KDH[:, hs], VTH[:, hs]),
                                    (KDH[:, hs], VTL[:, hs]),
                                    (KDL[:, hs], VTH[:, hs])]
                        t_kv = em.mm_group(
                            kv_pairs, P[6][:, 0:128],
                            deps=[t_ekh, t_ekl, t_evh, t_evl] +
                            ([bank_free[6]] if bank_free[6] else []))
                        last_mm = t_kv

                        o_pairs = [
                            (SCDH[:, :], VTH[:, hs]),
                            (SCDH[:, :], VTL[:, hs]),
                            (SCDL[:, :], VTH[:, hs]),
                            (gsl(QDH[pb_], ht, c * 128, (c + 1) * 128), STH[:, hs]),
                            (gsl(QDH[pb_], ht, c * 128, (c + 1) * 128), STL[:, hs]),
                            (gsl(QDL[pb_], ht, c * 128, (c + 1) * 128), STH[:, hs])]
                        t_o = em.mm_group(
                            o_pairs, P[5][:, 0:128],
                            deps=[t_sdh, t_sdl, t_evh, t_evl, qdh, qdl,
                                  st_tok["H"], st_tok["L"]] +
                            ([bank_free[5]] if bank_free[5] else []))
                        ev_deps = [t_o]
                        if ht == 0 and ost_free[c % 2] is not None:
                            ev_deps.append(ost_free[c % 2])
                        t_oev = em.op("vector",
                                      lambda e, o=OST[c % 2][:, hs]:
                                      e.tensor_copy(o, P[5][:, 0:128]), deps=ev_deps)
                        bank_free[5] = t_oev
                        t_sup = em.op("vector",
                                      lambda e, s=ST[:, hs], lam=sb_lam[:, ht:ht + 1]:
                                      e.scalar_tensor_tensor(
                                          s, s, lam, P[6][:, 0:128], ALU.mult, ALU.add),
                                      deps=[t_kv, st_tok["S"], st_tok["H"], st_tok["L"]])
                        bank_free[6] = t_sup
                        t_nh = em.op("scalar", lambda e, s=ST[:, hs], o=STH[:, hs]:
                                     e.copy(o, s), deps=[t_sup, t_o])
                        t_nl = em.op("vector", lambda e, s=ST[:, hs], o=STL[:, hs],
                                     h=STH[:, hs]:
                                     e.tensor_sub(o, s, h), deps=[t_sup, t_nh])
                        st_tok = {"S": t_sup, "H": t_nh, "L": t_nl}
                        if ht == HPC - 1:
                            def mk(o=OST[c % 2],
                                   dd=a2a_in.ap()[peer, lch * 128:(lch + 1) * 128, :]):
                                return lambda e: e.dma_start(out=dd, in_=o[:, :])
                            t_st = em.dma("gpsimd", mk(), deps=[t_oev])
                            ost_free[c % 2] = t_st
                            a2a_last = t_st
                    grp_done[g] = last_mm

                return [lambda c=c: unit(c) for c in range(GRP)]

            ngrp_run = 0 if stop == "init" else NGRP
            for g in range(ngrp_run):
                emit_load_split(g)
                gu = make_group_units(g)
                su = make_scan_units(g - 1) if (g >= 1 and stop != "noscan") else []
                seq = []
                for i in range(6):
                    seq.append(gu[i])
                    if i < len(su):
                        seq.append(su[i])
                seq += su[6:]
                for u in seq:
                    u()
            if ngrp_run and stop != "noscan":
                for u in make_scan_units(ngrp_run - 1):
                    u()

            if stop in ("phaseA", "init", "noscan"):
                em.wait("gpsimd", [("QG", em.n["QG"])])
                t_cc = None
            else:
                t_cc = em.cc(lambda e: e.collective_compute(
                    "AllToAll", ALU.bypass, replica_groups=[list(range(NCORES))],
                    ins=[a2a_in.ap().opt()], outs=[a2a_out.ap().opt()]),
                    deps=[a2a_last, ("QG", em.n["QG"])])
            if debug_dump:
                em.dma("gpsimd", lambda e: e.dma_start(
                    out=dbg_ain.ap().rearrange("c t d -> c (t d)"),
                    in_=a2a_in.ap().rearrange("c t d -> c (t d)")), deps=[t_cc])
                em.dma("gpsimd", lambda e: e.dma_start(
                    out=dbg_aout.ap().rearrange("c t d -> c (t d)"),
                    in_=a2a_out.ap().rearrange("c t d -> c (t d)")), deps=[t_cc])

        join = em.join_tokens()
        for i in range(8):
            bank_free[i] = None

        with ExitStack() as pb:
            P7F = pb.enter_context(nc.psum_tensor("pb7b", [128, 512], F32))
            PW = [P[6], P7F]
            sb_nwtm = sbuf("c_nwtm", [128, D], F32, pb)
            tok_nwtm = ld(sb_nwtm[:, :], dram["nwtm_in"].ap(), deps=join)
            XBT = [sbuf(f"XBT{i}", [128, (NKT // 8) * SLAB], F32, pb)
                   for i in range(2)]
            XBH = sbuf("XBH", [128, NKT * SLAB], BF16, pb)
            XBL = sbuf("XBL", [128, NKT * SLAB], BF16, pb)
            G1H = sbuf("G1H", [128, SLAB], BF16, pb)
            G1L = sbuf("G1L", [128, SLAB], BF16, pb)
            SIG = sbuf("SIGB", [128, 512], F32, pb)
            GBUF = SIG[:, 256:512]
            OTM = sbuf("OTM", [128, D], F32, pb)
            SQ = sbuf("SQ", [128, D], F32, pb)
            MS = sbuf("MS", [128, NKT], F32, pb)
            RSD = sbuf("RSD", [128, NKT], F32, pb)
            TMP1 = sbuf("TMP1", [128, NKT], F32, pb)
            RR = sbuf("RR", [128, NKT * SLAB], F32, pb)
            X2H = sbuf("X2H", [128, NKT * SLAB], BF16, pb)
            X2L = sbuf("X2L", [128, NKT * SLAB], BF16, pb)
            HH = sbuf("HH", [128, NHT * SLAB], BF16, pb)
            HL = sbuf("HL", [128, NHT * SLAB], BF16, pb)
            ONH = HH
            ONL = HL
            WOF = sbuf("WOF", [128, NKT * 128], F32, pb)
            WOH = [sbuf(f"WOH{i}", [128, NKT * 128], BF16, pb) for i in range(2)]
            WOL = [sbuf(f"WOL{i}", [128, NKT * 128], BF16, pb) for i in range(2)]
            BGB = [sbuf(f"BGB{i}", [128, NKT * 128], BF16, pb) for i in range(2)]
            BUB = [sbuf(f"BUB{i}", [128, NKT * 128], BF16, pb) for i in range(2)]
            BDB = [sbuf(f"BDB{i}", [128, (NHT // 2) * 128], BF16, pb)
                   for i in range(2)]
            HSUM = sbuf("HSUM", [1, SLAB], F32, pb)
            CORR = sbuf("CORR", [128, SLAB], F32, pb)
            MEANR = sbuf("MEANR", [1, SLAB], F32, pb)
            VARR = sbuf("VARR", [1, SLAB], F32, pb)
            RSR = sbuf("RSR", [1, SLAB], F32, pb)
            R2TM = OTM
            MCOL = sbuf("MCOL", [128, 2], F32, pb)
            VCOL = sbuf("VCOL", [128, 2], F32, pb)
            SCOL = sbuf("SCOL", [128, 4], F32, pb)
            if os.environ.get("KPRINT_SBUF"):
                print("phaseB peak, sbuf bytes remaining:", nc.sbuf_bytes_remaining)

            def newton_rsqrt(eng, y_ap, a_ap, tmp_ap, deps):
                t1 = em.op(eng, lambda e: e.tensor_mul(tmp_ap, y_ap, y_ap), deps=deps)
                t2 = em.op(eng, lambda e: e.tensor_mul(tmp_ap, tmp_ap, a_ap), deps=[t1])
                t3 = em.op(eng, lambda e: e.tensor_scalar(
                    tmp_ap, tmp_ap, -0.5, 1.5, ALU.mult, ALU.add), deps=[t2])
                return em.op(eng, lambda e: e.tensor_mul(y_ap, y_ap, tmp_ap), deps=[t3])

            nslab_run = {"phaseA": 0, "a2a": 0, "init": 0, "noscan": 0,
                         "slab1": 1}.get(stop, NSLAB)
            pd = {}
            xbt_free = [None, None]
            wo_slot_free = [None, None]
            bg_slot_free = [None, None]
            bd_slot_free = [None, None]
            wof_free = None
            for s in range(nslab_run):
                nd = {}
                th_x = tl_x = None
                kps = NKT // 8
                for sec in range(8):
                    tb = (s * 8 + sec) % 2
                    deps = (join if s == 0 and sec < 2 else [])
                    if xbt_free[tb] is not None:
                        deps = deps + [xbt_free[tb]]
                    t_xb = ld(XBT[tb][:, :].rearrange("p (k t) -> p k t", k=kps),
                              dram["xs_in"].ap()[sec * kps:(sec + 1) * kps, :,
                                                 s * SLAB:(s + 1) * SLAB]
                              .rearrange("k p t -> p k t"),
                              deps=deps)
                    sdeps = [t_xb]
                    if s and sec == 0:
                        sdeps.append(pd["xhl_used"])
                    cs = slice(sec * kps * SLAB, (sec + 1) * kps * SLAB)
                    th_x, tl_x = split_pair(XBT[tb][:, :], XBH[:, cs], XBL[:, cs],
                                            sdeps)
                    xbt_free[tb] = tl_x

                g1_pairs = []
                for wb, xb in ((WG1H, XBH), (WG1H, XBL), (WG1L, XBH)):
                    for kt in range(NKT):
                        g1_pairs.append((wb[:, kt * 128:(kt + 1) * 128],
                                         xb[:, kt * SLAB:(kt + 1) * SLAB]))
                t_g1 = em.mm_group(g1_pairs, P[1][:, 0:SLAB],
                                   deps=[th_x, tl_x] +
                                   ([bank_free[1]] if bank_free[1] else []))
                th_g1, tl_g1 = split_pair(P[1][:, 0:SLAB], G1H[:, :], G1L[:, :],
                                          [t_g1])
                bank_free[1] = tl_g1

                on_last = None
                otm_free = pd.get("otm_used")
                for bk in range(2):
                    deps = ([t_cc] + join) if (s == 0 and bk == 0) else []
                    if otm_free is not None:
                        deps = deps + [otm_free, t_cc]
                    if s and bk == 0 and "out_done" in pd:
                        deps = deps + [pd["out_done"]]
                    def mko(b=bk, p=0, ss=s):
                        return lambda e: e.dma_start(
                            out=OTM[:, p * HPC * HD:(p + 1) * HPC * HD],
                            in_=a2a_out.ap()[p, ss * SLAB + b * 128:
                                             ss * SLAB + (b + 1) * 128, :])
                    t_ol = em.dma_batch("scalar",
                                        [mko(bk, p) for p in range(NCORES)],
                                        deps=deps)
                    og_toks = []
                    for msl in range(4):
                        gate_pairs = [
                            (G1H[:, bk * 128:(bk + 1) * 128],
                             WG2H[:, msl * 512:(msl + 1) * 512]),
                            (G1H[:, bk * 128:(bk + 1) * 128],
                             WG2L[:, msl * 512:(msl + 1) * 512]),
                            (G1L[:, bk * 128:(bk + 1) * 128],
                             WG2H[:, msl * 512:(msl + 1) * 512])]
                        t_gt = em.mm_group(
                            gate_pairs, P[2][:, :],
                            deps=[th_g1, tl_g1] +
                            ([bank_free[2]] if bank_free[2] else []))
                        t_sg_ = em.op("scalar", lambda e: e.activation(
                            SIG[:, :], P[2][:, :], AF.Sigmoid),
                            deps=[t_gt] + (og_toks[-1:] if og_toks else []))
                        bank_free[2] = t_sg_
                        def mkog(m=msl):
                            return lambda e: e.tensor_mul(
                                OTM[:, m * 512:(m + 1) * 512],
                                OTM[:, m * 512:(m + 1) * 512], SIG[:, :])
                        t_og = em.op("vector", mkog(), deps=[t_sg_, t_ol])
                        og_toks.append(t_og)

                    t_sq = em.op("scalar",
                                 lambda e: e.square(SQ[:, :], OTM[:, :]),
                                 deps=og_toks +
                                 ([pd["sq_used"]] if (s and bk == 0) else []) +
                                 ([on_last] if on_last else []))
                    t_ms = em.op("vector", lambda e: e.tensor_reduce(
                        MS[:, :], SQ[:, :].rearrange("p (f e) -> p f e", f=NKT),
                        AX.X, ALU.add), deps=[t_sq])
                    t_a = em.op("vector", lambda e: e.tensor_scalar(
                        MS[:, :], MS[:, :], 1.0 / HD, RMS_EPS, ALU.mult, ALU.add),
                        deps=[t_ms])
                    t_s1 = em.op("scalar", lambda e: e.activation(
                        RSD[:, :], MS[:, :], AF.Sqrt), deps=[t_a])
                    t_rc = em.op("vector", lambda e: e.reciprocal(RSD[:, :], RSD[:, :]),
                                 deps=[t_s1])
                    t_rf = newton_rsqrt("vector", RSD[:, :], MS[:, :], TMP1[:, :],
                                        [t_rc, t_a])
                    sc_toks = []
                    for ft in range(NKT):
                        def mks(f=ft):
                            return lambda e: e.tensor_scalar_mul(
                                OTM[:, f * 128:(f + 1) * 128],
                                OTM[:, f * 128:(f + 1) * 128], RSD[:, f:f + 1])
                        sc_toks.append(em.op("vector", mks(), deps=[t_rf]))
                    t_nw = em.op("vector", lambda e: e.tensor_mul(
                        OTM[:, :], OTM[:, :], sb_nwtm[:, :]),
                        deps=sc_toks + [tok_nwtm])
                    sc_toks = [t_nw] * NKT
                    ev_prev = None
                    t_tp_last = None
                    for q in range(4):
                        bidx = 4 + (q % 2)
                        fill_deps = sc_toks[q * 4:(q + 1) * 4] + \
                            ([bank_free[bidx]] if bank_free[bidx] else [])
                        t_tp = None
                        for i in range(4):
                            ft = q * 4 + i
                            def mkt(f=ft, o=i * 128, pk=P[bidx]):
                                return lambda e: e.transpose(
                                    pk[:, o:o + 128],
                                    OTM[:, f * 128:(f + 1) * 128], sb_ident[:, :])
                            t_tp = em.op("tensor", mkt(),
                                         deps=fill_deps if i == 0 else (), inc=(i == 3))
                        t_tp_last = t_tp
                        for i in range(4):
                            ft = q * 4 + i
                            col = ft * SLAB + bk * 128
                            def mkh(c=col, o=i * 128, pk=P[bidx]):
                                return lambda e: e.copy(
                                    ONH[:, c:c + 128], pk[:, o:o + 128])

                            def mkl(c=col, o=i * 128, pk=P[bidx]):
                                return lambda e: e.tensor_sub(
                                    ONL[:, c:c + 128], pk[:, o:o + 128],
                                    ONH[:, c:c + 128])
                            t_h_ = em.op("scalar", mkh(),
                                         deps=[t_tp] +
                                         ([ev_prev] if (i > 0 and ev_prev) else []) +
                                         ([pd["on_used"]] if (s and bk == 0 and q == 0
                                                              and i == 0) else []))
                            t_l_ = em.op("vector", mkl(), deps=[t_tp, t_h_])
                            ev_prev = t_l_
                        bank_free[bidx] = ev_prev
                    on_last = ev_prev
                    otm_free = t_tp_last
                nd["otm_used"] = otm_free
                nd["sq_used"] = on_last

                r_toks = []
                for mt in range(NKT):
                    sl = mt % 2
                    t_wf = ld(WOF[:, :].rearrange("p (k a) -> p k a", k=NKT),
                              dram["wo_in"].ap()[mt].rearrange("k p a -> p k a"),
                              deps=([wof_free] if wof_free else []) +
                              (join if s == 0 and mt == 0 else []))
                    def mkwh(b=sl):
                        return lambda e: e.copy(WOH[b][:, :], WOF[:, :])

                    def mkwl(b=sl):
                        return lambda e: e.tensor_sub(WOL[b][:, :], WOF[:, :],
                                                      WOH[b][:, :])
                    t_wh = em.op("scalar", mkwh(),
                                 deps=[t_wf] +
                                 ([wo_slot_free[sl]] if wo_slot_free[sl] else
                                  (join if s == 0 else [])))
                    t_wl = em.op("vector", mkwl(), deps=[t_wf, t_wh])
                    wof_free = t_wl
                    wo_pairs = []
                    for wb, ob in ((WOH[sl], ONH), (WOH[sl], ONL), (WOL[sl], ONH)):
                        for kt in range(NKT):
                            wo_pairs.append((wb[:, kt * 128:(kt + 1) * 128],
                                             ob[:, kt * SLAB:(kt + 1) * SLAB]))
                    t_mm = em.mm_group(
                        wo_pairs, PW[sl][:, 0:SLAB],
                        deps=[on_last, t_wl] +
                        ([bank_free[6 + sl]] if bank_free[6 + sl] else []))
                    wo_slot_free[sl] = t_mm
                    def mr1(m=mt, pk=PW[sl]):
                        return lambda e: e.tensor_add(
                            RR[:, m * SLAB:(m + 1) * SLAB], pk[:, 0:SLAB],
                            XBH[:, m * SLAB:(m + 1) * SLAB])

                    def mr2(m=mt):
                        return lambda e: e.tensor_add(
                            RR[:, m * SLAB:(m + 1) * SLAB],
                            RR[:, m * SLAB:(m + 1) * SLAB],
                            XBL[:, m * SLAB:(m + 1) * SLAB])
                    t_r1 = em.op("vector", mr1(),
                                 deps=[t_mm, th_x] +
                                 ([pd["rr_used"]] if (s and mt == 0) else []))
                    t_r2_ = em.op("vector", mr2(), deps=[t_r1, tl_x])
                    bank_free[6 + sl] = t_r2_
                    r_toks.append(t_r2_)
                nd["xb_used"] = r_toks[-1]
                nd["xhl_used"] = r_toks[-1]
                nd["on_used"] = r_toks[-1]
                dbg_tok = None
                if debug_dump:
                    t_dON = em.dma("gpsimd", lambda e, ss=s: e.dma_start(
                        out=dbg_onh.ap()[ss], in_=ONH[:, :]), deps=[on_last])
                    t_dON = em.dma("gpsimd", lambda e, ss=s: e.dma_start(
                        out=dbg_onl.ap()[ss], in_=ONL[:, :]), deps=[on_last])
                    nd["on_used"] = t_dON
                    dbg_tok = em.dma("gpsimd", lambda e, ss=s: e.dma_start(
                        out=dbg_rr.ap()[ss], in_=RR[:, :]), deps=r_toks)

                mean_pairs = [(sb_invd[:, :], RR[:, kt * SLAB:(kt + 1) * SLAB])
                              for kt in range(NKT)]
                t_mean = em.mm_group(mean_pairs, P[0][0:1, 0:SLAB],
                                     deps=r_toks + ([bank_free[0]] if bank_free[0]
                                                    else []))
                t_mrow = em.op("vector", lambda e: e.tensor_copy(
                    MEANR[:, :], P[0][0:1, 0:SLAB]), deps=[t_mean])
                t_ss = None
                t_qq = None
                for half in range(2):
                    hw = NKT * SLAB // 2
                    def mkq(hf=half, w=hw):
                        return lambda e: e.square(SQ[:, 0:w], RR[:, hf * w:(hf + 1) * w])
                    t_qq = em.op("scalar", mkq(),
                                 deps=(r_toks + [on_last]) if half == 0 else
                                 [t_ss, t_qq])
                    sq_pairs = [(sb_invd[:, :], SQ[:, kt * SLAB:(kt + 1) * SLAB])
                                for kt in range(NKT // 2)]
                    t_ss = em.mm_group(
                        sq_pairs, P[1][0:1, 0:SLAB],
                        deps=[t_qq, tl_g1] +
                        ([bank_free[1]] if (half == 0 and bank_free[1]) else []),
                        start=(half == 0), stop=(half == 1))
                t_v1 = em.op("vector", lambda e: e.tensor_mul(
                    RSR[:, :], MEANR[:, :], MEANR[:, :]), deps=[t_mrow])
                t_v2 = em.op("vector", lambda e: e.tensor_sub(
                    VARR[:, :], P[1][0:1, 0:SLAB], RSR[:, :]), deps=[t_ss, t_v1])
                t_s2 = em.op("scalar", lambda e: e.activation(
                    RSR[:, :], VARR[:, :], AF.Sqrt, bias=LN_EPS), deps=[t_v2, t_v1])
                t_rc2 = em.op("vector", lambda e: e.reciprocal(RSR[:, :], RSR[:, :]),
                              deps=[t_s2])
                t_ve = em.op("vector", lambda e: e.tensor_scalar_add(
                    VARR[:, :], VARR[:, :], LN_EPS), deps=[t_v2])
                t_rf2 = newton_rsqrt("vector", RSR[:, :], VARR[:, :], HSUM[:, :],
                                     [t_rc2, t_ve])
                t_bm = em.mm_group([(sb_ones_row[:, :], MEANR[:, :])], P[0][:, 0:SLAB],
                                   deps=[t_mrow, t_v1, t_mean])
                t_br = em.mm_group([(sb_ones_row[:, :], RSR[:, :])], P[1][:, 0:SLAB],
                                   deps=[t_rf2, t_v2])
                x2_toks = []
                for kt in range(NKT):
                    def ms1(k=kt):
                        return lambda e: e.tensor_sub(
                            RR[:, k * SLAB:(k + 1) * SLAB],
                            RR[:, k * SLAB:(k + 1) * SLAB], P[0][:, 0:SLAB])

                    def ms2(k=kt):
                        return lambda e: e.tensor_mul(
                            RR[:, k * SLAB:(k + 1) * SLAB],
                            RR[:, k * SLAB:(k + 1) * SLAB], P[1][:, 0:SLAB])
                    t_c1 = em.op("vector", ms1(),
                                 deps=[t_bm] + ([dbg_tok] if dbg_tok else []))
                    t_c2 = em.op("vector", ms2(), deps=[t_br, t_c1])
                    th2, tl2 = split_pair(RR[:, kt * SLAB:(kt + 1) * SLAB],
                                          X2H[:, kt * SLAB:(kt + 1) * SLAB],
                                          X2L[:, kt * SLAB:(kt + 1) * SLAB],
                                          [t_c2] + ([pd["x2_used"]]
                                                    if (s and kt == 0) else []))
                    x2_toks += [th2, tl2]
                x2_last = x2_toks[-1]
                bank_free[0] = x2_last
                bank_free[1] = x2_last
                if debug_dump:
                    t_dx2 = em.dma("gpsimd", lambda e, ss=s: e.dma_start(
                        out=dbg_x2h.ap()[ss], in_=X2H[:, :]), deps=[x2_last])
                    nd["x2_used"] = t_dx2

                h_toks = []
                ev_prev = None
                for ht in range(NHT):
                    sl = ht % 2
                    def mbg(h=ht, b=sl):
                        return lambda e: e.dma_start(
                            out=BGB[b][:, :].rearrange("p (k a) -> p k a", k=NKT),
                            in_=dram["bg_in"].ap()[h].rearrange("k p a -> p k a"))

                    def mbu(h=ht, b=sl):
                        return lambda e: e.dma_start(
                            out=BUB[b][:, :].rearrange("p (k a) -> p k a", k=NKT),
                            in_=dram["bu_in"].ap()[h].rearrange("k p a -> p k a"))
                    t_bg = em.dma_batch("gpsimd", [mbg(), mbu()],
                                        deps=[bg_slot_free[sl]] if bg_slot_free[sl]
                                        else (join if s == 0 else []))
                    t_bu = t_bg
                    g_pairs = []
                    u_pairs = []
                    for xb in (X2H, X2L):
                        for kt in range(NKT):
                            g_pairs.append((BGB[sl][:, kt * 128:(kt + 1) * 128],
                                            xb[:, kt * SLAB:(kt + 1) * SLAB]))
                            u_pairs.append((BUB[sl][:, kt * 128:(kt + 1) * 128],
                                            xb[:, kt * SLAB:(kt + 1) * SLAB]))
                    t_g = em.mm_group(g_pairs, P[5][:, 0:SLAB],
                                      deps=x2_toks + [t_bg] +
                                      ([bank_free[5]] if bank_free[5] else []))
                    t_u = em.mm_group(u_pairs, P[4][:, 0:SLAB],
                                      deps=[t_bu, x2_last] +
                                      ([bank_free[4]] if bank_free[4] else []))
                    bg_slot_free[sl] = t_u
                    t_e0 = em.op("vector", lambda e: e.tensor_copy(
                        GBUF, P[5][:, 0:SLAB]),
                        deps=[t_g] + ([ev_prev] if ev_prev else []) +
                        ([x2_last] if ht == 0 else []))
                    bank_free[5] = t_e0
                    def me1(h=ht):
                        return lambda e: e.activation(
                            SIG[:, 0:SLAB], GBUF, AF.Sigmoid,
                            scale=sb_sg[:, h:h + 1])
                    t_e1 = em.op("scalar", me1(),
                                 deps=[t_e0] + ([ev_prev] if ev_prev else []))
                    t_e2 = em.op("vector", lambda e: e.tensor_mul(
                        SQ[:, 0:SLAB], GBUF, SIG[:, 0:SLAB]),
                        deps=[t_e1])
                    t_e3 = em.op("vector", lambda e: e.tensor_mul(
                        SQ[:, 0:SLAB], SQ[:, 0:SLAB], P[4][:, 0:SLAB]),
                        deps=[t_e2, t_u])
                    bank_free[4] = t_e3
                    def mh(h=ht):
                        return lambda e: e.activation(
                            HH[:, h * SLAB:(h + 1) * SLAB], SQ[:, 0:SLAB], AF.Copy,
                            scale=sb_sgu[:, h:h + 1])

                    def ml(h=ht):
                        return lambda e: e.scalar_tensor_tensor(
                            HL[:, h * SLAB:(h + 1) * SLAB], SQ[:, 0:SLAB],
                            sb_sgu[:, h:h + 1], HH[:, h * SLAB:(h + 1) * SLAB],
                            ALU.mult, ALU.subtract)
                    t_h_ = em.op("scalar", mh(),
                                 deps=[t_e3, tok_sgu] +
                                 ([pd["hh_used"]] if (s and ht == 0) else []))
                    t_l_ = em.op("vector", ml(), deps=[t_h_, t_e3])
                    ev_prev = t_l_
                    h_toks += [t_h_, t_l_]
                h_last = h_toks[-1]
                if debug_dump:
                    t_dhh = em.dma("gpsimd", lambda e, ss=s: e.dma_start(
                        out=dbg_hh.ap()[ss], in_=HH[:, :]), deps=h_toks)
                    nd["hh_used"] = t_dhh

                hs_pairs = []
                for hb in (HH, HL):
                    for ht in range(NHT):
                        hs_pairs.append((sb_onesb[:, :],
                                         hb[:, ht * SLAB:(ht + 1) * SLAB]))
                t_hs = em.mm_group(hs_pairs, P[0][0:1, 0:SLAB],
                                   deps=h_toks + [tok_onesb] +
                                   ([bank_free[0]] if bank_free[0] else []))
                t_hsum = em.op("vector", lambda e: e.tensor_copy(
                    HSUM[:, :], P[0][0:1, 0:SLAB]), deps=[t_hs, t_rf2])
                t_cmm = em.mm_group([(sb_c128_row[:, :], HSUM[:, :])], P[0][:, 0:SLAB],
                                    deps=[t_hsum])
                t_corr = em.op("vector", lambda e: e.tensor_copy(
                    CORR[:, :], P[0][:, 0:SLAB]), deps=[t_cmm])
                bank_free[0] = t_corr

                r2_toks = []
                NHH = NHT // 2
                for mt in range(NKT):
                    sl = mt % 2
                    t_half = None
                    for half in range(2):
                        hb_i = (mt * 2 + half) % 2
                        def mbd(m=mt, hf=half, b=hb_i):
                            return lambda e: e.dma_start(
                                out=BDB[b][:, :].rearrange("p (k a) -> p k a",
                                                           k=NHH),
                                in_=dram["bd_in"].ap()[m, hf * NHH:(hf + 1) * NHH]
                                .rearrange("k p a -> p k a"))
                        t_bd = em.dma("gpsimd", mbd(),
                                      deps=[bd_slot_free[hb_i]] if bd_slot_free[hb_i]
                                      else (join if s == 0 else []))
                        d_pairs = []
                        for hb in (HH, HL):
                            for hi in range(NHH):
                                ht = half * NHH + hi
                                d_pairs.append(
                                    (BDB[hb_i][:, hi * 128:(hi + 1) * 128],
                                     hb[:, ht * SLAB:(ht + 1) * SLAB]))
                        gdeps = [h_last, t_bd]
                        if half == 0 and bank_free[6 + sl]:
                            gdeps.append(bank_free[6 + sl])
                        if half == 1:
                            gdeps = [t_bd]
                        t_half = em.mm_group(d_pairs, PW[sl][:, 0:SLAB],
                                             deps=gdeps, start=(half == 0),
                                             stop=(half == 1), inc=True)
                        bd_slot_free[hb_i] = t_half
                    t_d = t_half
                    def ma1(pk=PW[sl]):
                        return lambda e: e.tensor_add(
                            SIG[:, 0:SLAB], pk[:, 0:SLAB], CORR[:, :])

                    def ma2(m=mt):
                        return lambda e: e.scalar_tensor_tensor(
                            RR[:, m * SLAB:(m + 1) * SLAB], SIG[:, 0:SLAB],
                            sb_sd[:, m:m + 1], X2H[:, m * SLAB:(m + 1) * SLAB],
                            ALU.mult, ALU.add)

                    def ma3(m=mt):
                        return lambda e: e.tensor_add(
                            RR[:, m * SLAB:(m + 1) * SLAB],
                            RR[:, m * SLAB:(m + 1) * SLAB],
                            X2L[:, m * SLAB:(m + 1) * SLAB])
                    t_a1 = em.op("vector", ma1(),
                                 deps=[t_d, t_corr] + ([h_last] if mt == 0 else []) +
                                 (r2_toks[-1:] if r2_toks else []))
                    t_a2 = em.op("vector", ma2(), deps=[t_a1, x2_last])
                    t_a3 = em.op("vector", ma3(), deps=[t_a2])
                    bank_free[6 + sl] = t_a3
                    r2_toks.append(t_a3)
                nd["hh_used"] = r2_toks[-1]
                nd["x2_used"] = r2_toks[-1]
                nd["rr_used"] = r2_toks[-1]
                if debug_dump:
                    t_dr2 = em.dma("gpsimd", lambda e, ss=s: e.dma_start(
                        out=dbg_r2.ap()[ss], in_=RR[:, :]), deps=r2_toks)
                    nd["rr_used"] = t_dr2

                out_prev = None
                for bk in range(2):
                    ev_toks = []
                    for q in range(4):
                        bidx = 2 + (q % 2)
                        fill_deps = [r2_toks[q * 4 + i] for i in range(4)] + \
                            ([bank_free[bidx]] if bank_free[bidx] else [])
                        t_tp = None
                        for i in range(4):
                            ft = q * 4 + i
                            def mkt2(b=bk, f=ft, o=i * 128, pk=P[bidx]):
                                return lambda e: e.transpose(
                                    pk[:, o:o + 128],
                                    RR[:, f * SLAB + b * 128: f * SLAB + (b + 1) * 128],
                                    sb_ident[:, :])
                            t_tp = em.op("tensor", mkt2(),
                                         deps=fill_deps if i == 0 else (),
                                         inc=(i == 3))
                        def mev(qq=q, pk=P[bidx]):
                            return lambda e: e.tensor_copy(
                                R2TM[:, qq * 512:(qq + 1) * 512], pk[:, :])
                        t_ev = em.op("vector", mev(),
                                     deps=[t_tp] + ([out_prev] if (out_prev and q == 0)
                                                    else []))
                        bank_free[bidx] = t_ev
                        ev_toks.append(t_ev)
                    t_sum = em.op("vector", lambda e: e.tensor_reduce(
                        MCOL[:, 0:1], R2TM[:, :], AX.X, ALU.add), deps=ev_toks)
                    t_sq_ = em.op("scalar", lambda e: e.square(SQ[:, :], R2TM[:, :]),
                                  deps=ev_toks + h_toks[-1:])
                    t_ssq = em.op("vector", lambda e: e.tensor_reduce(
                        MCOL[:, 1:2], SQ[:, :], AX.X, ALU.add), deps=[t_sq_])
                    t_m = em.op("vector", lambda e: e.tensor_scalar_mul(
                        VCOL[:, 0:1], MCOL[:, 0:1], 1.0 / D), deps=[t_sum])
                    t_m2 = em.op("vector", lambda e: e.tensor_mul(
                        VCOL[:, 1:2], VCOL[:, 0:1], VCOL[:, 0:1]), deps=[t_m])
                    t_v = em.op("vector", lambda e: e.scalar_tensor_tensor(
                        SCOL[:, 0:1], MCOL[:, 1:2], 1.0 / D, VCOL[:, 1:2],
                        ALU.mult, ALU.subtract), deps=[t_ssq, t_m2])
                    t_sr = em.op("scalar", lambda e: e.activation(
                        SCOL[:, 1:2], SCOL[:, 0:1], AF.Sqrt, bias=LN_EPS),
                        deps=[t_v])
                    t_rc_ = em.op("vector", lambda e: e.reciprocal(
                        SCOL[:, 1:2], SCOL[:, 1:2]), deps=[t_sr])
                    t_ve_ = em.op("vector", lambda e: e.tensor_scalar_add(
                        SCOL[:, 0:1], SCOL[:, 0:1], LN_EPS), deps=[t_v])
                    t_rf_ = newton_rsqrt("vector", SCOL[:, 1:2], SCOL[:, 0:1],
                                         SCOL[:, 2:3], [t_rc_, t_ve_])
                    t_out = em.op("vector", lambda e: e.tensor_scalar(
                        R2TM[:, :], R2TM[:, :], VCOL[:, 0:1], SCOL[:, 1:2],
                        ALU.subtract, ALU.mult), deps=[t_rf_] + ev_toks)
                    def mst(b=bk, ss=s):
                        return lambda e: e.dma_start(
                            out=out_t.ap()[ss * SLAB + b * 128:
                                           ss * SLAB + (b + 1) * 128, :],
                            in_=R2TM[:, :])
                    t_st = em.dma("gpsimd", mst(), deps=[t_out])
                    out_prev = t_st
                nd["out_done"] = out_prev
                pd = nd

            em.wait("gpsimd", [("QG", em.n["QG"])])

        def replay(eng_name):
            def go(eng):
                for fn, waits, inckey in em.plan[eng_name]:
                    for (k, v) in waits:
                        eng.wait_ge(sems[k], v)
                    if fn is None:
                        continue
                    ins = fn(eng)
                    if inckey is not None:
                        ins.then_inc(sems[inckey], 16 if inckey.startswith("Q") else 1)
            return go

        blk.sync(replay("sync"))
        blk.tensor(replay("tensor"))
        blk.vector(replay("vector"))
        blk.scalar(replay("scalar"))
        blk.gpsimd(replay("gpsimd"))

    return nc



def _prep_inputs(inputs):
    x = np.asarray(inputs["L0_attn_x"], dtype=np.float32)[0]
    xT = np.ascontiguousarray(x.T)
    xg = np.ascontiguousarray(
        xT.reshape(NKT, 128, NGRP, GW).transpose(2, 0, 1, 3))
    wq = np.asarray(inputs["L0_attn_Wq"], dtype=np.float32)
    wk = np.asarray(inputs["L0_attn_Wk"], dtype=np.float32)
    wv = np.asarray(inputs["L0_attn_Wv"], dtype=np.float32)
    wo = np.asarray(inputs["L0_attn_Wo"], dtype=np.float32)
    wg1 = np.asarray(inputs["L0_attn_Wg1"], dtype=np.float32)
    wg2 = np.asarray(inputs["L0_attn_Wg2"], dtype=np.float32)
    nw = np.asarray(inputs["L0_attn_norm_weight"], dtype=np.float32)
    bg = np.asarray(inputs["L0_ffn_wq_gate"]).astype(np.uint8)
    bu = np.asarray(inputs["L0_ffn_wq_up"]).astype(np.uint8)
    bd = np.asarray(inputs["L0_ffn_wq_down"]).astype(np.uint8)
    sg = np.asarray(inputs["L0_ffn_s_gate"], dtype=np.float32)
    su = np.asarray(inputs["L0_ffn_s_up"], dtype=np.float32)
    sd = np.asarray(inputs["L0_ffn_s_down"], dtype=np.float32)

    wo_t = np.ascontiguousarray(
        wo.reshape(NKT, 128, NKT, 128).transpose(2, 0, 1, 3))
    wg1_t = np.ascontiguousarray(wg1.reshape(NKT, 128, 128))
    bg_t = np.ascontiguousarray(
        bg.reshape(NHT, 128, NKT, 128).transpose(0, 2, 3, 1))
    bu_t = np.ascontiguousarray(
        bu.reshape(NHT, 128, NKT, 128).transpose(0, 2, 3, 1))
    bd_t = np.ascontiguousarray(
        bd.reshape(NKT, 128, NHT, 128).transpose(0, 2, 3, 1))
    sg_c = np.ascontiguousarray(sg.reshape(NHT, 128).T)
    su_c = np.ascontiguousarray(su.reshape(NHT, 128).T)
    sd_c = np.ascontiguousarray(sd.reshape(NKT, 128).T)
    nw_c = np.ascontiguousarray(nw.reshape(NKT, 128).T)

    j = np.arange(64, dtype=np.float64)
    theta = BASE ** (-2.0 * j / HD)
    th128 = np.concatenate([theta, theta])
    pos = np.arange(SEQ, dtype=np.float64)
    ang = th128[:, None] * pos[None, :]
    cos_t = np.cos(ang).astype(np.float32)
    sgn128 = np.where(np.arange(128) < 64, -1.0, 1.0)
    sinsg_t = (np.sin(ang) * sgn128[:, None]).astype(np.float32)
    cos_g = np.ascontiguousarray(cos_t.reshape(128, NGRP, GW).transpose(1, 0, 2))
    sinsg_g = np.ascontiguousarray(sinsg_t.reshape(128, NGRP, GW).transpose(1, 0, 2))
    ident = np.eye(128, dtype=np.float32)
    perm = np.zeros((128, 128), dtype=np.float32)
    perm[np.arange(128), (np.arange(128) + 64) % 128] = 1.0
    slopes = 2.0 ** (-8.0 * (np.arange(HEADS, dtype=np.float64) + 1.0) / HEADS)
    log_lam = -slopes
    i128 = np.arange(128, dtype=np.float64)
    tmins = i128[:, None] - i128[None, :]

    base = dict(
        xg_in=xg,
        wo_in=wo_t, wg1_in=wg1_t, wg2_in=wg2,
        bg_in=bg_t, bu_in=bu_t, bd_in=bd_t,
        sg_in=sg_c, su_in=su_c, sd_in=sd_c, nw_in=nw_c,
        nwtm_in=np.ascontiguousarray(np.broadcast_to(nw[None, :], (128, D))),
        cos_in=cos_g, sinsg_in=sinsg_g,
        ones_row_in=np.ones((1, 128), np.float32),
        c128_row_in=np.full((1, 128), -128.0, np.float32),
        invd_col_in=np.full((128, 1), 1.0 / D, np.float32),
        ones_colb_in=np.ones((128, 1), np.float32),
        ident_in=ident, perm_in=perm,
        zero_col_in=np.zeros((128, 1), np.float32),
        one_col_in=np.ones((128, 1), np.float32),
    )

    in_maps = []
    for c in range(NCORES):
        m = dict(base)
        m["xs_in"] = np.ascontiguousarray(
            xT[:, c * TPC:(c + 1) * TPC].reshape(NKT, 128, TPC))
        for nm, w in (("wq_in", wq), ("wk_in", wk), ("wv_in", wv)):
            sl = w[:, c * HPC * HD:(c + 1) * HPC * HD]
            m[nm] = np.ascontiguousarray(
                sl.reshape(NKT, 128, HPC, HD).transpose(2, 0, 1, 3))
        ll = log_lam[c * HPC:(c + 1) * HPC]
        dmat = np.where(tmins[None] >= 0, np.exp(ll[:, None, None] * tmins[None]), 0.0)
        m["dt_in"] = np.ascontiguousarray(
            dmat.transpose(0, 2, 1)).astype(np.float32)
        dq = np.exp(ll[:, None] * (i128[None, :] + 1.0))
        m["dq_in"] = np.ascontiguousarray(
            np.broadcast_to(np.tile(dq, (1, GRP))[:, None, :],
                            (HPC, 128, GW))).astype(np.float32)
        dk = np.exp(ll[:, None] * (CHUNK - 1.0 - i128[None, :]))
        m["dk_in"] = np.ascontiguousarray(dk.T).astype(np.float32)
        lamC = np.exp(ll * CHUNK)
        m["lam_in"] = np.ascontiguousarray(
            np.broadcast_to(lamC[None, :], (128, HPC))).astype(np.float32)
        in_maps.append(m)
    return in_maps


_NC_CACHE = {}


def kernel(**inputs):
    in_maps = _prep_inputs(inputs)
    if "nc" not in _NC_CACHE:
        _NC_CACHE["nc"] = build_program()
    nc = _NC_CACHE["nc"]
    res = run_bass_kernel_spmd(nc, in_maps, core_ids=list(range(NCORES)))
    out = np.concatenate([np.asarray(res.results[c]["out"]).reshape(TPC, D)
                          for c in range(NCORES)], axis=0)
    return np.ascontiguousarray(out.reshape(1, SEQ, D).astype(np.float32))


# version 09
# speedup vs baseline: 1.0113x; 1.0113x over previous
import os
os.environ.setdefault("TRN_ALLOW_ICI_ALL_TO_ALL", "1")

from contextlib import ExitStack

import numpy as np

import concourse.bass as bass
import concourse.mybir as mybir
from concourse.bass_utils import run_bass_kernel_spmd

F32 = mybir.dt.float32
BF16 = mybir.dt.bfloat16
U8 = mybir.dt.uint8
AF = mybir.ActivationFunctionType
ALU = mybir.AluOpType
AX = mybir.AxisListType

NCORES = 8
SEQ = 8192
D = 2048
HEADS = 16
HD = 128
CHUNK = 128
NCHUNK = SEQ // CHUNK
GRP = 4
NGRP = NCHUNK // GRP
HPC = HEADS // NCORES
TPC = SEQ // NCORES
SLAB = 256
NSLAB = TPC // SLAB
HID = 5632
NHT = HID // 128
NKT = D // 128
GW = GRP * 128
PI = float(np.pi)
LN_EPS = 1e-5
RMS_EPS = 1e-6
BASE = 10000.0

ENG2KEY = {"tensor": "T", "vector": "V", "scalar": "S", "gpsimd": "G"}
DMA2KEY = {"sync": "QS", "scalar": "QC", "gpsimd": "QG"}


class Em:

    def __init__(self, max_ops=None):
        self.plan = {e: [] for e in ("tensor", "vector", "scalar", "gpsimd", "sync")}
        self.n = {k: 0 for k in ("T", "V", "S", "G", "QS", "QC", "QG", "CC")}
        self.wm = {e: {} for e in self.plan}
        self.max_ops = max_ops
        self.emitted = 0

    def _emit(self, eng, fn, deps, inc_key, inc_amt):
        if self.max_ops is not None and fn is not None:
            if self.emitted >= self.max_ops:
                return None
            self.emitted += 1
        waits = []
        for d in deps:
            if d is None:
                continue
            k, v = d
            if v <= 0 or self.wm[eng].get(k, 0) >= v:
                continue
            self.wm[eng][k] = v
            waits.append((k, v))
        tok = None
        if inc_key is not None:
            self.n[inc_key] += inc_amt
            tok = (inc_key, self.n[inc_key])
        self.plan[eng].append((fn, waits, inc_key))
        return tok

    def op(self, eng, fn, deps=(), inc=True):
        return self._emit(eng, fn, deps, ENG2KEY[eng] if inc else None, 1)

    def dma(self, eng, fn, deps=()):
        return self.dma_batch(eng, [fn], deps)

    def dma_batch(self, eng, fns, deps=()):
        key = DMA2KEY[eng]
        pre = self.n[key]
        deps = list(deps)
        if pre:
            deps.append((key, pre))
        tok = None
        for i, fn in enumerate(fns):
            tok = self._emit(eng, fn, deps if i == 0 else (), key, 16)
        return tok

    def cc(self, fn, deps=()):
        return self._emit("gpsimd", fn, deps, "CC", 1)

    def wait(self, eng, deps):
        self._emit(eng, None, deps, None, 0)

    def join_tokens(self):
        return [(k, v) for k, v in self.n.items() if v > 0]

    def mm_group(self, pairs, out_ap, deps, start=True, stop=True, inc=True):
        n = len(pairs)
        tok = None
        for i, (l, r) in enumerate(pairs):
            def mk(l=l, r=r, o=out_ap, st=(start and i == 0),
                   sp=(stop and i == n - 1)):
                return lambda e: e.matmul(o, l, r, start=st, stop=sp)
            tok = self.op("tensor", mk(), deps=deps if i == 0 else (),
                          inc=(inc and i == n - 1))
        return tok


def build_program(debug_dump=False, stop="full", max_ops=None):
    nc = bass.Bass("TRN2", target_bir_lowering=False, debug=False)
    em = Em(max_ops)

    dram = {}

    def din(name, shape, dtype):
        dram[name] = nc.dram_tensor(name, list(shape), dtype, kind="ExternalInput")
        return dram[name]

    din("xg_in", (NGRP, NKT, 128, GW), F32)
    din("xs_in", (NKT, 128, TPC), F32)
    din("wq_in", (HPC, NKT, 128, 128), F32)
    din("wk_in", (HPC, NKT, 128, 128), F32)
    din("wv_in", (HPC, NKT, 128, 128), F32)
    din("wo_in", (NKT, NKT, 128, 128), F32)
    din("wg1_in", (NKT, 128, 128), F32)
    din("wg2_in", (128, D), F32)
    din("bg_in", (NHT, NKT, 128, 128), U8)
    din("bu_in", (NHT, NKT, 128, 128), U8)
    din("bd_in", (NKT, NHT, 128, 128), U8)
    din("sg_in", (128, NHT), F32)
    din("su_in", (128, NHT), F32)
    din("sd_in", (128, NKT), F32)
    din("nw_in", (128, NKT), F32)
    din("nwtm_in", (128, D), F32)
    din("cos_in", (NGRP, 128, GW), F32)
    din("sinsg_in", (NGRP, 128, GW), F32)
    din("ones_row_in", (1, 128), F32)
    din("c128_row_in", (1, 128), F32)
    din("invd_col_in", (128, 1), F32)
    din("ones_colb_in", (128, 1), F32)
    din("ident_in", (128, 128), F32)
    din("perm_in", (128, 128), F32)
    din("zero_col_in", (128, 1), F32)
    din("one_col_in", (128, 1), F32)
    din("dt_in", (HPC, 128, 128), F32)
    din("dq_in", (HPC, 128, GW), F32)
    din("dk_in", (128, HPC), F32)
    din("lam_in", (128, HPC), F32)

    out_t = nc.dram_tensor("out", [TPC, D], F32, kind="ExternalOutput")
    a2a_in = nc.dram_tensor("a2a_in", [NCORES, TPC, HPC * HD], F32)
    a2a_out = nc.dram_tensor("a2a_out", [NCORES, TPC, HPC * HD], F32)
    if debug_dump:
        dbg_ain = nc.dram_tensor("dbg_ain", [NCORES, TPC, HPC * HD], F32,
                                 kind="ExternalOutput")
        dbg_aout = nc.dram_tensor("dbg_aout", [NCORES, TPC, HPC * HD], F32,
                                  kind="ExternalOutput")
        dbg_onh = nc.dram_tensor("dbg_onh", [NSLAB, 128, NKT * SLAB], BF16,
                                 kind="ExternalOutput")
        dbg_onl = nc.dram_tensor("dbg_onl", [NSLAB, 128, NKT * SLAB], BF16,
                                 kind="ExternalOutput")
        dbg_rr = nc.dram_tensor("dbg_rr", [NSLAB, 128, NKT * SLAB], F32,
                                kind="ExternalOutput")
        dbg_x2h = nc.dram_tensor("dbg_x2h", [NSLAB, 128, NKT * SLAB], BF16,
                                 kind="ExternalOutput")
        dbg_hh = nc.dram_tensor("dbg_hh", [NSLAB, 128, NHT * SLAB], BF16,
                                kind="ExternalOutput")
        dbg_r2 = nc.dram_tensor("dbg_r2", [NSLAB, 128, NKT * SLAB], F32,
                                kind="ExternalOutput")

    sem_names = ["T", "V", "S", "G", "QS", "QC", "QG", "CC"]

    with ExitStack() as top:
        sems = {k: top.enter_context(nc.semaphore(k)) for k in sem_names}
        P = [top.enter_context(nc.psum_tensor(f"pb{i}", [128, 512], F32))
             for i in range(7)]
        bank_free = [None] * 8
        blk = top.enter_context(nc.Block())

        def sbuf(name, shape, dtype, stack):
            return stack.enter_context(nc.sbuf_tensor(name, list(shape), dtype))

        sb_ones_row = sbuf("c_onesr", [1, 128], F32, top)
        sb_c128_row = sbuf("c_c128r", [1, 128], F32, top)
        sb_invd = sbuf("c_invd", [128, 1], F32, top)
        sb_onesb_f = sbuf("c_onesbf", [128, 1], F32, top)
        sb_onesb = sbuf("c_onesb", [128, 1], BF16, top)
        sb_ident = sbuf("c_ident", [128, 128], F32, top)
        sb_perm = sbuf("c_perm", [128, 128], F32, top)
        sb_zero = sbuf("c_zero", [128, 1], F32, top)
        sb_one = sbuf("c_one", [128, 1], F32, top)
        sb_dk = sbuf("c_dk", [128, HPC], F32, top)
        sb_lam = sbuf("c_lam", [128, HPC], F32, top)
        sb_sg = sbuf("c_sg", [128, NHT], F32, top)
        sb_su = sbuf("c_su", [128, NHT], F32, top)
        sb_sgu = sbuf("c_sgu", [128, NHT], F32, top)
        sb_sd = sbuf("c_sd", [128, NKT], F32, top)
        sb_nw = sbuf("c_nw", [128, NKT], F32, top)
        WG1H = sbuf("wg1h", [128, NKT * 128], BF16, top)
        WG1L = sbuf("wg1l", [128, NKT * 128], BF16, top)
        WG2H = sbuf("wg2h", [128, D], BF16, top)
        WG2L = sbuf("wg2l", [128, D], BF16, top)

        sb_epsc = sbuf("c_epsc", [128, 1], F32, top)
        nc.const_aps.register(F32, 0.0, sb_zero[:, :])
        nc.const_aps.register(F32, LN_EPS, sb_epsc[:, :])
        em.op("vector", lambda e: e.memset(sb_epsc[:, :], LN_EPS), deps=())

        def ld(dst_ap, src_ap, eng="sync", deps=()):
            def f(e, d=dst_ap, s=src_ap):
                return e.dma_start(out=d, in_=s)
            return em.dma(eng, f, deps)

        tok_c = []
        tok_c.append(ld(sb_ones_row[:, :], dram["ones_row_in"].ap()))
        tok_c.append(ld(sb_c128_row[:, :], dram["c128_row_in"].ap()))
        tok_c.append(ld(sb_invd[:, :], dram["invd_col_in"].ap()))
        tok_c.append(ld(sb_onesb_f[:, :], dram["ones_colb_in"].ap()))
        tok_c.append(ld(sb_ident[:, :], dram["ident_in"].ap()))
        tok_c.append(ld(sb_perm[:, :], dram["perm_in"].ap()))
        tok_c.append(ld(sb_zero[:, :], dram["zero_col_in"].ap()))
        tok_c.append(ld(sb_one[:, :], dram["one_col_in"].ap()))
        tok_c.append(ld(sb_dk[:, :], dram["dk_in"].ap()))
        tok_c.append(ld(sb_lam[:, :], dram["lam_in"].ap()))
        tok_c.append(ld(sb_sg[:, :], dram["sg_in"].ap()))
        tok_c.append(ld(sb_su[:, :], dram["su_in"].ap()))
        tok_c.append(ld(sb_sd[:, :], dram["sd_in"].ap()))
        tok_c.append(ld(sb_nw[:, :], dram["nw_in"].ap()))
        tok_sgu = em.op("vector",
                        lambda e: e.tensor_mul(sb_sgu[:, :], sb_sg[:, :], sb_su[:, :]),
                        deps=tok_c)
        tok_onesb = em.op("scalar", lambda e: e.copy(sb_onesb[:, :], sb_onesb_f[:, :]),
                          deps=tok_c)

        def split_pair(f32_ap, hi_ap, lo_ap, dep_toks, scale_ap=None):
            if scale_ap is None:
                def fh(e, o=hi_ap, i=f32_ap):
                    return e.copy(o, i)

                def fl(e, o=lo_ap, i=f32_ap, h=hi_ap):
                    return e.tensor_sub(o, i, h)
            else:
                def fh(e, o=hi_ap, i=f32_ap, sc=scale_ap):
                    return e.activation(o, i, AF.Copy, scale=sc)

                def fl(e, o=lo_ap, i=f32_ap, sc=scale_ap, h=hi_ap):
                    return e.scalar_tensor_tensor(o, i, sc, h, ALU.mult, ALU.subtract)
            th = em.op("scalar", fh, deps=dep_toks)
            tl = em.op("vector", fl, deps=list(dep_toks) + [th])
            return th, tl


        NSEC = 8
        SECW = (NKT // NSEC) * GW
        with ExitStack() as pa:
            PB7 = pa.enter_context(nc.psum_tensor("pb7a", [128, 1024], BF16))
            sb_identb = sbuf("c_identb", [128, 128], BF16, pa)
            sb_dt = sbuf("c_dt", [128, HPC * 128], F32, pa)
            sb_dq = sbuf("c_dq", [128, HPC * GW], F32, pa)
            WQH = sbuf("wqh", [128, HPC * NKT * 128], BF16, pa)
            WQL = sbuf("wql", [128, HPC * NKT * 128], BF16, pa)
            WKH = sbuf("wkh", [128, HPC * NKT * 128], BF16, pa)
            WKL = sbuf("wkl", [128, HPC * NKT * 128], BF16, pa)
            WVH = sbuf("wvh", [128, HPC * NKT * 128], BF16, pa)
            WVL = sbuf("wvl", [128, HPC * NKT * 128], BF16, pa)
            XGT = [sbuf(f"XGT{i}", [128, SECW], F32, pa) for i in range(2)]
            XH = [sbuf(f"XH{i}", [128, NKT * GW], BF16, pa) for i in range(2)]
            XL = [sbuf(f"XL{i}", [128, NKT * GW], BF16, pa) for i in range(2)]

            tok_c.append(ld(sb_dt[:, :].rearrange("p (h a) -> p h a", h=HPC),
                            dram["dt_in"].ap().rearrange("h p a -> p h a")))
            tok_c.append(ld(sb_dq[:, :].rearrange("p (h a) -> p h a", h=HPC),
                            dram["dq_in"].ap().rearrange("h p a -> p h a")))
            tok_identb = em.op("vector",
                               lambda e: e.tensor_copy(sb_identb[:, :],
                                                       sb_ident[:, :]),
                               deps=tok_c)
            with nc.sbuf_tensor("wtmp", [128, HPC * NKT * 128], F32) as wtmp:
                prev_reads = list(tok_c)
                for nm, bh, bl in (("wq", WQH, WQL), ("wk", WKH, WKL),
                                   ("wv", WVH, WVL)):
                    t_ld = ld(wtmp[:, :].rearrange("p (h k a) -> p h k a",
                                                   h=HPC, k=NKT),
                              dram[nm + "_in"].ap().rearrange("h k p a -> p h k a"),
                              deps=prev_reads)
                    th, tl = split_pair(wtmp[:, :], bh[:, :], bl[:, :], [t_ld])
                    prev_reads = [th, tl]
                t_ld = ld(wtmp[:, 0:NKT * 128].rearrange("p (k a) -> p k a", k=NKT),
                          dram["wg1_in"].ap().rearrange("k p a -> p k a"),
                          deps=prev_reads)
                th, tl = split_pair(wtmp[:, 0:NKT * 128], WG1H[:, :], WG1L[:, :],
                                    [t_ld])
                t_ld = ld(wtmp[:, 0:D], dram["wg2_in"].ap(), deps=[th, tl])
                th, tl = split_pair(wtmp[:, 0:D], WG2H[:, :], WG2L[:, :], [t_ld])
                tok_wsplit = [th, tl]
            COS = [sbuf(f"COS{i}", [128, GW], F32, pa) for i in range(2)]
            SINSG = [sbuf(f"SINSG{i}", [128, GW], F32, pa) for i in range(2)]
            QS_ = sbuf("QS", [128, HPC * GW], F32, pa)
            KS_ = sbuf("KS", [128, HPC * GW], F32, pa)
            RT1 = sbuf("RT1", [128, GW], F32, pa)
            RT2 = sbuf("RT2", [128, GW], F32, pa)
            QR = QS_
            KR = KS_
            QRH = [sbuf(f"QRH{i}", [128, HPC * GW], BF16, pa) for i in range(2)]
            QRL = [sbuf(f"QRL{i}", [128, HPC * GW], BF16, pa) for i in range(2)]
            KRH = [sbuf(f"KRH{i}", [128, HPC * GW], BF16, pa) for i in range(2)]
            KRL = [sbuf(f"KRL{i}", [128, HPC * GW], BF16, pa) for i in range(2)]
            QDH = [sbuf(f"QDH{i}", [128, HPC * GW], BF16, pa) for i in range(2)]
            QDL = [sbuf(f"QDL{i}", [128, HPC * GW], BF16, pa) for i in range(2)]
            VH = [sbuf(f"VH{i}", [128, HPC * GW], BF16, pa) for i in range(2)]
            VL = [sbuf(f"VL{i}", [128, HPC * GW], BF16, pa) for i in range(2)]
            TKV = [sbuf(f"TKV{i}", [128, HPC * 512], BF16, pa) for i in range(2)]
            SCD = sbuf("SCD", [128, 128], F32, pa)
            SCDH = [sbuf(f"SCDH{i}", [128, 128], BF16, pa) for i in range(2)]
            SCDL = [sbuf(f"SCDL{i}", [128, 128], BF16, pa) for i in range(2)]
            ST = sbuf("STT", [128, HPC * 128], F32, pa)
            STH = [sbuf(f"STH{i}", [128, HPC * 128], BF16, pa) for i in range(2)]
            STL = [sbuf(f"STL{i}", [128, HPC * 128], BF16, pa) for i in range(2)]
            OST = [sbuf(f"OST{i}", [128, 128], F32, pa) for i in range(2)]
            if os.environ.get("KPRINT_SBUF"):
                print("phaseA peak, sbuf bytes remaining:", nc.sbuf_bytes_remaining)

            def gsl(buf, ht, lo=0, hi=GW):
                return buf[:, ht * GW + lo: ht * GW + hi]

            t_s0 = em.op("vector", lambda e: e.memset(ST[:, :], 0.0), deps=tok_c)
            st_tok = {}
            for ht_ in range(HPC):
                st_tok[("S", ht_)] = t_s0
            for par_ in range(2):
                t_h0 = em.op("scalar", lambda e, p=par_: e.memzero(STH[p][:, :]),
                             deps=tok_c)
                t_l0 = em.op("vector", lambda e, p=par_: e.memset(STL[p][:, :], 0.0),
                             deps=[t_s0])
                for ht_ in range(HPC):
                    st_tok[("H", ht_, par_)] = t_h0
                    st_tok[("L", ht_, par_)] = t_l0
            prev_o = None
            tkv_free = [None, None]
            scd_par_free = [None, None]

            xg_ld = [None] * NGRP
            xsplit = [None] * NGRP
            grp_data = [None] * NGRP
            grp_done = [None] * NGRP
            ost_free = [None, None]
            scd_free = None
            a2a_last = None
            trig_free = [None, None]

            sec_free = [None, None]
            def emit_load_split(g):
                b = g % 2
                kps = NKT // NSEC
                last_h = last_l = None
                for sec in range(NSEC):
                    tb = (g * NSEC + sec) % 2
                    deps = list(tok_wsplit)
                    if sec_free[tb] is not None:
                        deps.append(sec_free[tb])
                    t_ld = ld(XGT[tb][:, :].rearrange("p (k a) -> p k a", k=kps),
                              dram["xg_in"].ap()[g, sec * kps:(sec + 1) * kps]
                              .rearrange("k p a -> p k a"),
                              deps=deps)
                    sdeps = [t_ld]
                    if g >= 2 and sec == 0:
                        sdeps.append(grp_done[g - 2])
                    cs = slice(sec * SECW, (sec + 1) * SECW)
                    last_h, last_l = split_pair(XGT[tb][:, :], XH[b][:, cs],
                                                XL[b][:, cs], sdeps)
                    sec_free[tb] = last_l
                xsplit[g] = (last_h, last_l)

            UNIT_ORDER = [("q", 0), ("q", 1), ("k", 0), ("k", 1),
                          ("v", 0), ("v", 1)]

            def make_group_units(g):
                b = g % 2
                d = {"v": {}}
                grp_data[g] = d

                def unit(idx):
                    proj, ht = UNIT_ORDER[idx]
                    th_x, tl_x = xsplit[g]
                    if idx == 0:
                        tdeps = ([trig_free[b]] if trig_free[b] else [])
                        d["t_cos"] = ld(COS[b][:, :], dram["cos_in"].ap()[g],
                                        deps=tdeps)
                        d["t_sinsg"] = ld(SINSG[b][:, :],
                                          dram["sinsg_in"].ap()[g], deps=tdeps)
                    wh, wl = {"q": (WQH, WQL), "k": (WKH, WKL),
                              "v": (WVH, WVL)}[proj]
                    slot = idx % 2
                    bank = P[1 + slot]
                    pairs = []
                    for wb, xb in ((wh, XH[b]), (wh, XL[b]), (wl, XH[b])):
                        for kt in range(NKT):
                            pairs.append(
                                (wb[:, (ht * NKT + kt) * 128:
                                     (ht * NKT + kt + 1) * 128],
                                 xb[:, kt * GW:(kt + 1) * GW]))
                    mmdeps = [th_x, tl_x] + \
                        ([bank_free[1 + slot]] if bank_free[1 + slot] else [])
                    t_mm = em.mm_group(pairs, bank[:, 0:GW], mmdeps)
                    if proj == "v":
                        tvh = em.op("scalar",
                                    lambda e, p=bank, o=gsl(VH[b], ht):
                                    e.copy(o, p[:, 0:GW]), deps=[t_mm])
                        tvl = em.op("vector",
                                    lambda e, p=bank, o=gsl(VL[b], ht),
                                    h=gsl(VH[b], ht):
                                    e.tensor_sub(o, p[:, 0:GW], h),
                                    deps=[t_mm, tvh])
                        d["v"][ht] = (tvh, tvl)
                        bank_free[1 + slot] = tvl
                        return
                    sbuf_src = QS_ if proj == "q" else KS_
                    dst = gsl(sbuf_src, ht)
                    tsi = em.op("scalar",
                                lambda e, p=bank, o=dst:
                                e.activation(o, p[:, 0:GW], AF.Silu),
                                deps=[t_mm])
                    bank_free[1 + slot] = tsi
                    dsth, dstl = ((QRH[b], QRL[b]) if proj == "q"
                                  else (KRH[b], KRL[b]))
                    t_sw = em.mm_group(
                        [(sb_perm[:, :], dst)], P[3][:, 0:GW],
                        deps=[tsi] + ([bank_free[3]] if bank_free[3] else []))
                    t_m1 = em.op("vector",
                                 lambda e, s=dst, bb=b:
                                 e.tensor_mul(RT1[:, :], s, COS[bb][:, :]),
                                 deps=[tsi, d["t_cos"]])
                    t_m2 = em.op("vector",
                                 lambda e, bb=b: e.tensor_mul(
                                     RT2[:, :], P[3][:, 0:GW], SINSG[bb][:, :]),
                                 deps=[t_sw, d["t_sinsg"], t_m1])
                    bank_free[3] = t_m2
                    trig_free[b] = t_m2
                    t_rot = em.op("vector",
                                  lambda e, o=dst:
                                  e.tensor_add(o, RT1[:, :], RT2[:, :]),
                                  deps=[t_m1, t_m2])
                    sh, sl = split_pair(dst, gsl(dsth, ht), gsl(dstl, ht),
                                        [t_rot])
                    d[(proj, ht)] = (sh, sl)
                    if proj == "q":
                        t_qd = em.op("vector",
                                     lambda e, s=dst, q=gsl(sb_dq, ht):
                                     e.tensor_mul(RT1[:, :], s, q),
                                     deps=[t_rot, sh, sl])
                        dh, dl = split_pair(RT1[:, :], gsl(QDH[b], ht),
                                            gsl(QDL[b], ht), [t_qd])
                        d[("qd", ht)] = (dh, dl)

                return [lambda i=i: unit(i) for i in range(6)]

            def emit_o_lagged(ctx):
                nonlocal a2a_last, prev_o
                if ctx is None:
                    return None
                ip = ctx["it"] % 2
                hs = ctx["hs"]
                o_pairs = [
                    (SCDH[ip][:, :], TKV[ip][:, ctx["tb"] + 0:ctx["tb"] + 128]),
                    (SCDH[ip][:, :], TKV[ip][:, ctx["tb"] + 128:ctx["tb"] + 256]),
                    (SCDL[ip][:, :], TKV[ip][:, ctx["tb"] + 0:ctx["tb"] + 128]),
                    (ctx["qdh_ap"], STH[ctx["sp"]][:, hs]),
                    (ctx["qdh_ap"], STL[ctx["sp"]][:, hs]),
                    (ctx["qdl_ap"], STH[ctx["sp"]][:, hs])]
                t_o = em.mm_group(
                    o_pairs, P[5][:, 0:128],
                    deps=[ctx["sdh"], ctx["sdl"], ctx["tkv"], ctx["qdh"],
                          ctx["qdl"], ctx["stH"], ctx["stL"]] +
                    ([bank_free[5]] if bank_free[5] else []))
                ev_deps = [t_o]
                ob = ctx["it"] % 2
                if ost_free[ob] is not None:
                    ev_deps.append(ost_free[ob])
                t_oev = em.op("vector",
                              lambda e, o=OST[ob][:, 0:128]:
                              e.tensor_copy(o, P[5][:, 0:128]), deps=ev_deps)
                bank_free[5] = t_oev

                def mk(o=OST[ob],
                       dd=a2a_in.ap()[ctx["peer"],
                                      ctx["lch"] * 128:(ctx["lch"] + 1) * 128,
                                      ctx["ht"] * 128:(ctx["ht"] + 1) * 128]):
                    return lambda e: e.dma_start(out=dd, in_=o[:, 0:128])
                t_st = em.dma("gpsimd", mk(), deps=[t_oev])
                ost_free[ob] = t_st
                a2a_last = t_st
                tkv_free[ip] = t_o
                scd_par_free[ip] = t_o
                return t_o

            def make_scan_units(g):
                d = grp_data[g]
                pb_ = g % 2

                def unit(c):
                    nonlocal st_tok, scd_free, prev_o
                    gc = g * GRP + c
                    peer = gc // (TPC // CHUNK)
                    lch = gc % (TPC // CHUNK)
                    for ht in range(HPC):
                        it = gc * HPC + ht
                        ip = it % 2
                        hs = slice(ht * 128, (ht + 1) * 128)
                        tb = ht * 512
                        qh, ql = d[("q", ht)]
                        kh, kl = d[("k", ht)]
                        qdh, qdl = d[("qd", ht)]
                        vh, vl = d["v"][ht]
                        tr_deps = [vh, vl, kh, kl, tok_identb] + \
                            ([bank_free[7]] if bank_free[7] else [])
                        t_tr = None
                        srcs = [gsl(VH[pb_], ht, c * 128, (c + 1) * 128),
                                gsl(VL[pb_], ht, c * 128, (c + 1) * 128),
                                gsl(KRH[pb_], ht, c * 128, (c + 1) * 128),
                                gsl(KRL[pb_], ht, c * 128, (c + 1) * 128)]
                        for i, s_ap in enumerate(srcs):
                            def mk(s=s_ap, o=i * 128):
                                return lambda e: e.transpose(
                                    PB7[:, o:o + 128], s, sb_identb[:, :])
                            t_tr = em.op("tensor", mk(),
                                         deps=tr_deps if i == 0 else (),
                                         inc=(i == 3))
                        t_ev = em.op("vector",
                                     lambda e, o=TKV[ip][:, tb:tb + 512]:
                                     e.tensor_copy(o, PB7[:, 0:512]),
                                     deps=[t_tr] + ([tkv_free[ip]]
                                                    if tkv_free[ip] else []))
                        bank_free[7] = t_ev
                        t_kd = em.op("vector",
                                     lambda e, o=TKV[ip][:, tb + 256:tb + 512],
                                     k=sb_dk[:, ht:ht + 1]:
                                     e.tensor_scalar_mul(o, o, k), deps=[t_ev])

                        sc_pairs = [
                            (gsl(KRH[pb_], ht, c * 128, (c + 1) * 128),
                             gsl(QRH[pb_], ht, c * 128, (c + 1) * 128)),
                            (gsl(KRH[pb_], ht, c * 128, (c + 1) * 128),
                             gsl(QRL[pb_], ht, c * 128, (c + 1) * 128)),
                            (gsl(KRL[pb_], ht, c * 128, (c + 1) * 128),
                             gsl(QRH[pb_], ht, c * 128, (c + 1) * 128))]
                        t_sc = em.mm_group(
                            sc_pairs, P[4][:, 0:128],
                            deps=[qh, ql, kh, kl] +
                            ([bank_free[4]] if bank_free[4] else []))
                        kv_pairs = [
                            (TKV[ip][:, tb + 256:tb + 384],
                             TKV[ip][:, tb + 0:tb + 128]),
                            (TKV[ip][:, tb + 256:tb + 384],
                             TKV[ip][:, tb + 128:tb + 256]),
                            (TKV[ip][:, tb + 384:tb + 512],
                             TKV[ip][:, tb + 0:tb + 128])]
                        t_kv = em.mm_group(
                            kv_pairs, P[6][:, 0:128],
                            deps=[t_kd, t_ev] +
                            ([bank_free[6]] if bank_free[6] else []))
                        grp_done[g] = t_kv

                        t_o_prev = emit_o_lagged(prev_o)

                        t_scd = em.op("vector", lambda e, dt=sb_dt[:, hs]:
                                      e.tensor_mul(SCD[:, :], P[4][:, 0:128], dt),
                                      deps=[t_sc] + ([scd_free] if scd_free
                                                     else []))
                        bank_free[4] = t_scd
                        t_sdh, t_sdl = split_pair(SCD[:, :], SCDH[ip][:, :],
                                                  SCDL[ip][:, :],
                                                  [t_scd] + ([scd_par_free[ip]]
                                                             if scd_par_free[ip]
                                                             else []))
                        scd_free = t_sdl
                        sp_new = (c + 1) % 2
                        t_sup = em.op("vector",
                                      lambda e, s=ST[:, hs],
                                      lam=sb_lam[:, ht:ht + 1]:
                                      e.scalar_tensor_tensor(
                                          s, s, lam, P[6][:, 0:128],
                                          ALU.mult, ALU.add),
                                      deps=[t_kv, st_tok[("S", ht)]])
                        bank_free[6] = t_sup
                        nh_deps = [t_sup]
                        if t_o_prev is not None:
                            nh_deps.append(t_o_prev)
                        if st_tok[("H", ht, sp_new)] is not None:
                            nh_deps.append(st_tok[("H", ht, sp_new)])
                        t_nh = em.op("scalar",
                                     lambda e, s=ST[:, hs],
                                     o=STH[sp_new][:, hs]:
                                     e.copy(o, s), deps=nh_deps)
                        t_nl = em.op("vector",
                                     lambda e, s=ST[:, hs], o=STL[sp_new][:, hs],
                                     h=STH[sp_new][:, hs]:
                                     e.tensor_sub(o, s, h), deps=[t_sup, t_nh])
                        st_tok[("S", ht)] = t_sup
                        st_tok[("H", ht, sp_new)] = t_nh
                        st_tok[("L", ht, sp_new)] = t_nl
                        prev_o = {
                            "it": it, "hs": hs, "tb": tb, "ht": ht,
                            "peer": peer, "lch": lch, "sp": c % 2,
                            "sdh": t_sdh, "sdl": t_sdl, "tkv": t_kd,
                            "qdh": qdh, "qdl": qdl,
                            "qdh_ap": gsl(QDH[pb_], ht, c * 128, (c + 1) * 128),
                            "qdl_ap": gsl(QDL[pb_], ht, c * 128, (c + 1) * 128),
                            "stH": st_tok[("H", ht, c % 2)],
                            "stL": st_tok[("L", ht, c % 2)],
                        }

                return [lambda c=c: unit(c) for c in range(GRP)]

            ngrp_run = 0 if stop == "init" else NGRP
            for g in range(ngrp_run):
                emit_load_split(g)
                if prev_o is not None:
                    emit_o_lagged(prev_o)
                    prev_o = None
                gu = make_group_units(g)
                su = make_scan_units(g - 1) if (g >= 1 and stop != "noscan") else []
                seq = []
                for i in range(6):
                    seq.append(gu[i])
                    if i < len(su):
                        seq.append(su[i])
                seq += su[6:]
                for u in seq:
                    u()
            if ngrp_run and stop != "noscan":
                for u in make_scan_units(ngrp_run - 1):
                    u()
                emit_o_lagged(prev_o)
                prev_o = None

            if stop in ("phaseA", "init", "noscan"):
                em.wait("gpsimd", [("QG", em.n["QG"])])
                t_cc = None
            else:
                t_cc = em.cc(lambda e: e.collective_compute(
                    "AllToAll", ALU.bypass, replica_groups=[list(range(NCORES))],
                    ins=[a2a_in.ap().opt()], outs=[a2a_out.ap().opt()]),
                    deps=[a2a_last, ("QG", em.n["QG"])])
            if debug_dump:
                em.dma("gpsimd", lambda e: e.dma_start(
                    out=dbg_ain.ap().rearrange("c t d -> c (t d)"),
                    in_=a2a_in.ap().rearrange("c t d -> c (t d)")), deps=[t_cc])
                em.dma("gpsimd", lambda e: e.dma_start(
                    out=dbg_aout.ap().rearrange("c t d -> c (t d)"),
                    in_=a2a_out.ap().rearrange("c t d -> c (t d)")), deps=[t_cc])

        join = em.join_tokens()
        for i in range(8):
            bank_free[i] = None

        with ExitStack() as pb:
            P7F = pb.enter_context(nc.psum_tensor("pb7b", [128, 512], F32))
            PW = [P[6], P7F]
            sb_nwtm = sbuf("c_nwtm", [128, D], F32, pb)
            tok_nwtm = ld(sb_nwtm[:, :], dram["nwtm_in"].ap(), deps=join)
            XBT = [sbuf(f"XBT{i}", [128, (NKT // 8) * SLAB], F32, pb)
                   for i in range(2)]
            XBH = sbuf("XBH", [128, NKT * SLAB], BF16, pb)
            XBL = sbuf("XBL", [128, NKT * SLAB], BF16, pb)
            G1H = sbuf("G1H", [128, SLAB], BF16, pb)
            G1L = sbuf("G1L", [128, SLAB], BF16, pb)
            SIG = sbuf("SIGB", [128, 512], F32, pb)
            GBUF = SIG[:, 256:512]
            OTM = sbuf("OTM", [128, D], F32, pb)
            SQ = sbuf("SQ", [128, D], F32, pb)
            MS = sbuf("MS", [128, NKT], F32, pb)
            RSD = sbuf("RSD", [128, NKT], F32, pb)
            TMP1 = sbuf("TMP1", [128, NKT], F32, pb)
            RR = sbuf("RR", [128, NKT * SLAB], F32, pb)
            X2H = sbuf("X2H", [128, NKT * SLAB], BF16, pb)
            X2L = sbuf("X2L", [128, NKT * SLAB], BF16, pb)
            HH = sbuf("HH", [128, NHT * SLAB], BF16, pb)
            HL = sbuf("HL", [128, NHT * SLAB], BF16, pb)
            ONH = HH
            ONL = HL
            WOF = sbuf("WOF", [128, NKT * 128], F32, pb)
            WOH = [sbuf(f"WOH{i}", [128, NKT * 128], BF16, pb) for i in range(2)]
            WOL = [sbuf(f"WOL{i}", [128, NKT * 128], BF16, pb) for i in range(2)]
            BGB = [sbuf(f"BGB{i}", [128, NKT * 128], BF16, pb) for i in range(2)]
            BUB = [sbuf(f"BUB{i}", [128, NKT * 128], BF16, pb) for i in range(2)]
            BDB = [sbuf(f"BDB{i}", [128, (NHT // 2) * 128], BF16, pb)
                   for i in range(2)]
            HSUM = sbuf("HSUM", [1, SLAB], F32, pb)
            CORR = sbuf("CORR", [128, SLAB], F32, pb)
            MEANR = sbuf("MEANR", [1, SLAB], F32, pb)
            VARR = sbuf("VARR", [1, SLAB], F32, pb)
            RSR = sbuf("RSR", [1, SLAB], F32, pb)
            R2TM = OTM
            MCOL = sbuf("MCOL", [128, 2], F32, pb)
            VCOL = sbuf("VCOL", [128, 2], F32, pb)
            SCOL = sbuf("SCOL", [128, 4], F32, pb)
            if os.environ.get("KPRINT_SBUF"):
                print("phaseB peak, sbuf bytes remaining:", nc.sbuf_bytes_remaining)

            def newton_rsqrt(eng, y_ap, a_ap, tmp_ap, deps):
                t1 = em.op(eng, lambda e: e.tensor_mul(tmp_ap, y_ap, y_ap), deps=deps)
                t2 = em.op(eng, lambda e: e.tensor_mul(tmp_ap, tmp_ap, a_ap), deps=[t1])
                t3 = em.op(eng, lambda e: e.tensor_scalar(
                    tmp_ap, tmp_ap, -0.5, 1.5, ALU.mult, ALU.add), deps=[t2])
                return em.op(eng, lambda e: e.tensor_mul(y_ap, y_ap, tmp_ap), deps=[t3])

            nslab_run = {"phaseA": 0, "a2a": 0, "init": 0, "noscan": 0,
                         "slab1": 1}.get(stop, NSLAB)
            pd = {}
            xbt_free = [None, None]
            wo_slot_free = [None, None]
            bg_slot_free = [None, None]
            bd_slot_free = [None, None]
            wof_free = None
            for s in range(nslab_run):
                nd = {}
                th_x = tl_x = None
                kps = NKT // 8
                for sec in range(8):
                    tb = (s * 8 + sec) % 2
                    deps = (join if s == 0 and sec < 2 else [])
                    if xbt_free[tb] is not None:
                        deps = deps + [xbt_free[tb]]
                    t_xb = ld(XBT[tb][:, :].rearrange("p (k t) -> p k t", k=kps),
                              dram["xs_in"].ap()[sec * kps:(sec + 1) * kps, :,
                                                 s * SLAB:(s + 1) * SLAB]
                              .rearrange("k p t -> p k t"),
                              deps=deps)
                    sdeps = [t_xb]
                    if s and sec == 0:
                        sdeps.append(pd["xhl_used"])
                    cs = slice(sec * kps * SLAB, (sec + 1) * kps * SLAB)
                    th_x, tl_x = split_pair(XBT[tb][:, :], XBH[:, cs], XBL[:, cs],
                                            sdeps)
                    xbt_free[tb] = tl_x

                g1_pairs = []
                for wb, xb in ((WG1H, XBH), (WG1H, XBL), (WG1L, XBH)):
                    for kt in range(NKT):
                        g1_pairs.append((wb[:, kt * 128:(kt + 1) * 128],
                                         xb[:, kt * SLAB:(kt + 1) * SLAB]))
                t_g1 = em.mm_group(g1_pairs, P[1][:, 0:SLAB],
                                   deps=[th_x, tl_x] +
                                   ([bank_free[1]] if bank_free[1] else []))
                th_g1, tl_g1 = split_pair(P[1][:, 0:SLAB], G1H[:, :], G1L[:, :],
                                          [t_g1])
                bank_free[1] = tl_g1

                on_last = None
                otm_free = pd.get("otm_used")
                for bk in range(2):
                    deps = ([t_cc] + join) if (s == 0 and bk == 0) else []
                    if otm_free is not None:
                        deps = deps + [otm_free, t_cc]
                    if s and bk == 0 and "out_done" in pd:
                        deps = deps + [pd["out_done"]]
                    def mko(b=bk, p=0, ss=s):
                        return lambda e: e.dma_start(
                            out=OTM[:, p * HPC * HD:(p + 1) * HPC * HD],
                            in_=a2a_out.ap()[p, ss * SLAB + b * 128:
                                             ss * SLAB + (b + 1) * 128, :])
                    t_ol = em.dma_batch("scalar",
                                        [mko(bk, p) for p in range(NCORES)],
                                        deps=deps)
                    og_toks = []
                    for msl in range(4):
                        gate_pairs = [
                            (G1H[:, bk * 128:(bk + 1) * 128],
                             WG2H[:, msl * 512:(msl + 1) * 512]),
                            (G1H[:, bk * 128:(bk + 1) * 128],
                             WG2L[:, msl * 512:(msl + 1) * 512]),
                            (G1L[:, bk * 128:(bk + 1) * 128],
                             WG2H[:, msl * 512:(msl + 1) * 512])]
                        t_gt = em.mm_group(
                            gate_pairs, P[2][:, :],
                            deps=[th_g1, tl_g1] +
                            ([bank_free[2]] if bank_free[2] else []))
                        t_sg_ = em.op("scalar", lambda e: e.activation(
                            SIG[:, :], P[2][:, :], AF.Sigmoid),
                            deps=[t_gt] + (og_toks[-1:] if og_toks else []))
                        bank_free[2] = t_sg_
                        def mkog(m=msl):
                            return lambda e: e.tensor_mul(
                                OTM[:, m * 512:(m + 1) * 512],
                                OTM[:, m * 512:(m + 1) * 512], SIG[:, :])
                        t_og = em.op("vector", mkog(), deps=[t_sg_, t_ol])
                        og_toks.append(t_og)

                    t_sq = em.op("scalar",
                                 lambda e: e.square(SQ[:, :], OTM[:, :]),
                                 deps=og_toks +
                                 ([pd["sq_used"]] if (s and bk == 0) else []) +
                                 ([on_last] if on_last else []))
                    t_ms = em.op("vector", lambda e: e.tensor_reduce(
                        MS[:, :], SQ[:, :].rearrange("p (f e) -> p f e", f=NKT),
                        AX.X, ALU.add), deps=[t_sq])
                    t_a = em.op("vector", lambda e: e.tensor_scalar(
                        MS[:, :], MS[:, :], 1.0 / HD, RMS_EPS, ALU.mult, ALU.add),
                        deps=[t_ms])
                    t_s1 = em.op("scalar", lambda e: e.activation(
                        RSD[:, :], MS[:, :], AF.Sqrt), deps=[t_a])
                    t_rc = em.op("vector", lambda e: e.reciprocal(RSD[:, :], RSD[:, :]),
                                 deps=[t_s1])
                    t_rf = newton_rsqrt("vector", RSD[:, :], MS[:, :], TMP1[:, :],
                                        [t_rc, t_a])
                    sc_toks = []
                    for ft in range(NKT):
                        def mks(f=ft):
                            return lambda e: e.tensor_scalar_mul(
                                OTM[:, f * 128:(f + 1) * 128],
                                OTM[:, f * 128:(f + 1) * 128], RSD[:, f:f + 1])
                        sc_toks.append(em.op("vector", mks(), deps=[t_rf]))
                    t_nw = em.op("vector", lambda e: e.tensor_mul(
                        OTM[:, :], OTM[:, :], sb_nwtm[:, :]),
                        deps=sc_toks + [tok_nwtm])
                    sc_toks = [t_nw] * NKT
                    ev_prev = None
                    t_tp_last = None
                    for q in range(4):
                        bidx = 4 + (q % 2)
                        fill_deps = sc_toks[q * 4:(q + 1) * 4] + \
                            ([bank_free[bidx]] if bank_free[bidx] else [])
                        t_tp = None
                        for i in range(4):
                            ft = q * 4 + i
                            def mkt(f=ft, o=i * 128, pk=P[bidx]):
                                return lambda e: e.transpose(
                                    pk[:, o:o + 128],
                                    OTM[:, f * 128:(f + 1) * 128], sb_ident[:, :])
                            t_tp = em.op("tensor", mkt(),
                                         deps=fill_deps if i == 0 else (), inc=(i == 3))
                        t_tp_last = t_tp
                        for i in range(4):
                            ft = q * 4 + i
                            col = ft * SLAB + bk * 128
                            def mkh(c=col, o=i * 128, pk=P[bidx]):
                                return lambda e: e.copy(
                                    ONH[:, c:c + 128], pk[:, o:o + 128])

                            def mkl(c=col, o=i * 128, pk=P[bidx]):
                                return lambda e: e.tensor_sub(
                                    ONL[:, c:c + 128], pk[:, o:o + 128],
                                    ONH[:, c:c + 128])
                            t_h_ = em.op("scalar", mkh(),
                                         deps=[t_tp] +
                                         ([ev_prev] if (i > 0 and ev_prev) else []) +
                                         ([pd["on_used"]] if (s and bk == 0 and q == 0
                                                              and i == 0) else []))
                            t_l_ = em.op("vector", mkl(), deps=[t_tp, t_h_])
                            ev_prev = t_l_
                        bank_free[bidx] = ev_prev
                    on_last = ev_prev
                    otm_free = t_tp_last
                nd["otm_used"] = otm_free
                nd["sq_used"] = on_last

                r_toks = []
                for mt in range(NKT):
                    sl = mt % 2
                    t_wf = ld(WOF[:, :].rearrange("p (k a) -> p k a", k=NKT),
                              dram["wo_in"].ap()[mt].rearrange("k p a -> p k a"),
                              deps=([wof_free] if wof_free else []) +
                              (join if s == 0 and mt == 0 else []))
                    def mkwh(b=sl):
                        return lambda e: e.copy(WOH[b][:, :], WOF[:, :])

                    def mkwl(b=sl):
                        return lambda e: e.tensor_sub(WOL[b][:, :], WOF[:, :],
                                                      WOH[b][:, :])
                    t_wh = em.op("scalar", mkwh(),
                                 deps=[t_wf] +
                                 ([wo_slot_free[sl]] if wo_slot_free[sl] else
                                  (join if s == 0 else [])))
                    t_wl = em.op("vector", mkwl(), deps=[t_wf, t_wh])
                    wof_free = t_wl
                    wo_pairs = []
                    for wb, ob in ((WOH[sl], ONH), (WOH[sl], ONL), (WOL[sl], ONH)):
                        for kt in range(NKT):
                            wo_pairs.append((wb[:, kt * 128:(kt + 1) * 128],
                                             ob[:, kt * SLAB:(kt + 1) * SLAB]))
                    t_mm = em.mm_group(
                        wo_pairs, PW[sl][:, 0:SLAB],
                        deps=[on_last, t_wl] +
                        ([bank_free[6 + sl]] if bank_free[6 + sl] else []))
                    wo_slot_free[sl] = t_mm
                    def mr1(m=mt, pk=PW[sl]):
                        return lambda e: e.tensor_add(
                            RR[:, m * SLAB:(m + 1) * SLAB], pk[:, 0:SLAB],
                            XBH[:, m * SLAB:(m + 1) * SLAB])

                    def mr2(m=mt):
                        return lambda e: e.tensor_add(
                            RR[:, m * SLAB:(m + 1) * SLAB],
                            RR[:, m * SLAB:(m + 1) * SLAB],
                            XBL[:, m * SLAB:(m + 1) * SLAB])
                    t_r1 = em.op("vector", mr1(),
                                 deps=[t_mm, th_x] +
                                 ([pd["rr_used"]] if (s and mt == 0) else []))
                    t_r2_ = em.op("vector", mr2(), deps=[t_r1, tl_x])
                    bank_free[6 + sl] = t_r2_
                    r_toks.append(t_r2_)
                nd["xb_used"] = r_toks[-1]
                nd["xhl_used"] = r_toks[-1]
                nd["on_used"] = r_toks[-1]
                dbg_tok = None
                if debug_dump:
                    t_dON = em.dma("gpsimd", lambda e, ss=s: e.dma_start(
                        out=dbg_onh.ap()[ss], in_=ONH[:, :]), deps=[on_last])
                    t_dON = em.dma("gpsimd", lambda e, ss=s: e.dma_start(
                        out=dbg_onl.ap()[ss], in_=ONL[:, :]), deps=[on_last])
                    nd["on_used"] = t_dON
                    dbg_tok = em.dma("gpsimd", lambda e, ss=s: e.dma_start(
                        out=dbg_rr.ap()[ss], in_=RR[:, :]), deps=r_toks)

                mean_pairs = [(sb_invd[:, :], RR[:, kt * SLAB:(kt + 1) * SLAB])
                              for kt in range(NKT)]
                t_mean = em.mm_group(mean_pairs, P[0][0:1, 0:SLAB],
                                     deps=r_toks + ([bank_free[0]] if bank_free[0]
                                                    else []))
                t_mrow = em.op("vector", lambda e: e.tensor_copy(
                    MEANR[:, :], P[0][0:1, 0:SLAB]), deps=[t_mean])
                t_ss = None
                t_qq = None
                for half in range(2):
                    hw = NKT * SLAB // 2
                    def mkq(hf=half, w=hw):
                        return lambda e: e.square(SQ[:, 0:w], RR[:, hf * w:(hf + 1) * w])
                    t_qq = em.op("scalar", mkq(),
                                 deps=(r_toks + [on_last]) if half == 0 else
                                 [t_ss, t_qq])
                    sq_pairs = [(sb_invd[:, :], SQ[:, kt * SLAB:(kt + 1) * SLAB])
                                for kt in range(NKT // 2)]
                    t_ss = em.mm_group(
                        sq_pairs, P[1][0:1, 0:SLAB],
                        deps=[t_qq, tl_g1] +
                        ([bank_free[1]] if (half == 0 and bank_free[1]) else []),
                        start=(half == 0), stop=(half == 1))
                t_v1 = em.op("vector", lambda e: e.tensor_mul(
                    RSR[:, :], MEANR[:, :], MEANR[:, :]), deps=[t_mrow])
                t_v2 = em.op("vector", lambda e: e.tensor_sub(
                    VARR[:, :], P[1][0:1, 0:SLAB], RSR[:, :]), deps=[t_ss, t_v1])
                t_s2 = em.op("scalar", lambda e: e.activation(
                    RSR[:, :], VARR[:, :], AF.Sqrt, bias=LN_EPS), deps=[t_v2, t_v1])
                t_rc2 = em.op("vector", lambda e: e.reciprocal(RSR[:, :], RSR[:, :]),
                              deps=[t_s2])
                t_ve = em.op("vector", lambda e: e.tensor_scalar_add(
                    VARR[:, :], VARR[:, :], LN_EPS), deps=[t_v2])
                t_rf2 = newton_rsqrt("vector", RSR[:, :], VARR[:, :], HSUM[:, :],
                                     [t_rc2, t_ve])
                t_bm = em.mm_group([(sb_ones_row[:, :], MEANR[:, :])], P[0][:, 0:SLAB],
                                   deps=[t_mrow, t_v1, t_mean])
                t_br = em.mm_group([(sb_ones_row[:, :], RSR[:, :])], P[1][:, 0:SLAB],
                                   deps=[t_rf2, t_v2])
                x2_toks = []
                for kt in range(NKT):
                    def ms1(k=kt):
                        return lambda e: e.tensor_sub(
                            RR[:, k * SLAB:(k + 1) * SLAB],
                            RR[:, k * SLAB:(k + 1) * SLAB], P[0][:, 0:SLAB])

                    def ms2(k=kt):
                        return lambda e: e.tensor_mul(
                            RR[:, k * SLAB:(k + 1) * SLAB],
                            RR[:, k * SLAB:(k + 1) * SLAB], P[1][:, 0:SLAB])
                    t_c1 = em.op("vector", ms1(),
                                 deps=[t_bm] + ([dbg_tok] if dbg_tok else []))
                    t_c2 = em.op("vector", ms2(), deps=[t_br, t_c1])
                    th2, tl2 = split_pair(RR[:, kt * SLAB:(kt + 1) * SLAB],
                                          X2H[:, kt * SLAB:(kt + 1) * SLAB],
                                          X2L[:, kt * SLAB:(kt + 1) * SLAB],
                                          [t_c2] + ([pd["x2_used"]]
                                                    if (s and kt == 0) else []))
                    x2_toks += [th2, tl2]
                x2_last = x2_toks[-1]
                bank_free[0] = x2_last
                bank_free[1] = x2_last
                if debug_dump:
                    t_dx2 = em.dma("gpsimd", lambda e, ss=s: e.dma_start(
                        out=dbg_x2h.ap()[ss], in_=X2H[:, :]), deps=[x2_last])
                    nd["x2_used"] = t_dx2

                h_toks = []
                ev_prev = None
                for ht in range(NHT):
                    sl = ht % 2
                    def mbg(h=ht, b=sl):
                        return lambda e: e.dma_start(
                            out=BGB[b][:, :].rearrange("p (k a) -> p k a", k=NKT),
                            in_=dram["bg_in"].ap()[h].rearrange("k p a -> p k a"))

                    def mbu(h=ht, b=sl):
                        return lambda e: e.dma_start(
                            out=BUB[b][:, :].rearrange("p (k a) -> p k a", k=NKT),
                            in_=dram["bu_in"].ap()[h].rearrange("k p a -> p k a"))
                    t_bg = em.dma_batch("gpsimd", [mbg(), mbu()],
                                        deps=[bg_slot_free[sl]] if bg_slot_free[sl]
                                        else (join if s == 0 else []))
                    t_bu = t_bg
                    g_pairs = []
                    u_pairs = []
                    for xb in (X2H, X2L):
                        for kt in range(NKT):
                            g_pairs.append((BGB[sl][:, kt * 128:(kt + 1) * 128],
                                            xb[:, kt * SLAB:(kt + 1) * SLAB]))
                            u_pairs.append((BUB[sl][:, kt * 128:(kt + 1) * 128],
                                            xb[:, kt * SLAB:(kt + 1) * SLAB]))
                    t_g = em.mm_group(g_pairs, P[5][:, 0:SLAB],
                                      deps=x2_toks + [t_bg] +
                                      ([bank_free[5]] if bank_free[5] else []))
                    t_u = em.mm_group(u_pairs, P[4][:, 0:SLAB],
                                      deps=[t_bu, x2_last] +
                                      ([bank_free[4]] if bank_free[4] else []))
                    bg_slot_free[sl] = t_u
                    t_e0 = em.op("vector", lambda e: e.tensor_copy(
                        GBUF, P[5][:, 0:SLAB]),
                        deps=[t_g] + ([ev_prev] if ev_prev else []) +
                        ([x2_last] if ht == 0 else []))
                    bank_free[5] = t_e0
                    def me1(h=ht):
                        return lambda e: e.activation(
                            SIG[:, 0:SLAB], GBUF, AF.Sigmoid,
                            scale=sb_sg[:, h:h + 1])
                    t_e1 = em.op("scalar", me1(),
                                 deps=[t_e0] + ([ev_prev] if ev_prev else []))
                    t_e2 = em.op("vector", lambda e: e.tensor_mul(
                        SQ[:, 0:SLAB], GBUF, SIG[:, 0:SLAB]),
                        deps=[t_e1])
                    t_e3 = em.op("vector", lambda e: e.tensor_mul(
                        SQ[:, 0:SLAB], SQ[:, 0:SLAB], P[4][:, 0:SLAB]),
                        deps=[t_e2, t_u])
                    bank_free[4] = t_e3
                    def mh(h=ht):
                        return lambda e: e.activation(
                            HH[:, h * SLAB:(h + 1) * SLAB], SQ[:, 0:SLAB], AF.Copy,
                            scale=sb_sgu[:, h:h + 1])

                    def ml(h=ht):
                        return lambda e: e.scalar_tensor_tensor(
                            HL[:, h * SLAB:(h + 1) * SLAB], SQ[:, 0:SLAB],
                            sb_sgu[:, h:h + 1], HH[:, h * SLAB:(h + 1) * SLAB],
                            ALU.mult, ALU.subtract)
                    t_h_ = em.op("scalar", mh(),
                                 deps=[t_e3, tok_sgu] +
                                 ([pd["hh_used"]] if (s and ht == 0) else []))
                    t_l_ = em.op("vector", ml(), deps=[t_h_, t_e3])
                    ev_prev = t_l_
                    h_toks += [t_h_, t_l_]
                h_last = h_toks[-1]
                if debug_dump:
                    t_dhh = em.dma("gpsimd", lambda e, ss=s: e.dma_start(
                        out=dbg_hh.ap()[ss], in_=HH[:, :]), deps=h_toks)
                    nd["hh_used"] = t_dhh

                hs_pairs = []
                for hb in (HH, HL):
                    for ht in range(NHT):
                        hs_pairs.append((sb_onesb[:, :],
                                         hb[:, ht * SLAB:(ht + 1) * SLAB]))
                t_hs = em.mm_group(hs_pairs, P[0][0:1, 0:SLAB],
                                   deps=h_toks + [tok_onesb] +
                                   ([bank_free[0]] if bank_free[0] else []))
                t_hsum = em.op("vector", lambda e: e.tensor_copy(
                    HSUM[:, :], P[0][0:1, 0:SLAB]), deps=[t_hs, t_rf2])
                t_cmm = em.mm_group([(sb_c128_row[:, :], HSUM[:, :])], P[0][:, 0:SLAB],
                                    deps=[t_hsum])
                t_corr = em.op("vector", lambda e: e.tensor_copy(
                    CORR[:, :], P[0][:, 0:SLAB]), deps=[t_cmm])
                bank_free[0] = t_corr

                r2_toks = []
                NHH = NHT // 2
                for mt in range(NKT):
                    sl = mt % 2
                    t_half = None
                    for half in range(2):
                        hb_i = (mt * 2 + half) % 2
                        def mbd(m=mt, hf=half, b=hb_i):
                            return lambda e: e.dma_start(
                                out=BDB[b][:, :].rearrange("p (k a) -> p k a",
                                                           k=NHH),
                                in_=dram["bd_in"].ap()[m, hf * NHH:(hf + 1) * NHH]
                                .rearrange("k p a -> p k a"))
                        t_bd = em.dma("gpsimd", mbd(),
                                      deps=[bd_slot_free[hb_i]] if bd_slot_free[hb_i]
                                      else (join if s == 0 else []))
                        d_pairs = []
                        for hb in (HH, HL):
                            for hi in range(NHH):
                                ht = half * NHH + hi
                                d_pairs.append(
                                    (BDB[hb_i][:, hi * 128:(hi + 1) * 128],
                                     hb[:, ht * SLAB:(ht + 1) * SLAB]))
                        gdeps = [h_last, t_bd]
                        if half == 0 and bank_free[6 + sl]:
                            gdeps.append(bank_free[6 + sl])
                        if half == 1:
                            gdeps = [t_bd]
                        t_half = em.mm_group(d_pairs, PW[sl][:, 0:SLAB],
                                             deps=gdeps, start=(half == 0),
                                             stop=(half == 1), inc=True)
                        bd_slot_free[hb_i] = t_half
                    t_d = t_half
                    def ma1(pk=PW[sl]):
                        return lambda e: e.tensor_add(
                            SIG[:, 0:SLAB], pk[:, 0:SLAB], CORR[:, :])

                    def ma2(m=mt):
                        return lambda e: e.scalar_tensor_tensor(
                            RR[:, m * SLAB:(m + 1) * SLAB], SIG[:, 0:SLAB],
                            sb_sd[:, m:m + 1], X2H[:, m * SLAB:(m + 1) * SLAB],
                            ALU.mult, ALU.add)

                    def ma3(m=mt):
                        return lambda e: e.tensor_add(
                            RR[:, m * SLAB:(m + 1) * SLAB],
                            RR[:, m * SLAB:(m + 1) * SLAB],
                            X2L[:, m * SLAB:(m + 1) * SLAB])
                    t_a1 = em.op("vector", ma1(),
                                 deps=[t_d, t_corr] + ([h_last] if mt == 0 else []) +
                                 (r2_toks[-1:] if r2_toks else []))
                    t_a2 = em.op("vector", ma2(), deps=[t_a1, x2_last])
                    t_a3 = em.op("vector", ma3(), deps=[t_a2])
                    bank_free[6 + sl] = t_a3
                    r2_toks.append(t_a3)
                nd["hh_used"] = r2_toks[-1]
                nd["x2_used"] = r2_toks[-1]
                nd["rr_used"] = r2_toks[-1]
                if debug_dump:
                    t_dr2 = em.dma("gpsimd", lambda e, ss=s: e.dma_start(
                        out=dbg_r2.ap()[ss], in_=RR[:, :]), deps=r2_toks)
                    nd["rr_used"] = t_dr2

                out_prev = None
                for bk in range(2):
                    ev_toks = []
                    for q in range(4):
                        bidx = 2 + (q % 2)
                        fill_deps = [r2_toks[q * 4 + i] for i in range(4)] + \
                            ([bank_free[bidx]] if bank_free[bidx] else [])
                        t_tp = None
                        for i in range(4):
                            ft = q * 4 + i
                            def mkt2(b=bk, f=ft, o=i * 128, pk=P[bidx]):
                                return lambda e: e.transpose(
                                    pk[:, o:o + 128],
                                    RR[:, f * SLAB + b * 128: f * SLAB + (b + 1) * 128],
                                    sb_ident[:, :])
                            t_tp = em.op("tensor", mkt2(),
                                         deps=fill_deps if i == 0 else (),
                                         inc=(i == 3))
                        def mev(qq=q, pk=P[bidx]):
                            return lambda e: e.tensor_copy(
                                R2TM[:, qq * 512:(qq + 1) * 512], pk[:, :])
                        t_ev = em.op("vector", mev(),
                                     deps=[t_tp] + ([out_prev] if (out_prev and q == 0)
                                                    else []))
                        bank_free[bidx] = t_ev
                        ev_toks.append(t_ev)
                    t_sum = em.op("vector", lambda e: e.tensor_reduce(
                        MCOL[:, 0:1], R2TM[:, :], AX.X, ALU.add), deps=ev_toks)
                    t_sq_ = em.op("scalar", lambda e: e.square(SQ[:, :], R2TM[:, :]),
                                  deps=ev_toks + h_toks[-1:])
                    t_ssq = em.op("vector", lambda e: e.tensor_reduce(
                        MCOL[:, 1:2], SQ[:, :], AX.X, ALU.add), deps=[t_sq_])
                    t_m = em.op("vector", lambda e: e.tensor_scalar_mul(
                        VCOL[:, 0:1], MCOL[:, 0:1], 1.0 / D), deps=[t_sum])
                    t_m2 = em.op("vector", lambda e: e.tensor_mul(
                        VCOL[:, 1:2], VCOL[:, 0:1], VCOL[:, 0:1]), deps=[t_m])
                    t_v = em.op("vector", lambda e: e.scalar_tensor_tensor(
                        SCOL[:, 0:1], MCOL[:, 1:2], 1.0 / D, VCOL[:, 1:2],
                        ALU.mult, ALU.subtract), deps=[t_ssq, t_m2])
                    t_sr = em.op("scalar", lambda e: e.activation(
                        SCOL[:, 1:2], SCOL[:, 0:1], AF.Sqrt, bias=LN_EPS),
                        deps=[t_v])
                    t_rc_ = em.op("vector", lambda e: e.reciprocal(
                        SCOL[:, 1:2], SCOL[:, 1:2]), deps=[t_sr])
                    t_ve_ = em.op("vector", lambda e: e.tensor_scalar_add(
                        SCOL[:, 0:1], SCOL[:, 0:1], LN_EPS), deps=[t_v])
                    t_rf_ = newton_rsqrt("vector", SCOL[:, 1:2], SCOL[:, 0:1],
                                         SCOL[:, 2:3], [t_rc_, t_ve_])
                    t_out = em.op("vector", lambda e: e.tensor_scalar(
                        R2TM[:, :], R2TM[:, :], VCOL[:, 0:1], SCOL[:, 1:2],
                        ALU.subtract, ALU.mult), deps=[t_rf_] + ev_toks)
                    def mst(b=bk, ss=s):
                        return lambda e: e.dma_start(
                            out=out_t.ap()[ss * SLAB + b * 128:
                                           ss * SLAB + (b + 1) * 128, :],
                            in_=R2TM[:, :])
                    t_st = em.dma("gpsimd", mst(), deps=[t_out])
                    out_prev = t_st
                nd["out_done"] = out_prev
                pd = nd

            em.wait("gpsimd", [("QG", em.n["QG"])])

        def replay(eng_name):
            def go(eng):
                for fn, waits, inckey in em.plan[eng_name]:
                    for (k, v) in waits:
                        eng.wait_ge(sems[k], v)
                    if fn is None:
                        continue
                    ins = fn(eng)
                    if inckey is not None:
                        ins.then_inc(sems[inckey], 16 if inckey.startswith("Q") else 1)
            return go

        blk.sync(replay("sync"))
        blk.tensor(replay("tensor"))
        blk.vector(replay("vector"))
        blk.scalar(replay("scalar"))
        blk.gpsimd(replay("gpsimd"))

    return nc



def _prep_inputs(inputs):
    x = np.asarray(inputs["L0_attn_x"], dtype=np.float32)[0]
    xT = np.ascontiguousarray(x.T)
    xg = np.ascontiguousarray(
        xT.reshape(NKT, 128, NGRP, GW).transpose(2, 0, 1, 3))
    wq = np.asarray(inputs["L0_attn_Wq"], dtype=np.float32)
    wk = np.asarray(inputs["L0_attn_Wk"], dtype=np.float32)
    wv = np.asarray(inputs["L0_attn_Wv"], dtype=np.float32)
    wo = np.asarray(inputs["L0_attn_Wo"], dtype=np.float32)
    wg1 = np.asarray(inputs["L0_attn_Wg1"], dtype=np.float32)
    wg2 = np.asarray(inputs["L0_attn_Wg2"], dtype=np.float32)
    nw = np.asarray(inputs["L0_attn_norm_weight"], dtype=np.float32)
    bg = np.asarray(inputs["L0_ffn_wq_gate"]).astype(np.uint8)
    bu = np.asarray(inputs["L0_ffn_wq_up"]).astype(np.uint8)
    bd = np.asarray(inputs["L0_ffn_wq_down"]).astype(np.uint8)
    sg = np.asarray(inputs["L0_ffn_s_gate"], dtype=np.float32)
    su = np.asarray(inputs["L0_ffn_s_up"], dtype=np.float32)
    sd = np.asarray(inputs["L0_ffn_s_down"], dtype=np.float32)

    wo_t = np.ascontiguousarray(
        wo.reshape(NKT, 128, NKT, 128).transpose(2, 0, 1, 3))
    wg1_t = np.ascontiguousarray(wg1.reshape(NKT, 128, 128))
    bg_t = np.ascontiguousarray(
        bg.reshape(NHT, 128, NKT, 128).transpose(0, 2, 3, 1))
    bu_t = np.ascontiguousarray(
        bu.reshape(NHT, 128, NKT, 128).transpose(0, 2, 3, 1))
    bd_t = np.ascontiguousarray(
        bd.reshape(NKT, 128, NHT, 128).transpose(0, 2, 3, 1))
    sg_c = np.ascontiguousarray(sg.reshape(NHT, 128).T)
    su_c = np.ascontiguousarray(su.reshape(NHT, 128).T)
    sd_c = np.ascontiguousarray(sd.reshape(NKT, 128).T)
    nw_c = np.ascontiguousarray(nw.reshape(NKT, 128).T)

    j = np.arange(64, dtype=np.float64)
    theta = BASE ** (-2.0 * j / HD)
    th128 = np.concatenate([theta, theta])
    pos = np.arange(SEQ, dtype=np.float64)
    ang = th128[:, None] * pos[None, :]
    cos_t = np.cos(ang).astype(np.float32)
    sgn128 = np.where(np.arange(128) < 64, -1.0, 1.0)
    sinsg_t = (np.sin(ang) * sgn128[:, None]).astype(np.float32)
    cos_g = np.ascontiguousarray(cos_t.reshape(128, NGRP, GW).transpose(1, 0, 2))
    sinsg_g = np.ascontiguousarray(sinsg_t.reshape(128, NGRP, GW).transpose(1, 0, 2))
    ident = np.eye(128, dtype=np.float32)
    perm = np.zeros((128, 128), dtype=np.float32)
    perm[np.arange(128), (np.arange(128) + 64) % 128] = 1.0
    slopes = 2.0 ** (-8.0 * (np.arange(HEADS, dtype=np.float64) + 1.0) / HEADS)
    log_lam = -slopes
    i128 = np.arange(128, dtype=np.float64)
    tmins = i128[:, None] - i128[None, :]

    base = dict(
        xg_in=xg,
        wo_in=wo_t, wg1_in=wg1_t, wg2_in=wg2,
        bg_in=bg_t, bu_in=bu_t, bd_in=bd_t,
        sg_in=sg_c, su_in=su_c, sd_in=sd_c, nw_in=nw_c,
        nwtm_in=np.ascontiguousarray(np.broadcast_to(nw[None, :], (128, D))),
        cos_in=cos_g, sinsg_in=sinsg_g,
        ones_row_in=np.ones((1, 128), np.float32),
        c128_row_in=np.full((1, 128), -128.0, np.float32),
        invd_col_in=np.full((128, 1), 1.0 / D, np.float32),
        ones_colb_in=np.ones((128, 1), np.float32),
        ident_in=ident, perm_in=perm,
        zero_col_in=np.zeros((128, 1), np.float32),
        one_col_in=np.ones((128, 1), np.float32),
    )

    in_maps = []
    for c in range(NCORES):
        m = dict(base)
        m["xs_in"] = np.ascontiguousarray(
            xT[:, c * TPC:(c + 1) * TPC].reshape(NKT, 128, TPC))
        for nm, w in (("wq_in", wq), ("wk_in", wk), ("wv_in", wv)):
            sl = w[:, c * HPC * HD:(c + 1) * HPC * HD]
            m[nm] = np.ascontiguousarray(
                sl.reshape(NKT, 128, HPC, HD).transpose(2, 0, 1, 3))
        ll = log_lam[c * HPC:(c + 1) * HPC]
        dmat = np.where(tmins[None] >= 0, np.exp(ll[:, None, None] * tmins[None]), 0.0)
        m["dt_in"] = np.ascontiguousarray(
            dmat.transpose(0, 2, 1)).astype(np.float32)
        dq = np.exp(ll[:, None] * (i128[None, :] + 1.0))
        m["dq_in"] = np.ascontiguousarray(
            np.broadcast_to(np.tile(dq, (1, GRP))[:, None, :],
                            (HPC, 128, GW))).astype(np.float32)
        dk = np.exp(ll[:, None] * (CHUNK - 1.0 - i128[None, :]))
        m["dk_in"] = np.ascontiguousarray(dk.T).astype(np.float32)
        lamC = np.exp(ll * CHUNK)
        m["lam_in"] = np.ascontiguousarray(
            np.broadcast_to(lamC[None, :], (128, HPC))).astype(np.float32)
        in_maps.append(m)
    return in_maps


_NC_CACHE = {}


def kernel(**inputs):
    in_maps = _prep_inputs(inputs)
    if "nc" not in _NC_CACHE:
        _NC_CACHE["nc"] = build_program()
    nc = _NC_CACHE["nc"]
    res = run_bass_kernel_spmd(nc, in_maps, core_ids=list(range(NCORES)))
    out = np.concatenate([np.asarray(res.results[c]["out"]).reshape(TPC, D)
                          for c in range(NCORES)], axis=0)
    return np.ascontiguousarray(out.reshape(1, SEQ, D).astype(np.float32))


# version 10 (final)
# speedup vs baseline: 1.0319x; 1.0203x over previous
import os
os.environ.setdefault("TRN_ALLOW_ICI_ALL_TO_ALL", "1")

from contextlib import ExitStack

import numpy as np

import concourse.bass as bass
import concourse.mybir as mybir
from concourse.bass_utils import run_bass_kernel_spmd

F32 = mybir.dt.float32
BF16 = mybir.dt.bfloat16
U8 = mybir.dt.uint8
AF = mybir.ActivationFunctionType
ALU = mybir.AluOpType
AX = mybir.AxisListType

NCORES = 8
SEQ = 8192
D = 2048
HEADS = 16
HD = 128
CHUNK = 128
NCHUNK = SEQ // CHUNK
GRP = 4
NGRP = NCHUNK // GRP
HPC = HEADS // NCORES
TPC = SEQ // NCORES
SLAB = 256
NSLAB = TPC // SLAB
HID = 5632
NHT = HID // 128
NKT = D // 128
GW = GRP * 128
PI = float(np.pi)
LN_EPS = 1e-5
RMS_EPS = 1e-6
BASE = 10000.0

ENG2KEY = {"tensor": "T", "vector": "V", "scalar": "S", "gpsimd": "G"}
DMA2KEY = {"sync": "QS", "scalar": "QC", "gpsimd": "QG"}


class Em:

    def __init__(self, max_ops=None):
        self.plan = {e: [] for e in ("tensor", "vector", "scalar", "gpsimd", "sync")}
        self.n = {k: 0 for k in ("T", "V", "S", "G", "QS", "QC", "QG", "CC")}
        self.wm = {e: {} for e in self.plan}
        self.max_ops = max_ops
        self.emitted = 0

    def _emit(self, eng, fn, deps, inc_key, inc_amt):
        if self.max_ops is not None and fn is not None:
            if self.emitted >= self.max_ops:
                return None
            self.emitted += 1
        waits = []
        for d in deps:
            if d is None:
                continue
            k, v = d
            if v <= 0 or self.wm[eng].get(k, 0) >= v:
                continue
            self.wm[eng][k] = v
            waits.append((k, v))
        tok = None
        if inc_key is not None:
            self.n[inc_key] += inc_amt
            tok = (inc_key, self.n[inc_key])
        self.plan[eng].append((fn, waits, inc_key))
        return tok

    def op(self, eng, fn, deps=(), inc=True):
        return self._emit(eng, fn, deps, ENG2KEY[eng] if inc else None, 1)

    def dma(self, eng, fn, deps=()):
        return self.dma_batch(eng, [fn], deps)

    def dma_batch(self, eng, fns, deps=()):
        key = DMA2KEY[eng]
        pre = self.n[key]
        deps = list(deps)
        if pre:
            deps.append((key, pre))
        tok = None
        for i, fn in enumerate(fns):
            tok = self._emit(eng, fn, deps if i == 0 else (), key, 16)
        return tok

    def cc(self, fn, deps=()):
        return self._emit("gpsimd", fn, deps, "CC", 1)

    def wait(self, eng, deps):
        self._emit(eng, None, deps, None, 0)

    def join_tokens(self):
        return [(k, v) for k, v in self.n.items() if v > 0]

    def mm_group(self, pairs, out_ap, deps, start=True, stop=True, inc=True):
        n = len(pairs)
        tok = None
        for i, (l, r) in enumerate(pairs):
            def mk(l=l, r=r, o=out_ap, st=(start and i == 0),
                   sp=(stop and i == n - 1)):
                return lambda e: e.matmul(o, l, r, start=st, stop=sp)
            tok = self.op("tensor", mk(), deps=deps if i == 0 else (),
                          inc=(inc and i == n - 1))
        return tok


def build_program(debug_dump=False, stop="full", max_ops=None):
    nc = bass.Bass("TRN2", target_bir_lowering=False, debug=False)
    em = Em(max_ops)

    dram = {}

    def din(name, shape, dtype):
        dram[name] = nc.dram_tensor(name, list(shape), dtype, kind="ExternalInput")
        return dram[name]

    din("xg_in", (NGRP, NKT, 128, GW), F32)
    din("xs_in", (NKT, 128, TPC), F32)
    din("wq_in", (HPC, NKT, 128, 128), F32)
    din("wk_in", (HPC, NKT, 128, 128), F32)
    din("wv_in", (HPC, NKT, 128, 128), F32)
    din("wo_in", (NKT, NKT, 128, 128), F32)
    din("wg1_in", (NKT, 128, 128), F32)
    din("wg2_in", (128, D), F32)
    din("bg_in", (NHT, NKT, 128, 128), U8)
    din("bu_in", (NHT, NKT, 128, 128), U8)
    din("bd_in", (NKT, NHT, 128, 128), U8)
    din("sg_in", (128, NHT), F32)
    din("su_in", (128, NHT), F32)
    din("sd_in", (128, NKT), F32)
    din("nw_in", (128, NKT), F32)
    din("nwtm_in", (128, D), F32)
    din("cos_in", (NGRP, 128, GW), F32)
    din("sinsg_in", (NGRP, 128, GW), F32)
    din("ones_row_in", (1, 128), F32)
    din("c128_row_in", (1, 128), F32)
    din("invd_col_in", (128, 1), F32)
    din("ones_colb_in", (128, 1), F32)
    din("ident_in", (128, 128), F32)
    din("perm_in", (128, 128), F32)
    din("zero_col_in", (128, 1), F32)
    din("one_col_in", (128, 1), F32)
    din("dt_in", (HPC, 128, 128), F32)
    din("dq_in", (HPC, 128, GW), F32)
    din("dk_in", (128, HPC), F32)
    din("lam_in", (128, HPC), F32)

    out_t = nc.dram_tensor("out", [TPC, D], F32, kind="ExternalOutput")
    a2a_in = nc.dram_tensor("a2a_in", [NCORES, TPC, HPC * HD], F32)
    a2a_out = nc.dram_tensor("a2a_out", [NCORES, TPC, HPC * HD], F32)
    if debug_dump:
        dbg_ain = nc.dram_tensor("dbg_ain", [NCORES, TPC, HPC * HD], F32,
                                 kind="ExternalOutput")
        dbg_aout = nc.dram_tensor("dbg_aout", [NCORES, TPC, HPC * HD], F32,
                                  kind="ExternalOutput")
        dbg_onh = nc.dram_tensor("dbg_onh", [NSLAB, 128, NKT * SLAB], BF16,
                                 kind="ExternalOutput")
        dbg_onl = nc.dram_tensor("dbg_onl", [NSLAB, 128, NKT * SLAB], BF16,
                                 kind="ExternalOutput")
        dbg_rr = nc.dram_tensor("dbg_rr", [NSLAB, 128, NKT * SLAB], F32,
                                kind="ExternalOutput")
        dbg_x2h = nc.dram_tensor("dbg_x2h", [NSLAB, 128, NKT * SLAB], BF16,
                                 kind="ExternalOutput")
        dbg_hh = nc.dram_tensor("dbg_hh", [NSLAB, 128, NHT * SLAB], BF16,
                                kind="ExternalOutput")
        dbg_r2 = nc.dram_tensor("dbg_r2", [NSLAB, 128, NKT * SLAB], F32,
                                kind="ExternalOutput")

    sem_names = ["T", "V", "S", "G", "QS", "QC", "QG", "CC"]

    with ExitStack() as top:
        sems = {k: top.enter_context(nc.semaphore(k)) for k in sem_names}
        P = [top.enter_context(nc.psum_tensor(f"pb{i}", [128, 512], F32))
             for i in range(7)]
        bank_free = [None] * 8
        blk = top.enter_context(nc.Block())

        def sbuf(name, shape, dtype, stack):
            return stack.enter_context(nc.sbuf_tensor(name, list(shape), dtype))

        sb_ones_row = sbuf("c_onesr", [1, 128], F32, top)
        sb_c128_row = sbuf("c_c128r", [1, 128], F32, top)
        sb_invd = sbuf("c_invd", [128, 1], F32, top)
        sb_onesb_f = sbuf("c_onesbf", [128, 1], F32, top)
        sb_onesb = sbuf("c_onesb", [128, 1], BF16, top)
        sb_ident = sbuf("c_ident", [128, 128], F32, top)
        sb_perm = sbuf("c_perm", [128, 128], F32, top)
        sb_zero = sbuf("c_zero", [128, 1], F32, top)
        sb_one = sbuf("c_one", [128, 1], F32, top)
        sb_dk = sbuf("c_dk", [128, HPC], F32, top)
        sb_lam = sbuf("c_lam", [128, HPC], F32, top)
        sb_sg = sbuf("c_sg", [128, NHT], F32, top)
        sb_su = sbuf("c_su", [128, NHT], F32, top)
        sb_sgu = sbuf("c_sgu", [128, NHT], F32, top)
        sb_sd = sbuf("c_sd", [128, NKT], F32, top)
        sb_nw = sbuf("c_nw", [128, NKT], F32, top)
        sb_nwtm = sbuf("c_nwtm", [128, D], F32, top)
        WG1H = sbuf("wg1h", [128, NKT * 128], BF16, top)
        WG1L = sbuf("wg1l", [128, NKT * 128], BF16, top)
        WG2H = sbuf("wg2h", [128, D], BF16, top)
        WG2L = sbuf("wg2l", [128, D], BF16, top)

        sb_epsc = sbuf("c_epsc", [128, 1], F32, top)
        nc.const_aps.register(F32, 0.0, sb_zero[:, :])
        nc.const_aps.register(F32, LN_EPS, sb_epsc[:, :])
        em.op("vector", lambda e: e.memset(sb_epsc[:, :], LN_EPS), deps=())

        def ld(dst_ap, src_ap, eng="sync", deps=()):
            def f(e, d=dst_ap, s=src_ap):
                return e.dma_start(out=d, in_=s)
            return em.dma(eng, f, deps)

        tok_c = []
        tok_c.append(ld(sb_ones_row[:, :], dram["ones_row_in"].ap()))
        tok_c.append(ld(sb_c128_row[:, :], dram["c128_row_in"].ap()))
        tok_c.append(ld(sb_invd[:, :], dram["invd_col_in"].ap()))
        tok_c.append(ld(sb_onesb_f[:, :], dram["ones_colb_in"].ap()))
        tok_c.append(ld(sb_ident[:, :], dram["ident_in"].ap()))
        tok_c.append(ld(sb_perm[:, :], dram["perm_in"].ap()))
        tok_c.append(ld(sb_zero[:, :], dram["zero_col_in"].ap()))
        tok_c.append(ld(sb_one[:, :], dram["one_col_in"].ap()))
        tok_c.append(ld(sb_dk[:, :], dram["dk_in"].ap()))
        tok_c.append(ld(sb_lam[:, :], dram["lam_in"].ap()))
        tok_c.append(ld(sb_sg[:, :], dram["sg_in"].ap()))
        tok_c.append(ld(sb_su[:, :], dram["su_in"].ap()))
        tok_c.append(ld(sb_sd[:, :], dram["sd_in"].ap()))
        tok_c.append(ld(sb_nw[:, :], dram["nw_in"].ap()))
        tok_c.append(ld(sb_nwtm[:, :], dram["nwtm_in"].ap()))
        tok_sgu = em.op("vector",
                        lambda e: e.tensor_mul(sb_sgu[:, :], sb_sg[:, :], sb_su[:, :]),
                        deps=tok_c)
        tok_onesb = em.op("scalar", lambda e: e.copy(sb_onesb[:, :], sb_onesb_f[:, :]),
                          deps=tok_c)

        def split_pair(f32_ap, hi_ap, lo_ap, dep_toks, scale_ap=None):
            if scale_ap is None:
                def fh(e, o=hi_ap, i=f32_ap):
                    return e.copy(o, i)

                def fl(e, o=lo_ap, i=f32_ap, h=hi_ap):
                    return e.tensor_sub(o, i, h)
            else:
                def fh(e, o=hi_ap, i=f32_ap, sc=scale_ap):
                    return e.activation(o, i, AF.Copy, scale=sc)

                def fl(e, o=lo_ap, i=f32_ap, sc=scale_ap, h=hi_ap):
                    return e.scalar_tensor_tensor(o, i, sc, h, ALU.mult, ALU.subtract)
            th = em.op("scalar", fh, deps=dep_toks)
            tl = em.op("vector", fl, deps=list(dep_toks) + [th])
            return th, tl


        NSEC = 4
        SECW = (NKT // NSEC) * GW
        with ExitStack() as pa:
            PB7 = pa.enter_context(nc.psum_tensor("pb7a", [128, 1024], BF16))
            sb_identb = sbuf("c_identb", [128, 128], BF16, pa)
            sb_dt = sbuf("c_dt", [128, HPC * 128], F32, pa)
            sb_dq = sbuf("c_dq", [128, HPC * GW], F32, pa)
            WQH = sbuf("wqh", [128, HPC * NKT * 128], BF16, pa)
            WQL = sbuf("wql", [128, HPC * NKT * 128], BF16, pa)
            WKH = sbuf("wkh", [128, HPC * NKT * 128], BF16, pa)
            WKL = sbuf("wkl", [128, HPC * NKT * 128], BF16, pa)
            WVH = sbuf("wvh", [128, HPC * NKT * 128], BF16, pa)
            WVL = sbuf("wvl", [128, HPC * NKT * 128], BF16, pa)
            XGT = [sbuf(f"XGT{i}", [128, SECW], F32, pa) for i in range(2)]
            XH = [sbuf(f"XH{i}", [128, NKT * GW], BF16, pa) for i in range(2)]
            XL = [sbuf(f"XL{i}", [128, NKT * GW], BF16, pa) for i in range(2)]

            tok_c.append(ld(sb_dt[:, :].rearrange("p (h a) -> p h a", h=HPC),
                            dram["dt_in"].ap().rearrange("h p a -> p h a")))
            tok_c.append(ld(sb_dq[:, :].rearrange("p (h a) -> p h a", h=HPC),
                            dram["dq_in"].ap().rearrange("h p a -> p h a")))
            tok_identb = em.op("vector",
                               lambda e: e.tensor_copy(sb_identb[:, :],
                                                       sb_ident[:, :]),
                               deps=tok_c)
            with nc.sbuf_tensor("wtmp", [128, HPC * NKT * 128], F32) as wtmp:
                prev_reads = list(tok_c)
                for nm, bh, bl in (("wq", WQH, WQL), ("wk", WKH, WKL),
                                   ("wv", WVH, WVL)):
                    t_ld = ld(wtmp[:, :].rearrange("p (h k a) -> p h k a",
                                                   h=HPC, k=NKT),
                              dram[nm + "_in"].ap().rearrange("h k p a -> p h k a"),
                              deps=prev_reads)
                    th, tl = split_pair(wtmp[:, :], bh[:, :], bl[:, :], [t_ld])
                    prev_reads = [th, tl]
                t_ld = ld(wtmp[:, 0:NKT * 128].rearrange("p (k a) -> p k a", k=NKT),
                          dram["wg1_in"].ap().rearrange("k p a -> p k a"),
                          deps=prev_reads)
                th, tl = split_pair(wtmp[:, 0:NKT * 128], WG1H[:, :], WG1L[:, :],
                                    [t_ld])
                t_ld = ld(wtmp[:, 0:D], dram["wg2_in"].ap(), deps=[th, tl])
                th, tl = split_pair(wtmp[:, 0:D], WG2H[:, :], WG2L[:, :], [t_ld])
                tok_wsplit = [th, tl]
            COS = [sbuf(f"COS{i}", [128, GW], F32, pa) for i in range(2)]
            SINSG = [sbuf(f"SINSG{i}", [128, GW], F32, pa) for i in range(2)]
            QS_ = sbuf("QS", [128, HPC * GW], F32, pa)
            KS_ = sbuf("KS", [128, HPC * GW], F32, pa)
            RT1 = sbuf("RT1", [128, GW], F32, pa)
            RT2 = sbuf("RT2", [128, GW], F32, pa)
            QR = QS_
            KR = KS_
            QD_ = sbuf("QD", [128, HPC * GW], F32, pa)
            QRH = sbuf("QRH", [128, HPC * GW], BF16, pa)
            QRL = sbuf("QRL", [128, HPC * GW], BF16, pa)
            KRH = sbuf("KRH", [128, HPC * GW], BF16, pa)
            KRL = sbuf("KRL", [128, HPC * GW], BF16, pa)
            QDH = sbuf("QDH", [128, HPC * GW], BF16, pa)
            QDL = sbuf("QDL", [128, HPC * GW], BF16, pa)
            VH = sbuf("VH", [128, HPC * GW], BF16, pa)
            VL = sbuf("VL", [128, HPC * GW], BF16, pa)
            VTH = sbuf("VTH", [128, HPC * 128], BF16, pa)
            VTL = sbuf("VTL", [128, HPC * 128], BF16, pa)
            KDH = sbuf("KDH", [128, HPC * 128], BF16, pa)
            KDL = sbuf("KDL", [128, HPC * 128], BF16, pa)
            SCD = sbuf("SCD", [128, 128], F32, pa)
            SCDH = sbuf("SCDH", [128, 128], BF16, pa)
            SCDL = sbuf("SCDL", [128, 128], BF16, pa)
            ST = sbuf("STT", [128, HPC * 128], F32, pa)
            STH = sbuf("STH", [128, HPC * 128], BF16, pa)
            STL = sbuf("STL", [128, HPC * 128], BF16, pa)
            OST = [sbuf(f"OST{i}", [128, HPC * 128], F32, pa) for i in range(2)]
            if os.environ.get("KPRINT_SBUF"):
                print("phaseA peak, sbuf bytes remaining:", nc.sbuf_bytes_remaining)

            def gsl(buf, ht, lo=0, hi=GW):
                return buf[:, ht * GW + lo: ht * GW + hi]

            t_s0 = em.op("vector", lambda e: e.memset(ST[:, :], 0.0), deps=tok_c)
            t_sh0 = em.op("scalar", lambda e: e.memzero(STH[:, :]), deps=tok_c)
            t_sl0 = em.op("vector", lambda e: e.memset(STL[:, :], 0.0), deps=[t_s0])
            st_tok = {"S": t_s0, "H": t_sh0, "L": t_sl0}

            xg_ld = [None] * NGRP
            xsplit = [None] * NGRP
            grp_data = [None] * NGRP
            grp_done = [None] * NGRP
            ost_free = [None, None]
            scd_free = None
            a2a_last = None
            trig_free = [None, None]

            sec_free = [None, None]
            def emit_load_split(g):
                b = g % 2
                kps = NKT // NSEC
                last_h = last_l = None
                for sec in range(NSEC):
                    tb = (g * NSEC + sec) % 2
                    deps = list(tok_wsplit)
                    if sec_free[tb] is not None:
                        deps.append(sec_free[tb])
                    t_ld = ld(XGT[tb][:, :].rearrange("p (k a) -> p k a", k=kps),
                              dram["xg_in"].ap()[g, sec * kps:(sec + 1) * kps]
                              .rearrange("k p a -> p k a"),
                              deps=deps)
                    sdeps = [t_ld]
                    if g >= 2 and sec == 0:
                        sdeps.append(grp_done[g - 2])
                    cs = slice(sec * SECW, (sec + 1) * SECW)
                    last_h, last_l = split_pair(XGT[tb][:, :], XH[b][:, cs],
                                                XL[b][:, cs], sdeps)
                    sec_free[tb] = last_l
                xsplit[g] = (last_h, last_l)

            def emit_group(g):
                b = g % 2
                th_x, tl_x = xsplit[g]
                d = {}
                tdeps = ([trig_free[b]] if trig_free[b] else [])
                t_cos = ld(COS[b][:, :], dram["cos_in"].ap()[g], deps=tdeps)
                t_sinsg = ld(SINSG[b][:, :], dram["sinsg_in"].ap()[g], deps=tdeps)

                silu_t = {}
                vsp = {}
                for pi_, (wh, wl, proj) in enumerate(
                        ((WQH, WQL, "q"), (WKH, WKL, "k"), (WVH, WVL, "v"))):
                    for ht in range(HPC):
                        slot = (pi_ * HPC + ht) % 2
                        bank = P[1 + slot]
                        pairs = []
                        for wb, xb in ((wh, XH[b]), (wh, XL[b]), (wl, XH[b])):
                            for kt in range(NKT):
                                pairs.append(
                                    (wb[:, (ht * NKT + kt) * 128:(ht * NKT + kt + 1) * 128],
                                     xb[:, kt * GW:(kt + 1) * GW]))
                        mmdeps = [th_x, tl_x] + \
                            ([bank_free[1 + slot]] if bank_free[1 + slot] else [])
                        t_mm = em.mm_group(pairs, bank[:, 0:GW], mmdeps)
                        if proj == "v":
                            tvh = em.op("scalar",
                                        lambda e, p=bank, o=gsl(VH, ht): e.copy(o, p[:, 0:GW]),
                                        deps=[t_mm])
                            tvl = em.op("vector",
                                        lambda e, p=bank, o=gsl(VL, ht), h=gsl(VH, ht):
                                        e.tensor_sub(o, p[:, 0:GW], h),
                                        deps=[t_mm, tvh])
                            vsp[ht] = (tvh, tvl)
                            bank_free[1 + slot] = tvl
                        else:
                            dst = gsl(QS_ if proj == "q" else KS_, ht)
                            tsi = em.op("scalar",
                                        lambda e, p=bank, o=dst:
                                        e.activation(o, p[:, 0:GW], AF.Silu),
                                        deps=[t_mm])
                            silu_t[(proj, ht)] = tsi
                            bank_free[1 + slot] = tsi

                for proj, src, dst, dsth, dstl in (
                        ("q", QS_, QR, QRH, QRL), ("k", KS_, KR, KRH, KRL)):
                    for ht in range(HPC):
                        tsi = silu_t[(proj, ht)]
                        t_sw = em.mm_group(
                            [(sb_perm[:, :], gsl(src, ht))], P[3][:, 0:GW],
                            deps=[tsi] + ([bank_free[3]] if bank_free[3] else []))
                        t_m1 = em.op("vector",
                                     lambda e, s=gsl(src, ht), bb=b:
                                     e.tensor_mul(RT1[:, :], s, COS[bb][:, :]),
                                     deps=[tsi, t_cos])
                        t_m2 = em.op("vector",
                                     lambda e, bb=b: e.tensor_mul(
                                         RT2[:, :], P[3][:, 0:GW], SINSG[bb][:, :]),
                                     deps=[t_sw, t_sinsg, t_m1])
                        bank_free[3] = t_m2
                        trig_free[b] = t_m2
                        t_rot = em.op("vector",
                                      lambda e, o=gsl(dst, ht):
                                      e.tensor_add(o, RT1[:, :], RT2[:, :]),
                                      deps=[t_m1, t_m2])
                        sh, sl = split_pair(gsl(dst, ht), gsl(dsth, ht),
                                            gsl(dstl, ht), [t_rot])
                        d[(proj, ht)] = (sh, sl)
                        if proj == "q":
                            t_qd = em.op("vector",
                                         lambda e, o=gsl(QD_, ht), s=gsl(dst, ht),
                                         q=gsl(sb_dq, ht):
                                         e.tensor_mul(o, s, q), deps=[t_rot, sh, sl])
                            dh, dl = split_pair(gsl(QD_, ht), gsl(QDH, ht),
                                                gsl(QDL, ht), [t_qd])
                            d[("qd", ht)] = (dh, dl)
                d["v"] = vsp
                grp_data[g] = d

            def emit_scan(g):
                nonlocal st_tok, scd_free, a2a_last
                d = grp_data[g]
                last_mm = None
                for c in range(GRP):
                    gc = g * GRP + c
                    peer = gc // (TPC // CHUNK)
                    lch = gc % (TPC // CHUNK)
                    for ht in range(HPC):
                        hs = slice(ht * 128, (ht + 1) * 128)
                        qh, ql = d[("q", ht)]
                        kh, kl = d[("k", ht)]
                        qdh, qdl = d[("qd", ht)]
                        vh, vl = d["v"][ht]
                        tr_deps = [vh, vl, kh, kl, tok_identb] + \
                            ([bank_free[7]] if bank_free[7] else [])
                        t_tr = None
                        srcs = [gsl(VH, ht, c * 128, (c + 1) * 128),
                                gsl(VL, ht, c * 128, (c + 1) * 128),
                                gsl(KRH, ht, c * 128, (c + 1) * 128),
                                gsl(KRL, ht, c * 128, (c + 1) * 128)]
                        for i, s_ap in enumerate(srcs):
                            def mk(s=s_ap, o=i * 256):
                                return lambda e: e.transpose(
                                    PB7[:, o:o + 128], s, sb_identb[:, :])
                            t_tr = em.op("tensor", mk(),
                                         deps=tr_deps if i == 0 else (), inc=(i == 3))
                        t_evh = em.op("scalar", lambda e, o=VTH[:, hs]:
                                      e.copy(o, PB7[:, 0:128]), deps=[t_tr])
                        t_evl = em.op("vector", lambda e, o=VTL[:, hs]:
                                      e.tensor_copy(o, PB7[:, 256:384]),
                                      deps=[t_tr, t_evh])
                        t_ekh0 = em.op("scalar",
                                       lambda e, o=KDH[:, hs]:
                                       e.copy(o, PB7[:, 512:640]),
                                       deps=[t_tr, t_evl])
                        t_ekl0 = em.op("vector",
                                       lambda e, o=KDL[:, hs]:
                                       e.tensor_copy(o, PB7[:, 768:896]),
                                       deps=[t_tr, t_ekh0])
                        bank_free[7] = t_ekl0
                        t_ekh = em.op("vector",
                                      lambda e, o=KDH[:, hs], k=sb_dk[:, ht:ht + 1]:
                                      e.tensor_scalar_mul(o, o, k),
                                      deps=[t_ekh0, t_ekl0])
                        t_ekl = em.op("vector",
                                      lambda e, o=KDL[:, hs], k=sb_dk[:, ht:ht + 1]:
                                      e.tensor_scalar_mul(o, o, k), deps=[t_ekl0])

                        sc_pairs = [
                            (gsl(KRH, ht, c * 128, (c + 1) * 128),
                             gsl(QRH, ht, c * 128, (c + 1) * 128)),
                            (gsl(KRH, ht, c * 128, (c + 1) * 128),
                             gsl(QRL, ht, c * 128, (c + 1) * 128)),
                            (gsl(KRL, ht, c * 128, (c + 1) * 128),
                             gsl(QRH, ht, c * 128, (c + 1) * 128))]
                        t_sc = em.mm_group(
                            sc_pairs, P[4][:, 0:128],
                            deps=[qh, ql, kh, kl] +
                            ([bank_free[4]] if bank_free[4] else []))
                        t_scd = em.op("vector", lambda e, dt=sb_dt[:, hs]:
                                      e.tensor_mul(SCD[:, :], P[4][:, 0:128], dt),
                                      deps=[t_sc] + ([scd_free] if scd_free else []))
                        bank_free[4] = t_scd
                        t_sdh, t_sdl = split_pair(SCD[:, :], SCDH[:, :], SCDL[:, :],
                                                  [t_scd])
                        scd_free = t_sdl

                        kv_pairs = [(KDH[:, hs], VTH[:, hs]),
                                    (KDH[:, hs], VTL[:, hs]),
                                    (KDL[:, hs], VTH[:, hs])]
                        t_kv = em.mm_group(
                            kv_pairs, P[6][:, 0:128],
                            deps=[t_ekh, t_ekl, t_evh, t_evl] +
                            ([bank_free[6]] if bank_free[6] else []))
                        last_mm = t_kv

                        o_pairs = [
                            (SCDH[:, :], VTH[:, hs]),
                            (SCDH[:, :], VTL[:, hs]),
                            (SCDL[:, :], VTH[:, hs]),
                            (gsl(QDH, ht, c * 128, (c + 1) * 128), STH[:, hs]),
                            (gsl(QDH, ht, c * 128, (c + 1) * 128), STL[:, hs]),
                            (gsl(QDL, ht, c * 128, (c + 1) * 128), STH[:, hs])]
                        t_o = em.mm_group(
                            o_pairs, P[5][:, 0:128],
                            deps=[t_sdh, t_sdl, t_evh, t_evl, qdh, qdl,
                                  st_tok["H"], st_tok["L"]] +
                            ([bank_free[5]] if bank_free[5] else []))
                        ev_deps = [t_o]
                        if ht == 0 and ost_free[c % 2] is not None:
                            ev_deps.append(ost_free[c % 2])
                        t_oev = em.op("vector",
                                      lambda e, o=OST[c % 2][:, hs]:
                                      e.tensor_copy(o, P[5][:, 0:128]), deps=ev_deps)
                        bank_free[5] = t_oev
                        t_sup = em.op("vector",
                                      lambda e, s=ST[:, hs], lam=sb_lam[:, ht:ht + 1]:
                                      e.scalar_tensor_tensor(
                                          s, s, lam, P[6][:, 0:128], ALU.mult, ALU.add),
                                      deps=[t_kv, st_tok["S"], st_tok["H"], st_tok["L"]])
                        bank_free[6] = t_sup
                        t_nh = em.op("scalar", lambda e, s=ST[:, hs], o=STH[:, hs]:
                                     e.copy(o, s), deps=[t_sup, t_o])
                        t_nl = em.op("vector", lambda e, s=ST[:, hs], o=STL[:, hs],
                                     h=STH[:, hs]:
                                     e.tensor_sub(o, s, h), deps=[t_sup, t_nh])
                        st_tok = {"S": t_sup, "H": t_nh, "L": t_nl}
                        if ht == HPC - 1:
                            def mk(o=OST[c % 2],
                                   dd=a2a_in.ap()[peer, lch * 128:(lch + 1) * 128, :]):
                                return lambda e: e.dma_start(out=dd, in_=o[:, :])
                            t_st = em.dma("gpsimd", mk(), deps=[t_oev])
                            ost_free[c % 2] = t_st
                            a2a_last = t_st
                grp_done[g] = last_mm

            ngrp_run = 0 if stop == "init" else NGRP
            for g in range(ngrp_run):
                emit_load_split(g)
                emit_group(g)
                if stop != "noscan":
                    emit_scan(g)

            if stop in ("phaseA", "init", "noscan"):
                em.wait("gpsimd", [("QG", em.n["QG"])])
                t_cc = None
            else:
                t_cc = em.cc(lambda e: e.collective_compute(
                    "AllToAll", ALU.bypass, replica_groups=[list(range(NCORES))],
                    ins=[a2a_in.ap().opt()], outs=[a2a_out.ap().opt()]),
                    deps=[a2a_last, ("QG", em.n["QG"])])
            if debug_dump:
                em.dma("gpsimd", lambda e: e.dma_start(
                    out=dbg_ain.ap().rearrange("c t d -> c (t d)"),
                    in_=a2a_in.ap().rearrange("c t d -> c (t d)")), deps=[t_cc])
                em.dma("gpsimd", lambda e: e.dma_start(
                    out=dbg_aout.ap().rearrange("c t d -> c (t d)"),
                    in_=a2a_out.ap().rearrange("c t d -> c (t d)")), deps=[t_cc])

        join = [t for t in em.join_tokens() if t[0] != "CC"]
        for i in range(8):
            bank_free[i] = None

        with ExitStack() as pb:
            P7F = pb.enter_context(nc.psum_tensor("pb7b", [128, 512], F32))
            PW = [P[6], P7F]
            XBT = [sbuf(f"XBT{i}", [128, (NKT // 8) * SLAB], F32, pb)
                   for i in range(2)]
            XBH = sbuf("XBH", [128, NKT * SLAB], BF16, pb)
            XBL = sbuf("XBL", [128, NKT * SLAB], BF16, pb)
            G1H = sbuf("G1H", [128, SLAB], BF16, pb)
            G1L = sbuf("G1L", [128, SLAB], BF16, pb)
            SIG = sbuf("SIGB", [128, 512], F32, pb)
            GBUF = SIG[:, 256:512]
            OTM = sbuf("OTM", [128, D], F32, pb)
            SQ = sbuf("SQ", [128, D], F32, pb)
            MS = sbuf("MS", [128, NKT], F32, pb)
            RSD = sbuf("RSD", [128, NKT], F32, pb)
            TMP1 = sbuf("TMP1", [128, NKT], F32, pb)
            RR = sbuf("RR", [128, NKT * SLAB], F32, pb)
            X2H = sbuf("X2H", [128, NKT * SLAB], BF16, pb)
            X2L = sbuf("X2L", [128, NKT * SLAB], BF16, pb)
            HH = sbuf("HH", [128, NHT * SLAB], BF16, pb)
            HL = sbuf("HL", [128, NHT * SLAB], BF16, pb)
            ONH = HH
            ONL = HL
            WOF = sbuf("WOF", [128, NKT * 128], F32, pb)
            WOH = [sbuf(f"WOH{i}", [128, NKT * 128], BF16, pb) for i in range(2)]
            WOL = [sbuf(f"WOL{i}", [128, NKT * 128], BF16, pb) for i in range(2)]
            BGB = [sbuf(f"BGB{i}", [128, NKT * 128], BF16, pb) for i in range(2)]
            BUB = [sbuf(f"BUB{i}", [128, NKT * 128], BF16, pb) for i in range(2)]
            BDB = [sbuf(f"BDB{i}", [128, (NHT // 2) * 128], BF16, pb)
                   for i in range(2)]
            HSUM = sbuf("HSUM", [1, SLAB], F32, pb)
            CORR = sbuf("CORR", [128, SLAB], F32, pb)
            MEANR = sbuf("MEANR", [1, SLAB], F32, pb)
            VARR = sbuf("VARR", [1, SLAB], F32, pb)
            RSR = sbuf("RSR", [1, SLAB], F32, pb)
            R2TM = OTM
            MCOL = sbuf("MCOL", [128, 2], F32, pb)
            VCOL = sbuf("VCOL", [128, 2], F32, pb)
            SCOL = sbuf("SCOL", [128, 4], F32, pb)
            if os.environ.get("KPRINT_SBUF"):
                print("phaseB peak, sbuf bytes remaining:", nc.sbuf_bytes_remaining)

            def newton_rsqrt(eng, y_ap, a_ap, tmp_ap, deps):
                t1 = em.op(eng, lambda e: e.tensor_mul(tmp_ap, y_ap, y_ap), deps=deps)
                t2 = em.op(eng, lambda e: e.tensor_mul(tmp_ap, tmp_ap, a_ap), deps=[t1])
                t3 = em.op(eng, lambda e: e.tensor_scalar(
                    tmp_ap, tmp_ap, -0.5, 1.5, ALU.mult, ALU.add), deps=[t2])
                return em.op(eng, lambda e: e.tensor_mul(y_ap, y_ap, tmp_ap), deps=[t3])

            nslab_run = {"phaseA": 0, "a2a": 0, "init": 0, "noscan": 0,
                         "slab1": 1}.get(stop, NSLAB)
            pd = {}
            xbt_free = [None, None]
            wo_slot_free = [None, None]
            bg_slot_free = [None, None]
            bd_slot_free = [None, None]
            wof_free = None
            for s in range(nslab_run):
                nd = {}
                th_x = tl_x = None
                kps = NKT // 8
                for sec in range(8):
                    tb = (s * 8 + sec) % 2
                    deps = (join if s == 0 and sec < 2 else [])
                    if xbt_free[tb] is not None:
                        deps = deps + [xbt_free[tb]]
                    t_xb = ld(XBT[tb][:, :].rearrange("p (k t) -> p k t", k=kps),
                              dram["xs_in"].ap()[sec * kps:(sec + 1) * kps, :,
                                                 s * SLAB:(s + 1) * SLAB]
                              .rearrange("k p t -> p k t"),
                              deps=deps)
                    sdeps = [t_xb]
                    if s and sec == 0:
                        sdeps.append(pd["xhl_used"])
                    cs = slice(sec * kps * SLAB, (sec + 1) * kps * SLAB)
                    th_x, tl_x = split_pair(XBT[tb][:, :], XBH[:, cs], XBL[:, cs],
                                            sdeps)
                    xbt_free[tb] = tl_x

                g1_pairs = []
                for wb, xb in ((WG1H, XBH), (WG1H, XBL), (WG1L, XBH)):
                    for kt in range(NKT):
                        g1_pairs.append((wb[:, kt * 128:(kt + 1) * 128],
                                         xb[:, kt * SLAB:(kt + 1) * SLAB]))
                t_g1 = em.mm_group(g1_pairs, P[1][:, 0:SLAB],
                                   deps=[th_x, tl_x] +
                                   ([bank_free[1]] if bank_free[1] else []))
                th_g1, tl_g1 = split_pair(P[1][:, 0:SLAB], G1H[:, :], G1L[:, :],
                                          [t_g1])
                bank_free[1] = tl_g1

                on_last = None
                otm_free = pd.get("otm_used")
                for bk in range(2):
                    deps = ([t_cc] + join) if (s == 0 and bk == 0) else []
                    if otm_free is not None:
                        deps = deps + [otm_free, t_cc]
                    if s and bk == 0 and "out_done" in pd:
                        deps = deps + [pd["out_done"]]
                    def mko(b=bk, p=0, ss=s):
                        return lambda e: e.dma_start(
                            out=OTM[:, p * HPC * HD:(p + 1) * HPC * HD],
                            in_=a2a_out.ap()[p, ss * SLAB + b * 128:
                                             ss * SLAB + (b + 1) * 128, :])
                    t_ol = em.dma_batch("scalar",
                                        [mko(bk, p) for p in range(NCORES)],
                                        deps=deps)
                    og_toks = []
                    for msl in range(4):
                        gate_pairs = [
                            (G1H[:, bk * 128:(bk + 1) * 128],
                             WG2H[:, msl * 512:(msl + 1) * 512]),
                            (G1H[:, bk * 128:(bk + 1) * 128],
                             WG2L[:, msl * 512:(msl + 1) * 512]),
                            (G1L[:, bk * 128:(bk + 1) * 128],
                             WG2H[:, msl * 512:(msl + 1) * 512])]
                        t_gt = em.mm_group(
                            gate_pairs, P[2][:, :],
                            deps=[th_g1, tl_g1] +
                            ([bank_free[2]] if bank_free[2] else []))
                        t_sg_ = em.op("scalar", lambda e: e.activation(
                            SIG[:, :], P[2][:, :], AF.Sigmoid),
                            deps=[t_gt] + (og_toks[-1:] if og_toks else []))
                        bank_free[2] = t_sg_
                        def mkog(m=msl):
                            return lambda e: e.tensor_mul(
                                OTM[:, m * 512:(m + 1) * 512],
                                OTM[:, m * 512:(m + 1) * 512], SIG[:, :])
                        t_og = em.op("vector", mkog(), deps=[t_sg_, t_ol])
                        og_toks.append(t_og)

                    t_sq = em.op("scalar",
                                 lambda e: e.square(SQ[:, :], OTM[:, :]),
                                 deps=og_toks +
                                 ([pd["sq_used"]] if (s and bk == 0) else []) +
                                 ([on_last] if on_last else []))
                    t_ms = em.op("vector", lambda e: e.tensor_reduce(
                        MS[:, :], SQ[:, :].rearrange("p (f e) -> p f e", f=NKT),
                        AX.X, ALU.add), deps=[t_sq])
                    t_a = em.op("vector", lambda e: e.tensor_scalar(
                        MS[:, :], MS[:, :], 1.0 / HD, RMS_EPS, ALU.mult, ALU.add),
                        deps=[t_ms])
                    t_s1 = em.op("scalar", lambda e: e.activation(
                        RSD[:, :], MS[:, :], AF.Sqrt), deps=[t_a])
                    t_rc = em.op("vector", lambda e: e.reciprocal(RSD[:, :], RSD[:, :]),
                                 deps=[t_s1])
                    t_rf = newton_rsqrt("vector", RSD[:, :], MS[:, :], TMP1[:, :],
                                        [t_rc, t_a])
                    sc_toks = []
                    for ft in range(NKT):
                        def mks(f=ft):
                            return lambda e: e.tensor_scalar_mul(
                                OTM[:, f * 128:(f + 1) * 128],
                                OTM[:, f * 128:(f + 1) * 128], RSD[:, f:f + 1])
                        sc_toks.append(em.op("vector", mks(), deps=[t_rf]))
                    t_nw = em.op("vector", lambda e: e.tensor_mul(
                        OTM[:, :], OTM[:, :], sb_nwtm[:, :]), deps=sc_toks)
                    sc_toks = [t_nw] * NKT
                    ev_prev = None
                    t_tp_last = None
                    for q in range(4):
                        bidx = 4 + (q % 2)
                        fill_deps = sc_toks[q * 4:(q + 1) * 4] + \
                            ([bank_free[bidx]] if bank_free[bidx] else [])
                        t_tp = None
                        for i in range(4):
                            ft = q * 4 + i
                            def mkt(f=ft, o=i * 128, pk=P[bidx]):
                                return lambda e: e.transpose(
                                    pk[:, o:o + 128],
                                    OTM[:, f * 128:(f + 1) * 128], sb_ident[:, :])
                            t_tp = em.op("tensor", mkt(),
                                         deps=fill_deps if i == 0 else (), inc=(i == 3))
                        t_tp_last = t_tp
                        for i in range(4):
                            ft = q * 4 + i
                            col = ft * SLAB + bk * 128
                            def mkh(c=col, o=i * 128, pk=P[bidx]):
                                return lambda e: e.copy(
                                    ONH[:, c:c + 128], pk[:, o:o + 128])

                            def mkl(c=col, o=i * 128, pk=P[bidx]):
                                return lambda e: e.tensor_sub(
                                    ONL[:, c:c + 128], pk[:, o:o + 128],
                                    ONH[:, c:c + 128])
                            t_h_ = em.op("scalar", mkh(),
                                         deps=[t_tp] +
                                         ([ev_prev] if (i > 0 and ev_prev) else []) +
                                         ([pd["on_used"]] if (s and bk == 0 and q == 0
                                                              and i == 0) else []))
                            t_l_ = em.op("vector", mkl(), deps=[t_tp, t_h_])
                            ev_prev = t_l_
                        bank_free[bidx] = ev_prev
                    on_last = ev_prev
                    otm_free = t_tp_last
                nd["otm_used"] = otm_free
                nd["sq_used"] = on_last

                r_toks = []
                for mt in range(NKT):
                    sl = mt % 2
                    t_wf = ld(WOF[:, :].rearrange("p (k a) -> p k a", k=NKT),
                              dram["wo_in"].ap()[mt].rearrange("k p a -> p k a"),
                              deps=([wof_free] if wof_free else []) +
                              (join if s == 0 and mt == 0 else []))
                    def mkwh(b=sl):
                        return lambda e: e.copy(WOH[b][:, :], WOF[:, :])

                    def mkwl(b=sl):
                        return lambda e: e.tensor_sub(WOL[b][:, :], WOF[:, :],
                                                      WOH[b][:, :])
                    t_wh = em.op("scalar", mkwh(),
                                 deps=[t_wf] +
                                 ([wo_slot_free[sl]] if wo_slot_free[sl] else
                                  (join if s == 0 else [])))
                    t_wl = em.op("vector", mkwl(), deps=[t_wf, t_wh])
                    wof_free = t_wl
                    wo_pairs = []
                    for wb, ob in ((WOH[sl], ONH), (WOH[sl], ONL), (WOL[sl], ONH)):
                        for kt in range(NKT):
                            wo_pairs.append((wb[:, kt * 128:(kt + 1) * 128],
                                             ob[:, kt * SLAB:(kt + 1) * SLAB]))
                    t_mm = em.mm_group(
                        wo_pairs, PW[sl][:, 0:SLAB],
                        deps=[on_last, t_wl] +
                        ([bank_free[6 + sl]] if bank_free[6 + sl] else []))
                    wo_slot_free[sl] = t_mm
                    def mr1(m=mt, pk=PW[sl]):
                        return lambda e: e.tensor_add(
                            RR[:, m * SLAB:(m + 1) * SLAB], pk[:, 0:SLAB],
                            XBH[:, m * SLAB:(m + 1) * SLAB])

                    def mr2(m=mt):
                        return lambda e: e.tensor_add(
                            RR[:, m * SLAB:(m + 1) * SLAB],
                            RR[:, m * SLAB:(m + 1) * SLAB],
                            XBL[:, m * SLAB:(m + 1) * SLAB])
                    t_r1 = em.op("vector", mr1(),
                                 deps=[t_mm, th_x] +
                                 ([pd["rr_used"]] if (s and mt == 0) else []))
                    t_r2_ = em.op("vector", mr2(), deps=[t_r1, tl_x])
                    bank_free[6 + sl] = t_r2_
                    r_toks.append(t_r2_)
                nd["xb_used"] = r_toks[-1]
                nd["xhl_used"] = r_toks[-1]
                nd["on_used"] = r_toks[-1]
                dbg_tok = None
                if debug_dump:
                    t_dON = em.dma("gpsimd", lambda e, ss=s: e.dma_start(
                        out=dbg_onh.ap()[ss], in_=ONH[:, :]), deps=[on_last])
                    t_dON = em.dma("gpsimd", lambda e, ss=s: e.dma_start(
                        out=dbg_onl.ap()[ss], in_=ONL[:, :]), deps=[on_last])
                    nd["on_used"] = t_dON
                    dbg_tok = em.dma("gpsimd", lambda e, ss=s: e.dma_start(
                        out=dbg_rr.ap()[ss], in_=RR[:, :]), deps=r_toks)

                mean_pairs = [(sb_invd[:, :], RR[:, kt * SLAB:(kt + 1) * SLAB])
                              for kt in range(NKT)]
                t_mean = em.mm_group(mean_pairs, P[0][0:1, 0:SLAB],
                                     deps=r_toks + ([bank_free[0]] if bank_free[0]
                                                    else []))
                t_mrow = em.op("vector", lambda e: e.tensor_copy(
                    MEANR[:, :], P[0][0:1, 0:SLAB]), deps=[t_mean])
                t_ss = None
                t_qq = None
                for half in range(2):
                    hw = NKT * SLAB // 2
                    def mkq(hf=half, w=hw):
                        return lambda e: e.square(SQ[:, 0:w], RR[:, hf * w:(hf + 1) * w])
                    t_qq = em.op("scalar", mkq(),
                                 deps=(r_toks + [on_last]) if half == 0 else
                                 [t_ss, t_qq])
                    sq_pairs = [(sb_invd[:, :], SQ[:, kt * SLAB:(kt + 1) * SLAB])
                                for kt in range(NKT // 2)]
                    t_ss = em.mm_group(
                        sq_pairs, P[1][0:1, 0:SLAB],
                        deps=[t_qq, tl_g1] +
                        ([bank_free[1]] if (half == 0 and bank_free[1]) else []),
                        start=(half == 0), stop=(half == 1))
                t_v1 = em.op("vector", lambda e: e.tensor_mul(
                    RSR[:, :], MEANR[:, :], MEANR[:, :]), deps=[t_mrow])
                t_v2 = em.op("vector", lambda e: e.tensor_sub(
                    VARR[:, :], P[1][0:1, 0:SLAB], RSR[:, :]), deps=[t_ss, t_v1])
                t_s2 = em.op("scalar", lambda e: e.activation(
                    RSR[:, :], VARR[:, :], AF.Sqrt, bias=LN_EPS), deps=[t_v2, t_v1])
                t_rc2 = em.op("vector", lambda e: e.reciprocal(RSR[:, :], RSR[:, :]),
                              deps=[t_s2])
                t_ve = em.op("vector", lambda e: e.tensor_scalar_add(
                    VARR[:, :], VARR[:, :], LN_EPS), deps=[t_v2])
                t_rf2 = newton_rsqrt("vector", RSR[:, :], VARR[:, :], HSUM[:, :],
                                     [t_rc2, t_ve])
                t_bm = em.mm_group([(sb_ones_row[:, :], MEANR[:, :])], P[0][:, 0:SLAB],
                                   deps=[t_mrow, t_v1, t_mean])
                t_br = em.mm_group([(sb_ones_row[:, :], RSR[:, :])], P[1][:, 0:SLAB],
                                   deps=[t_rf2, t_v2])
                x2_toks = []
                for kt in range(NKT):
                    def ms1(k=kt):
                        return lambda e: e.tensor_sub(
                            RR[:, k * SLAB:(k + 1) * SLAB],
                            RR[:, k * SLAB:(k + 1) * SLAB], P[0][:, 0:SLAB])

                    def ms2(k=kt):
                        return lambda e: e.tensor_mul(
                            RR[:, k * SLAB:(k + 1) * SLAB],
                            RR[:, k * SLAB:(k + 1) * SLAB], P[1][:, 0:SLAB])
                    t_c1 = em.op("vector", ms1(),
                                 deps=[t_bm] + ([dbg_tok] if dbg_tok else []))
                    t_c2 = em.op("vector", ms2(), deps=[t_br, t_c1])
                    th2, tl2 = split_pair(RR[:, kt * SLAB:(kt + 1) * SLAB],
                                          X2H[:, kt * SLAB:(kt + 1) * SLAB],
                                          X2L[:, kt * SLAB:(kt + 1) * SLAB],
                                          [t_c2] + ([pd["x2_used"]]
                                                    if (s and kt == 0) else []))
                    x2_toks += [th2, tl2]
                x2_last = x2_toks[-1]
                bank_free[0] = x2_last
                bank_free[1] = x2_last
                if debug_dump:
                    t_dx2 = em.dma("gpsimd", lambda e, ss=s: e.dma_start(
                        out=dbg_x2h.ap()[ss], in_=X2H[:, :]), deps=[x2_last])
                    nd["x2_used"] = t_dx2

                h_toks = []
                ev_prev = None
                for ht in range(NHT):
                    sl = ht % 2
                    def mbg(h=ht, b=sl):
                        return lambda e: e.dma_start(
                            out=BGB[b][:, :].rearrange("p (k a) -> p k a", k=NKT),
                            in_=dram["bg_in"].ap()[h].rearrange("k p a -> p k a"))

                    def mbu(h=ht, b=sl):
                        return lambda e: e.dma_start(
                            out=BUB[b][:, :].rearrange("p (k a) -> p k a", k=NKT),
                            in_=dram["bu_in"].ap()[h].rearrange("k p a -> p k a"))
                    t_bg = em.dma_batch("gpsimd", [mbg(), mbu()],
                                        deps=[bg_slot_free[sl]] if bg_slot_free[sl]
                                        else (join if s == 0 else []))
                    t_bu = t_bg
                    g_pairs = []
                    u_pairs = []
                    for xb in (X2H, X2L):
                        for kt in range(NKT):
                            g_pairs.append((BGB[sl][:, kt * 128:(kt + 1) * 128],
                                            xb[:, kt * SLAB:(kt + 1) * SLAB]))
                            u_pairs.append((BUB[sl][:, kt * 128:(kt + 1) * 128],
                                            xb[:, kt * SLAB:(kt + 1) * SLAB]))
                    t_g = em.mm_group(g_pairs, P[5][:, 0:SLAB],
                                      deps=x2_toks + [t_bg] +
                                      ([bank_free[5]] if bank_free[5] else []))
                    t_u = em.mm_group(u_pairs, P[4][:, 0:SLAB],
                                      deps=[t_bu, x2_last] +
                                      ([bank_free[4]] if bank_free[4] else []))
                    bg_slot_free[sl] = t_u
                    t_e0 = em.op("vector", lambda e: e.tensor_copy(
                        GBUF, P[5][:, 0:SLAB]),
                        deps=[t_g] + ([ev_prev] if ev_prev else []) +
                        ([x2_last] if ht == 0 else []))
                    bank_free[5] = t_e0
                    def me1(h=ht):
                        return lambda e: e.activation(
                            SIG[:, 0:SLAB], GBUF, AF.Sigmoid,
                            scale=sb_sg[:, h:h + 1])
                    t_e1 = em.op("scalar", me1(),
                                 deps=[t_e0] + ([ev_prev] if ev_prev else []))
                    t_e2 = em.op("vector", lambda e: e.tensor_mul(
                        SQ[:, 0:SLAB], GBUF, SIG[:, 0:SLAB]),
                        deps=[t_e1])
                    t_e3 = em.op("vector", lambda e: e.tensor_mul(
                        SQ[:, 0:SLAB], SQ[:, 0:SLAB], P[4][:, 0:SLAB]),
                        deps=[t_e2, t_u])
                    bank_free[4] = t_e3
                    def mh(h=ht):
                        return lambda e: e.activation(
                            HH[:, h * SLAB:(h + 1) * SLAB], SQ[:, 0:SLAB], AF.Copy,
                            scale=sb_sgu[:, h:h + 1])

                    def ml(h=ht):
                        return lambda e: e.scalar_tensor_tensor(
                            HL[:, h * SLAB:(h + 1) * SLAB], SQ[:, 0:SLAB],
                            sb_sgu[:, h:h + 1], HH[:, h * SLAB:(h + 1) * SLAB],
                            ALU.mult, ALU.subtract)
                    t_h_ = em.op("scalar", mh(),
                                 deps=[t_e3, tok_sgu] +
                                 ([pd["hh_used"]] if (s and ht == 0) else []))
                    t_l_ = em.op("vector", ml(), deps=[t_h_, t_e3])
                    ev_prev = t_l_
                    h_toks += [t_h_, t_l_]
                h_last = h_toks[-1]
                if debug_dump:
                    t_dhh = em.dma("gpsimd", lambda e, ss=s: e.dma_start(
                        out=dbg_hh.ap()[ss], in_=HH[:, :]), deps=h_toks)
                    nd["hh_used"] = t_dhh

                hs_pairs = []
                for hb in (HH, HL):
                    for ht in range(NHT):
                        hs_pairs.append((sb_onesb[:, :],
                                         hb[:, ht * SLAB:(ht + 1) * SLAB]))
                t_hs = em.mm_group(hs_pairs, P[0][0:1, 0:SLAB],
                                   deps=h_toks + [tok_onesb] +
                                   ([bank_free[0]] if bank_free[0] else []))
                t_hsum = em.op("vector", lambda e: e.tensor_copy(
                    HSUM[:, :], P[0][0:1, 0:SLAB]), deps=[t_hs, t_rf2])
                t_cmm = em.mm_group([(sb_c128_row[:, :], HSUM[:, :])], P[0][:, 0:SLAB],
                                    deps=[t_hsum])
                t_corr = em.op("vector", lambda e: e.tensor_copy(
                    CORR[:, :], P[0][:, 0:SLAB]), deps=[t_cmm])
                bank_free[0] = t_corr

                r2_toks = []
                NHH = NHT // 2
                for mt in range(NKT):
                    sl = mt % 2
                    t_half = None
                    for half in range(2):
                        hb_i = (mt * 2 + half) % 2
                        def mbd(m=mt, hf=half, b=hb_i):
                            return lambda e: e.dma_start(
                                out=BDB[b][:, :].rearrange("p (k a) -> p k a",
                                                           k=NHH),
                                in_=dram["bd_in"].ap()[m, hf * NHH:(hf + 1) * NHH]
                                .rearrange("k p a -> p k a"))
                        t_bd = em.dma("gpsimd", mbd(),
                                      deps=[bd_slot_free[hb_i]] if bd_slot_free[hb_i]
                                      else (join if s == 0 else []))
                        d_pairs = []
                        for hb in (HH, HL):
                            for hi in range(NHH):
                                ht = half * NHH + hi
                                d_pairs.append(
                                    (BDB[hb_i][:, hi * 128:(hi + 1) * 128],
                                     hb[:, ht * SLAB:(ht + 1) * SLAB]))
                        gdeps = [h_last, t_bd]
                        if half == 0 and bank_free[6 + sl]:
                            gdeps.append(bank_free[6 + sl])
                        if half == 1:
                            gdeps = [t_bd]
                        t_half = em.mm_group(d_pairs, PW[sl][:, 0:SLAB],
                                             deps=gdeps, start=(half == 0),
                                             stop=(half == 1), inc=True)
                        bd_slot_free[hb_i] = t_half
                    t_d = t_half
                    def ma1(pk=PW[sl]):
                        return lambda e: e.tensor_add(
                            SIG[:, 0:SLAB], pk[:, 0:SLAB], CORR[:, :])

                    def ma2(m=mt):
                        return lambda e: e.scalar_tensor_tensor(
                            RR[:, m * SLAB:(m + 1) * SLAB], SIG[:, 0:SLAB],
                            sb_sd[:, m:m + 1], X2H[:, m * SLAB:(m + 1) * SLAB],
                            ALU.mult, ALU.add)

                    def ma3(m=mt):
                        return lambda e: e.tensor_add(
                            RR[:, m * SLAB:(m + 1) * SLAB],
                            RR[:, m * SLAB:(m + 1) * SLAB],
                            X2L[:, m * SLAB:(m + 1) * SLAB])
                    t_a1 = em.op("vector", ma1(),
                                 deps=[t_d, t_corr] + ([h_last] if mt == 0 else []) +
                                 (r2_toks[-1:] if r2_toks else []))
                    t_a2 = em.op("vector", ma2(), deps=[t_a1, x2_last])
                    t_a3 = em.op("vector", ma3(), deps=[t_a2])
                    bank_free[6 + sl] = t_a3
                    r2_toks.append(t_a3)
                nd["hh_used"] = r2_toks[-1]
                nd["x2_used"] = r2_toks[-1]
                nd["rr_used"] = r2_toks[-1]
                if debug_dump:
                    t_dr2 = em.dma("gpsimd", lambda e, ss=s: e.dma_start(
                        out=dbg_r2.ap()[ss], in_=RR[:, :]), deps=r2_toks)
                    nd["rr_used"] = t_dr2

                out_prev = None
                for bk in range(2):
                    ev_toks = []
                    for q in range(4):
                        bidx = 2 + (q % 2)
                        fill_deps = [r2_toks[q * 4 + i] for i in range(4)] + \
                            ([bank_free[bidx]] if bank_free[bidx] else [])
                        t_tp = None
                        for i in range(4):
                            ft = q * 4 + i
                            def mkt2(b=bk, f=ft, o=i * 128, pk=P[bidx]):
                                return lambda e: e.transpose(
                                    pk[:, o:o + 128],
                                    RR[:, f * SLAB + b * 128: f * SLAB + (b + 1) * 128],
                                    sb_ident[:, :])
                            t_tp = em.op("tensor", mkt2(),
                                         deps=fill_deps if i == 0 else (),
                                         inc=(i == 3))
                        def mev(qq=q, pk=P[bidx]):
                            return lambda e: e.tensor_copy(
                                R2TM[:, qq * 512:(qq + 1) * 512], pk[:, :])
                        t_ev = em.op("vector", mev(),
                                     deps=[t_tp] + ([out_prev] if (out_prev and q == 0)
                                                    else []))
                        bank_free[bidx] = t_ev
                        ev_toks.append(t_ev)
                    t_sum = em.op("vector", lambda e: e.tensor_reduce(
                        MCOL[:, 0:1], R2TM[:, :], AX.X, ALU.add), deps=ev_toks)
                    t_sq_ = em.op("scalar", lambda e: e.square(SQ[:, :], R2TM[:, :]),
                                  deps=ev_toks + h_toks[-1:])
                    t_ssq = em.op("vector", lambda e: e.tensor_reduce(
                        MCOL[:, 1:2], SQ[:, :], AX.X, ALU.add), deps=[t_sq_])
                    t_m = em.op("vector", lambda e: e.tensor_scalar_mul(
                        VCOL[:, 0:1], MCOL[:, 0:1], 1.0 / D), deps=[t_sum])
                    t_m2 = em.op("vector", lambda e: e.tensor_mul(
                        VCOL[:, 1:2], VCOL[:, 0:1], VCOL[:, 0:1]), deps=[t_m])
                    t_v = em.op("vector", lambda e: e.scalar_tensor_tensor(
                        SCOL[:, 0:1], MCOL[:, 1:2], 1.0 / D, VCOL[:, 1:2],
                        ALU.mult, ALU.subtract), deps=[t_ssq, t_m2])
                    t_sr = em.op("scalar", lambda e: e.activation(
                        SCOL[:, 1:2], SCOL[:, 0:1], AF.Sqrt, bias=LN_EPS),
                        deps=[t_v])
                    t_rc_ = em.op("vector", lambda e: e.reciprocal(
                        SCOL[:, 1:2], SCOL[:, 1:2]), deps=[t_sr])
                    t_ve_ = em.op("vector", lambda e: e.tensor_scalar_add(
                        SCOL[:, 0:1], SCOL[:, 0:1], LN_EPS), deps=[t_v])
                    t_rf_ = newton_rsqrt("vector", SCOL[:, 1:2], SCOL[:, 0:1],
                                         SCOL[:, 2:3], [t_rc_, t_ve_])
                    t_out = em.op("vector", lambda e: e.tensor_scalar(
                        R2TM[:, :], R2TM[:, :], VCOL[:, 0:1], SCOL[:, 1:2],
                        ALU.subtract, ALU.mult), deps=[t_rf_] + ev_toks)
                    def mst(b=bk, ss=s):
                        return lambda e: e.dma_start(
                            out=out_t.ap()[ss * SLAB + b * 128:
                                           ss * SLAB + (b + 1) * 128, :],
                            in_=R2TM[:, :])
                    t_st = em.dma("gpsimd", mst(), deps=[t_out])
                    out_prev = t_st
                nd["out_done"] = out_prev
                pd = nd

            em.wait("gpsimd", [("QG", em.n["QG"])])

        def replay(eng_name):
            def go(eng):
                for fn, waits, inckey in em.plan[eng_name]:
                    for (k, v) in waits:
                        eng.wait_ge(sems[k], v)
                    if fn is None:
                        continue
                    ins = fn(eng)
                    if inckey is not None:
                        ins.then_inc(sems[inckey], 16 if inckey.startswith("Q") else 1)
            return go

        blk.sync(replay("sync"))
        blk.tensor(replay("tensor"))
        blk.vector(replay("vector"))
        blk.scalar(replay("scalar"))
        blk.gpsimd(replay("gpsimd"))

    return nc



def _prep_inputs(inputs):
    x = np.asarray(inputs["L0_attn_x"], dtype=np.float32)[0]
    xT = np.ascontiguousarray(x.T)
    xg = np.ascontiguousarray(
        xT.reshape(NKT, 128, NGRP, GW).transpose(2, 0, 1, 3))
    wq = np.asarray(inputs["L0_attn_Wq"], dtype=np.float32)
    wk = np.asarray(inputs["L0_attn_Wk"], dtype=np.float32)
    wv = np.asarray(inputs["L0_attn_Wv"], dtype=np.float32)
    wo = np.asarray(inputs["L0_attn_Wo"], dtype=np.float32)
    wg1 = np.asarray(inputs["L0_attn_Wg1"], dtype=np.float32)
    wg2 = np.asarray(inputs["L0_attn_Wg2"], dtype=np.float32)
    nw = np.asarray(inputs["L0_attn_norm_weight"], dtype=np.float32)
    bg = np.asarray(inputs["L0_ffn_wq_gate"]).astype(np.uint8)
    bu = np.asarray(inputs["L0_ffn_wq_up"]).astype(np.uint8)
    bd = np.asarray(inputs["L0_ffn_wq_down"]).astype(np.uint8)
    sg = np.asarray(inputs["L0_ffn_s_gate"], dtype=np.float32)
    su = np.asarray(inputs["L0_ffn_s_up"], dtype=np.float32)
    sd = np.asarray(inputs["L0_ffn_s_down"], dtype=np.float32)

    wo_t = np.ascontiguousarray(
        wo.reshape(NKT, 128, NKT, 128).transpose(2, 0, 1, 3))
    wg1_t = np.ascontiguousarray(wg1.reshape(NKT, 128, 128))
    bg_t = np.ascontiguousarray(
        bg.reshape(NHT, 128, NKT, 128).transpose(0, 2, 3, 1))
    bu_t = np.ascontiguousarray(
        bu.reshape(NHT, 128, NKT, 128).transpose(0, 2, 3, 1))
    bd_t = np.ascontiguousarray(
        bd.reshape(NKT, 128, NHT, 128).transpose(0, 2, 3, 1))
    sg_c = np.ascontiguousarray(sg.reshape(NHT, 128).T)
    su_c = np.ascontiguousarray(su.reshape(NHT, 128).T)
    sd_c = np.ascontiguousarray(sd.reshape(NKT, 128).T)
    nw_c = np.ascontiguousarray(nw.reshape(NKT, 128).T)

    j = np.arange(64, dtype=np.float64)
    theta = BASE ** (-2.0 * j / HD)
    th128 = np.concatenate([theta, theta])
    pos = np.arange(SEQ, dtype=np.float64)
    ang = th128[:, None] * pos[None, :]
    cos_t = np.cos(ang).astype(np.float32)
    sgn128 = np.where(np.arange(128) < 64, -1.0, 1.0)
    sinsg_t = (np.sin(ang) * sgn128[:, None]).astype(np.float32)
    cos_g = np.ascontiguousarray(cos_t.reshape(128, NGRP, GW).transpose(1, 0, 2))
    sinsg_g = np.ascontiguousarray(sinsg_t.reshape(128, NGRP, GW).transpose(1, 0, 2))
    ident = np.eye(128, dtype=np.float32)
    perm = np.zeros((128, 128), dtype=np.float32)
    perm[np.arange(128), (np.arange(128) + 64) % 128] = 1.0
    slopes = 2.0 ** (-8.0 * (np.arange(HEADS, dtype=np.float64) + 1.0) / HEADS)
    log_lam = -slopes
    i128 = np.arange(128, dtype=np.float64)
    tmins = i128[:, None] - i128[None, :]

    base = dict(
        xg_in=xg,
        wo_in=wo_t, wg1_in=wg1_t, wg2_in=wg2,
        bg_in=bg_t, bu_in=bu_t, bd_in=bd_t,
        sg_in=sg_c, su_in=su_c, sd_in=sd_c, nw_in=nw_c,
        nwtm_in=np.ascontiguousarray(np.broadcast_to(nw[None, :], (128, D))),
        cos_in=cos_g, sinsg_in=sinsg_g,
        ones_row_in=np.ones((1, 128), np.float32),
        c128_row_in=np.full((1, 128), -128.0, np.float32),
        invd_col_in=np.full((128, 1), 1.0 / D, np.float32),
        ones_colb_in=np.ones((128, 1), np.float32),
        ident_in=ident, perm_in=perm,
        zero_col_in=np.zeros((128, 1), np.float32),
        one_col_in=np.ones((128, 1), np.float32),
    )

    in_maps = []
    for c in range(NCORES):
        m = dict(base)
        m["xs_in"] = np.ascontiguousarray(
            xT[:, c * TPC:(c + 1) * TPC].reshape(NKT, 128, TPC))
        for nm, w in (("wq_in", wq), ("wk_in", wk), ("wv_in", wv)):
            sl = w[:, c * HPC * HD:(c + 1) * HPC * HD]
            m[nm] = np.ascontiguousarray(
                sl.reshape(NKT, 128, HPC, HD).transpose(2, 0, 1, 3))
        ll = log_lam[c * HPC:(c + 1) * HPC]
        dmat = np.where(tmins[None] >= 0, np.exp(ll[:, None, None] * tmins[None]), 0.0)
        m["dt_in"] = np.ascontiguousarray(
            dmat.transpose(0, 2, 1)).astype(np.float32)
        dq = np.exp(ll[:, None] * (i128[None, :] + 1.0))
        m["dq_in"] = np.ascontiguousarray(
            np.broadcast_to(np.tile(dq, (1, GRP))[:, None, :],
                            (HPC, 128, GW))).astype(np.float32)
        dk = np.exp(ll[:, None] * (CHUNK - 1.0 - i128[None, :]))
        m["dk_in"] = np.ascontiguousarray(dk.T).astype(np.float32)
        lamC = np.exp(ll * CHUNK)
        m["lam_in"] = np.ascontiguousarray(
            np.broadcast_to(lamC[None, :], (128, HPC))).astype(np.float32)
        in_maps.append(m)
    return in_maps


_NC_CACHE = {}


def kernel(**inputs):
    in_maps = _prep_inputs(inputs)
    if "nc" not in _NC_CACHE:
        _NC_CACHE["nc"] = build_program()
    nc = _NC_CACHE["nc"]
    res = run_bass_kernel_spmd(nc, in_maps, core_ids=list(range(NCORES)))
    out = np.concatenate([np.asarray(res.results[c]["out"]).reshape(TPC, D)
                          for c in range(NCORES)], axis=0)
    return np.ascontiguousarray(out.reshape(1, SEQ, D).astype(np.float32))
